# version 03
# baseline (speedup 1.0000x reference)
import math
import contextlib
import numpy as np
import ml_dtypes
import concourse.bass as bass
import concourse.mybir as mybir
from concourse.bass_utils import run_bass_kernel_spmd

F32 = mybir.dt.float32
BF16 = mybir.dt.bfloat16
AF = mybir.ActivationFunctionType
ALU = mybir.AluOpType
AX = mybir.AxisListType

NCORE = 8
TOK = 2048
D = 4096
S = 4096
NKC = 32
HFF = 11008
NMC = 86
DQ = 1024
NQC = 8
HQ = 2752
NHC = 22
NT = 4
NST = 16
NJ = 16
NSUP = 8
SCL = 1.0 / math.sqrt(128.0)
PAIRS = [[0, 1], [2, 3], [4, 5], [6, 7]]
NEG = -1.0e9


def _alibi_slopes(n):
    p = 2 ** math.floor(math.log2(n))
    base = 2.0 ** (-(2.0 ** -(math.log2(p) - 3)))
    sl = [base ** (i + 1) for i in range(p)]
    if p != n:
        eb = 2.0 ** (-(2.0 ** -(math.log2(2 * p) - 3)))
        sl += [eb ** (2 * i + 1) for i in range(min(p, n - p))]
    return np.asarray(sl, dtype=np.float32)


def _hs_plans():
    plans = []
    first = {}
    for mc in range(NMC):
        h0 = 128 * mc
        bounds = {0, 64, 128}
        r = (-h0) % HQ
        if 0 < r < 128:
            bounds.add(r)
        bounds = sorted(bounds)
        segs = []
        for lo, hi in zip(bounds[:-1], bounds[1:]):
            hp = (h0 + lo) % HQ
            c, dlo = hp // 128, hp % 128
            rot = (dlo != lo)
            if rot:
                assert dlo == (lo + 64) % 128, (mc, lo, dlo)
            key = (c, dlo, hi - lo)
            isfirst = key not in first and (c, dlo) not in {(k[0], k[1]) for k in first}
            kk = (c, dlo)
            isfirst = kk not in first
            if isfirst:
                first[kk] = mc
                assert not rot, f"first write must be aligned mc={mc} seg={lo}"
            segs.append((lo, hi, c, dlo, rot, isfirst))
        plans.append(segs)
    merged = []
    for segs in plans:
        out = []
        for sg in segs:
            if out:
                lo, hi, c, dlo, rot, fi = out[-1]
                lo2, hi2, c2, dlo2, rot2, fi2 = sg
                if (not rot and not rot2 and c == c2 and hi == lo2
                        and dlo + (hi - lo) == dlo2 and fi == fi2):
                    out[-1] = (lo, hi2, c, dlo, rot, fi)
                    continue
            out.append(sg)
        merged.append(out)
    return merged


HS_PLANS = _hs_plans()

_UID = [0]


def _un(name):
    _UID[0] += 1
    return f"{name}_{_UID[0]}"


class Sem:
    def __init__(self, es, nc, name):
        self.h = es.enter_context(nc.semaphore(name))
        self.n = 0
        self.name = name

    def bump(self, amt):
        self.n += amt
        return self.n


class Em:
    ENGS = ("sync", "tensor", "scalar", "vector", "gpsimd")

    def __init__(self, nc):
        self.nc = nc
        self.ops = {e: [] for e in self.ENGS}
        self.lastwait = {e: {} for e in self.ENGS}

    def op(self, eng, fn, wait=(), inc=None, amt=1):
        w = []
        lw = self.lastwait[eng]
        for sem, v in wait:
            if sem is None or v is None or v <= 0:
                continue
            if lw.get(id(sem), 0) >= v:
                continue
            lw[id(sem)] = v
            w.append((sem, v))
        ret = inc.bump(amt) if inc is not None else None
        self.ops[eng].append((w, fn, inc, amt))
        return ret

    def run(self):
        with self.nc.Block() as blk:
            for eng in self.ENGS:
                if not self.ops[eng]:
                    continue

                def body(e, _ops=self.ops[eng]):
                    for w, fn, inc, amt in _ops:
                        for sem, v in w:
                            e.wait_ge(sem.h, v)
                        ins = fn(e)
                        if inc is not None:
                            ins.then_inc(inc.h, amt)
                getattr(blk, eng)(body)


class St:
    pass


def build_nc(taps=(), snaps=(), expt=()):
    nc = bass.Bass("TRN2", target_bir_lowering=False, num_devices=8)
    st = St()
    st.nc = nc
    st.want_snaps = set(snaps)
    st.snap_outs = {}
    st.expt = set(expt)

    def dri(name, shape):
        return nc.dram_tensor(name, shape, F32, kind="ExternalInput")

    def drib(name, shape):
        return nc.dram_tensor(name, shape, BF16, kind="ExternalInput")

    st.xin = dri("x", [TOK, D])
    st.w1t = dri("w1t", [NMC * 128, DQ])
    st.w3t = dri("w3t", [NMC * 128, DQ])
    st.w2t = dri("w2t", [NKC * 128, NHC * 128])
    st.wqkvt = dri("wqkvt", [48 * 128, NKC * 128])
    st.woutt = dri("woutt", [32 * 128, NKC * 128])
    st.b1t = dri("b1t", [2, HFF])
    st.b3t = dri("b3t", [2, HFF])
    st.b2t = dri("b2t", [2, D])
    st.a1t = dri("a1t", [128, 64])
    st.a3t = dri("a3t", [128, 64])
    st.a2t = dri("a2t", [128, 172])
    st.bqkvt = dri("bqkvt", [128, 48])
    st.boutt = dri("boutt", [128, 32])
    st.identt = drib("identt", [128, 128])
    st.p64t = drib("p64t", [128, 128])
    st.maskwt = dri("maskwt", [128, 1024])
    st.al2t = dri("al2t", [NJ * 128, 512])
    st.actt = dri("actt", [NJ * 128, 256])
    st.mat = dri("mat", [128, 1])
    st.mbt = dri("mbt", [128, 1])

    st.out = nc.dram_tensor("out", [TOK, D], F32, kind="ExternalOutput")

    def drs(name, shape, dt=BF16):
        return nc.dram_tensor(name, shape, dt)

    st.w1b = drs("w1b", [NMC * 128, DQ])
    st.w3b = drs("w3b", [NMC * 128, DQ])
    st.b1b = drs("b1b", [2, HFF])
    st.b3b = drs("b3b", [2, HFF])
    st.w2b = drs("w2b", [NKC * 128, NHC * 128])
    st.wqb = drs("wqb", [48 * 128, NKC * 128])
    st.wob = drs("wob", [32 * 128, NKC * 128])
    st.lntm = drs("lntm", [TOK, D])
    st.ddelta = drs("ddelta", [D, TOK])
    st.x1res = drs("x1res", [TOK, D], F32)
    st.x2res = drs("x2res", [TOK, D], F32)
    st.cc1i = drs("cc1i", [TOK, D])
    st.cc1o = drs("cc1o", [2 * TOK, D])
    st.qs = drs("qs", [NJ * 128, S])
    st.ks = drs("ks", [NJ * 128, S])
    st.vsf = drs("vsf", [NJ * 128, S])
    st.cc2i = drs("cc2i", [2 * TOK, 2 * D // 2])
    st.cc2o = drs("cc2o", [TOK, 2 * D // 2])

    st.taps = {}
    for tname in taps:
        src = getattr(st, tname)
        shp = list(src.shape)
        dt = src.dtype if hasattr(src, "dtype") else F32
        st.taps[tname] = nc.dram_tensor("tap_" + tname, shp, dt, kind="ExternalOutput")

    with contextlib.ExitStack() as es:
        st.es = es
        st.A = [Sem(es, nc, f"sA{i}") for i in range(4)]
        st.S = [Sem(es, nc, f"sS{i}") for i in range(2)]
        st.sSU = Sem(es, nc, "sSU")
        st.cW13 = Sem(es, nc, "cW13")
        st.cW2 = Sem(es, nc, "cW2")
        st.cWQ = Sem(es, nc, "cWQ")
        st.cWO = Sem(es, nc, "cWO")
        st.cAS = Sem(es, nc, "cAS")
        st.sCAo = Sem(es, nc, "sCAo")
        st.sLB = Sem(es, nc, "sLB")
        st.sCC = Sem(es, nc, "sCC")
        st.sPE = Sem(es, nc, "sPE")
        st.sAC = Sem(es, nc, "sAC")
        st.sDV = Sem(es, nc, "sDV")

        def sb(name, shape, dt=BF16):
            return es.enter_context(nc.sbuf_tensor(_un(name), shape, dt))

        st.b2s = sb("b2s", [2, D])
        st.a1s = sb("a1s", [128, 64])
        st.a3s = sb("a3s", [128, 64])
        st.a2s = sb("a2s", [128, 172])
        st.bqs = sb("bqs", [128, 48], F32)
        st.bqsc = sb("bqsc", [128, 16], F32)
        st.bos = sb("bos", [128, 32], F32)
        st.maskws = sb("maskws", [128, 1024], F32)
        st.idents = sb("idents", [128, 128])
        st.p64s = sb("p64s", [128, 128])
        st.ones1s = sb("ones1s", [1, 128])
        st.ones128s = sb("ones128s", [128, 1])
        st.mas = sb("mas", [128, 1], F32)
        st.mbs = sb("mbs", [128, 1], F32)
        st.u1s = sb("u1s", [2, TOK])
        st.u3s = sb("u3s", [2, TOK])
        st.u2s = sb("u2s", [2, TOK])

        _setup_and_casts(st)
        _ffn_pass(st, src=st.xin, dst=st.x1res, write_ln2=True, tag="f1")
        _cc1(st)
        _qkv(st)
        _attention(st)
        _cc2(st)
        _outproj(st)
        _epilogue_pass(st, delta_scale=1.0, src=st.x1res, dst=st.x2res, write_ln2=False)
        _ffn_pass(st, src=st.x2res, dst=st.out, write_ln2=False, tag="f3")
        _final(st)
    return nc


def _snap(st, tag, src, shape=None, dt=None):
    if tag not in st.want_snaps:
        return
    nc = st.nc
    if shape is None:
        shape = list(src.shape)
    if dt is None:
        dt = src.dtype
    out = nc.dram_tensor("snap_" + tag, shape, dt, kind="ExternalOutput")
    st.snap_outs[tag] = out
    em = Em(nc)
    em.op("gpsimd", lambda e: e.dma_start(out=out.ap(), in_=(src.ap() if hasattr(src, "ap") else src[:, :])),
          wait=[(st.S[0], st.S[0].n), (st.S[1], st.S[1].n),
                (st.sDV, st.sDV.n), (st.sAC, st.sAC.n),
                (st.sPE, st.sPE.n), (st.sCAo, st.sCAo.n)],
          inc=st.sCC, amt=16)
    em.op("gpsimd", lambda e: None, wait=[(st.sCC, st.sCC.n)])
    em.run()


def _setup_and_casts(st):
    nc = st.nc
    em = Em(nc)

    for dst, src in [(st.bqs, st.bqkvt), (st.bos, st.boutt), (st.maskws, st.maskwt),
                     (st.idents, st.identt), (st.p64s, st.p64t),
                     (st.mas, st.mat), (st.mbs, st.mbt)]:
        em.op("sync", lambda e, d=dst, s=src: e.dma_start(out=d[:, :], in_=s.ap()),
              inc=st.sSU, amt=16)
    st.v_setup = st.sSU.n

    em.op("vector", lambda e: e.memset(st.ones1s[:, :], 1.0))
    em.op("vector", lambda e: e.memset(st.ones128s[:, :], 1.0), inc=st.sDV)
    st.v_memset = st.sDV.n

    for dst, src in [(st.a1s, st.a1t), (st.a3s, st.a3t), (st.a2s, st.a2t)]:
        em.op("gpsimd", lambda e, d=dst, s=src: e.dma_start(
            out=d[:, :], in_=s.ap(), max_dma_last_dim=8192), inc=st.cAS, amt=16)
    st.v_small_cast = st.cAS.n
    em.op("gpsimd", lambda e: e.dma_start(out=st.b2s[:, :], in_=st.b2t.ap(),
                                          max_dma_last_dim=8192),
          inc=st.sCAo, amt=16)
    for dst, src in [(st.b1b, st.b1t), (st.b3b, st.b3t)]:
        em.op("gpsimd", lambda e, d=dst, s=src: e.dma_start(
            out=d.ap(), in_=s.ap(), max_dma_last_dim=8192), inc=st.sCAo, amt=16)
    st.v_odd_cast = st.sCAo.n

    def cast_chunks(sem, dst, src, nrows, nchunks):
        step = nrows // nchunks
        assert step % 16 == 0
        for i in range(nchunks):
            r0, r1 = i * step, (i + 1) * step if i < nchunks - 1 else nrows
            em.op("gpsimd", lambda e, a=r0, b=r1, d=dst, s=src: e.dma_start(
                out=d.ap()[a:b, :], in_=s.ap()[a:b, :], max_dma_last_dim=8192),
                inc=sem, amt=16)
        return sem.n

    st.vW13 = cast_chunks(st.cW13, st.w1b, st.w1t, NMC * 128, 8)
    st.vW13 = cast_chunks(st.cW13, st.w3b, st.w3t, NMC * 128, 8)
    st.vW2 = cast_chunks(st.cW2, st.w2b, st.w2t, NKC * 128, 4)
    st.vWQ = cast_chunks(st.cWQ, st.wqb, st.wqkvt, 48 * 128, 8)
    st.vWO = cast_chunks(st.cWO, st.wob, st.woutt, 32 * 128, 4)
    em.run()


def _ln_chain(st, em, src_ap_fn, dst_sb, stats, war_dst=()):
    sbn, sagg, sstd, srstd, epss = stats
    src = src_ap_fn
    for k in range(8):
        if k < 7:
            em.op("vector", lambda e, kk=k: e.bn_stats(sbn[:, 6 * kk:6 * kk + 6],
                                                       src[:, 512 * kk:512 * kk + 512]))
        else:
            em.op("vector", lambda e, kk=k: e.bn_stats(sbn[:, 6 * kk:6 * kk + 6],
                                                       src[:, 512 * kk:512 * kk + 512]),
                  inc=st.sDV)
    v_bn = st.sDV.n
    em.op("vector", lambda e: e.bn_aggr(sagg[:, :], sbn[:, :]),
          wait=[(st.sDV, v_bn)], inc=st.sDV)
    v_agg = st.sDV.n
    em.op("scalar", lambda e: e.activation(out=sstd[:, :], in_=sagg[:, 1:2],
                                           func=AF.Sqrt, bias=epss[:, :]),
          wait=[(st.sDV, v_agg)], inc=st.sAC)
    v_sqrt = st.sAC.n
    em.op("vector", lambda e: e.reciprocal(srstd[:, :], sstd[:, :]),
          wait=[(st.sAC, v_sqrt)], inc=st.sDV)
    v_rec = st.sDV.n
    em.op("vector", lambda e: e.tensor_scalar(out=dst_sb[:, :], in0=src[:, :],
                                              scalar1=sagg[:, 0:1], scalar2=srstd[:, :],
                                              op0=ALU.subtract, op1=ALU.mult),
          wait=[(st.sDV, v_rec)] + list(war_dst), inc=st.sDV)
    return st.sDV.n


def _ffn_pass(st, src, dst, write_ln2, tag="f1"):
    nc = st.nc
    is_out = (dst is st.out)

    with contextlib.ExitStack() as es:
        def sb(name, shape, dt=BF16):
            return es.enter_context(nc.sbuf_tensor(_un(name), shape, dt))
        xsub = [sb(f"a_xsub{i}", [128, D], F32) for i in range(2)]
        xlnt = [sb(f"a_xlnt{i}", [128, D]) for i in range(2)]
        sbn = sb("a_sbn", [128, 48], F32)
        sagg = sb("a_sagg", [128, 2], F32)
        sstd = sb("a_sstd", [128, 1], F32)
        srstd = sb("a_srstd", [128, 1], F32)
        epss = sb("a_eps", [128, 1], F32)
        em = Em(nc)
        em.op("vector", lambda e: e.memset(epss[:, :], 1e-5), inc=st.sDV)
        norm_v = [0, 0]
        store_v = [0, 0]
        for t in range(NST):
            par = t % 2
            em.op("sync", lambda e, tt=t, p=par: e.dma_start(
                out=xsub[p][:, :], in_=src.ap()[128 * tt:128 * tt + 128, :]),
                wait=[(st.sDV, norm_v[par])],
                inc=st.A[par], amt=16)
            v_ld = st.A[par].n
            em.op("vector", lambda e, p=par: e.bn_stats(sbn[:, 0:6], xsub[p][:, 0:512]),
                  wait=[(st.A[par], v_ld)])
            for k in range(1, 8):
                if k < 7:
                    em.op("vector", lambda e, kk=k, p=par: e.bn_stats(
                        sbn[:, 6 * kk:6 * kk + 6], xsub[p][:, 512 * kk:512 * kk + 512]))
                else:
                    em.op("vector", lambda e, kk=k, p=par: e.bn_stats(
                        sbn[:, 6 * kk:6 * kk + 6], xsub[p][:, 512 * kk:512 * kk + 512]),
                        inc=st.sDV)
            v_bn = st.sDV.n
            em.op("vector", lambda e: e.bn_aggr(sagg[:, :], sbn[:, :]),
                  wait=[(st.sDV, v_bn)], inc=st.sDV)
            v_agg = st.sDV.n
            em.op("scalar", lambda e: e.activation(out=sstd[:, :], in_=sagg[:, 1:2],
                                                   func=AF.Sqrt, bias=epss[:, :]),
                  wait=[(st.sDV, v_agg)], inc=st.sAC)
            v_sq = st.sAC.n
            em.op("vector", lambda e: e.reciprocal(srstd[:, :], sstd[:, :]),
                  wait=[(st.sAC, v_sq)], inc=st.sDV)
            v_rec = st.sDV.n
            em.op("vector", lambda e, p=par: e.tensor_scalar(
                out=xlnt[p][:, :], in0=xsub[p][:, :], scalar1=sagg[:, 0:1],
                scalar2=srstd[:, :], op0=ALU.subtract, op1=ALU.mult),
                wait=[(st.sDV, v_rec), (st.S[par], store_v[par])], inc=st.sDV)
            norm_v[par] = st.sDV.n
            em.op("sync", lambda e, tt=t, p=par: e.dma_start(
                out=st.lntm.ap()[128 * tt:128 * tt + 128, :], in_=xlnt[p][:, :]),
                wait=[(st.sDV, norm_v[par])], inc=st.S[par], amt=16)
            store_v[par] = st.S[par].n
        em.run()
    _snap(st, tag + "_lntm", st.lntm)

    with contextlib.ExitStack() as esp:
        def sbp(name, shape, dt=BF16):
            return esp.enter_context(nc.sbuf_tensor(_un(name), shape, dt))
        xs = sbp("p_xs", [128, NQC * TOK])
        hs = sbp("p_hs", [128, NHC * TOK])

        with contextlib.ExitStack() as es:
            def sb(name, shape, dt=BF16):
                return es.enter_context(nc.sbuf_tensor(_un(name), shape, dt))
            xfm = sb("b_xfm", [128, NKC * 512])
            t1 = sb("b_t1", [128, 512])
            t2 = sb("b_t2", [128, 512])
            pu1 = [es.enter_context(nc.psum_tensor(_un(f"b_pu1{i}"), [128, 512], F32)) for i in range(2)]
            pu3 = [es.enter_context(nc.psum_tensor(_un(f"b_pu3{i}"), [128, 512], F32)) for i in range(2)]
            em = Em(nc)
            pe_used = 0
            dv_used = 0
            evac_v = [0, 0]
            for t in range(NT):
                par = t % 2
                for kc in range(NKC):
                    em.op("sync", lambda e, tt=t, k=kc: e.dma_start(
                        out=xfm[:, 512 * k:512 * k + 512],
                        in_=st.lntm.ap()[512 * tt:512 * tt + 512, 128 * k:128 * k + 128],
                        transpose=True),
                        wait=[(st.sPE, pe_used), (st.sDV, dv_used),
                              (st.S[0], st.S[0].n if kc == 0 and t == 0 else 0),
                              (st.S[1], st.S[1].n if kc == 0 and t == 0 else 0)],
                        inc=st.A[0], amt=16)
                ldv = st.A[0].n
                for kq in range(NQC):
                    em.op("vector", lambda e, k=kq: e.tensor_tensor(
                        out=t1[:, :], in0=xfm[:, 512 * k:512 * k + 512],
                        in1=xfm[:, 512 * (k + 8):512 * (k + 8) + 512], op=ALU.add),
                        wait=[(st.A[0], ldv)])
                    em.op("vector", lambda e, k=kq: e.tensor_tensor(
                        out=t2[:, :], in0=xfm[:, 512 * (k + 16):512 * (k + 16) + 512],
                        in1=xfm[:, 512 * (k + 24):512 * (k + 24) + 512], op=ALU.add))
                    em.op("vector", lambda e, k=kq, tt=t: e.tensor_tensor(
                        out=xs[:, TOK * k + 512 * tt:TOK * k + 512 * tt + 512],
                        in0=t1[:, :], in1=t2[:, :], op=ALU.add),
                        inc=(st.sDV if k == NQC - 1 else None))
                dv_used = st.sDV.n
                for (pu, asb) in ((pu1[par], st.a1s), (pu3[par], st.a3s)):
                    for kc in range(NKC):
                        em.op("tensor", lambda e, k=kc, b=pu, a=asb: e.matmul(
                            b[0:2, :], a[:, 2 * k:2 * k + 2],
                            xfm[:, 512 * k:512 * k + 512],
                            start=(k == 0), stop=(k == NKC - 1)),
                            wait=[(st.A[0], ldv), (st.cAS, st.v_small_cast),
                                  (st.sAC, evac_v[par])],
                            inc=(st.sPE if kc == NKC - 1 else None))
                pe_used = st.sPE.n
                v_mm = st.sPE.n
                em.op("scalar", lambda e, tt=t, p=par: e.activation(
                    out=st.u1s[0:2, 512 * tt:512 * tt + 512], in_=pu1[p][0:2, :],
                    func=AF.Identity, scale=0.5),
                    wait=[(st.sPE, v_mm)], inc=st.sAC)
                em.op("scalar", lambda e, tt=t, p=par: e.activation(
                    out=st.u3s[0:2, 512 * tt:512 * tt + 512], in_=pu3[p][0:2, :],
                    func=AF.Identity, scale=0.5),
                    inc=st.sAC)
                evac_v[par] = st.sAC.n
            em.run()
        _snap(st, tag + "_xs", xs, shape=[128, NQC * TOK], dt=BF16)
        _snap(st, tag + "_u1", st.u1s, shape=[2, TOK], dt=BF16)
        _snap(st, tag + "_u3", st.u3s, shape=[2, TOK], dt=BF16)

        with contextlib.ExitStack() as es:
            def sb(name, shape, dt=BF16):
                return es.enter_context(nc.sbuf_tensor(_un(name), shape, dt))
            w1s = [sb(f"c_w1s{i}", [128, DQ]) for i in range(2)]
            w3s = [sb(f"c_w3s{i}", [128, DQ]) for i in range(2)]
            b1p = sb("c_b1p", [2, HFF])
            b3p = sb("c_b3p", [2, HFF])
            gsb = [sb(f"c_g{i}", [128, 512]) for i in range(2)]
            usb = [sb(f"c_u{i}", [128, 512]) for i in range(2)]
            NHP = 6
            hp = [sb(f"c_hp{i}", [128, 512]) for i in range(NHP)]
            pg = es.enter_context(nc.psum_tensor(_un("c_pg"), [128, 512], F32))
            pup = es.enter_context(nc.psum_tensor(_un("c_pup"), [128, 512], F32))
            pu2 = [es.enter_context(nc.psum_tensor(_un(f"c_pu2{i}"), [128, 512], F32)) for i in range(NT)]
            prot = [es.enter_context(nc.psum_tensor(_un(f"c_prot{i}"), [128, 512], F32)) for i in range(2)]

            em = Em(nc)
            IT = NMC * NT
            em.op("sync", lambda e: e.dma_start(out=b1p[:, :], in_=st.b1b.ap()),
                  wait=[(st.sCAo, st.v_odd_cast)], inc=st.sLB, amt=16)
            em.op("sync", lambda e: e.dma_start(out=b3p[:, :], in_=st.b3b.ap()),
                  inc=st.sLB, amt=16)
            v_lb = st.sLB.n
            wld_v = [0] * NMC
            wuse_v = [0, 0]
            silu_v = [0] * IT
            upev_v = [0] * IT
            mult_v = [0] * IT
            hsal_v = [0] * IT
            u2_v = [0] * IT
            rot_v = [0] * IT
            rotadd_v = [0, 0]
            g_war = 0
            u_war = 0
            nrot = 0

            for it in range(IT + 2):
                if it < IT:
                    mc, tau = divmod(it, NT)
                    par = mc % 2
                    if tau == 0:
                        for (wbuf, wsrc) in ((w1s[par], st.w1b),
                                             (w3s[par], st.w3b)):
                            em.op("sync", lambda e, d=wbuf, s=wsrc, m=mc: e.dma_start(
                                out=d[:, :], in_=s.ap()[128 * m:128 * m + 128, :]),
                                wait=[(st.cW13, st.vW13),
                                      (st.sPE, wuse_v[par])],
                                inc=st.A[par], amt=16)
                        wld_v[mc] = st.A[par].n
                    for kc in range(NQC + 1):
                        if kc < NQC:
                            em.op("tensor", lambda e, p=par, k=kc, tt=tau: e.matmul(
                                pg[:, :], w1s[p][:, 128 * k:128 * k + 128],
                                xs[:, TOK * k + 512 * tt:TOK * k + 512 * tt + 512],
                                start=(k == 0), stop=False),
                                wait=[(st.A[par], wld_v[mc]),
                                      (st.sAC, silu_v[it - 1] if it >= 1 else 0)])
                        else:
                            em.op("tensor", lambda e, m=mc, tt=tau: e.matmul(
                                pg[:, :], b1p[0:2, 128 * m:128 * m + 128],
                                st.u1s[0:2, 512 * tt:512 * tt + 512],
                                start=False, stop=True),
                                wait=[(st.sLB, v_lb)], inc=st.sPE)
                    v_g = st.sPE.n
                    for kc in range(NQC + 1):
                        if kc < NQC:
                            em.op("tensor", lambda e, p=par, k=kc, tt=tau: e.matmul(
                                pup[:, :], w3s[p][:, 128 * k:128 * k + 128],
                                xs[:, TOK * k + 512 * tt:TOK * k + 512 * tt + 512],
                                start=(k == 0), stop=False),
                                wait=[(st.sAC, upev_v[it - 1] if it >= 1 else 0)])
                        else:
                            em.op("tensor", lambda e, m=mc, tt=tau: e.matmul(
                                pup[:, :], b3p[0:2, 128 * m:128 * m + 128],
                                st.u3s[0:2, 512 * tt:512 * tt + 512],
                                start=False, stop=True), inc=st.sPE)
                    v_u = st.sPE.n
                    if tau == NT - 1:
                        wuse_v[par] = st.sPE.n

                jt = it - 2
                if 0 <= jt < IT:
                    mcj, tauj = divmod(jt, NT)
                    if "no_u2" in st.expt:
                        em.op("tensor", lambda e: None, wait=[(st.sDV, mult_v[jt])])
                        u2_v[jt] = st.sPE.n
                    else:
                        em.op("tensor", lambda e, m=mcj, tt=tauj, sl=jt % NHP: e.matmul(
                            pu2[tt][0:2, :], st.a2s[:, 2 * m:2 * m + 2], hp[sl][:, :],
                            start=(m == 0), stop=(m == NMC - 1)),
                            wait=[(st.sDV, mult_v[jt])], inc=st.sPE)
                        u2_v[jt] = st.sPE.n
                    if any(sg[4] for sg in HS_PLANS[mcj]) and "no_rot" not in st.expt:
                        parr = nrot % 2
                        em.op("tensor", lambda e, sl=jt % NHP, p=parr: e.matmul(
                            prot[p][:, :], st.p64s[:, :], hp[sl][:, :],
                            start=True, stop=True),
                            wait=[(st.sDV, rotadd_v[parr]), (st.sSU, st.v_setup)],
                            inc=st.sPE)
                        rot_v[jt] = (st.sPE.n, parr)
                        nrot += 1

                if it < IT:
                    em.op("scalar", lambda e, p=it % 2: e.activation(
                        out=gsb[p][:, :], in_=pg[:, :], func=AF.Silu),
                        wait=[(st.sPE, v_g), (st.sDV, mult_v[it - 2] if it >= 2 else 0)],
                        inc=st.sAC)
                    silu_v[it] = st.sAC.n
                    em.op("scalar", lambda e, p=it % 2: e.activation(
                        out=usb[p][:, :], in_=pup[:, :], func=AF.Identity),
                        wait=[(st.sPE, v_u)], inc=st.sAC)
                    upev_v[it] = st.sAC.n
                    slot = it % NHP
                    hp_war = 0
                    if it >= NHP:
                        hp_war = u2_v[it - NHP]
                        if rot_v[it - NHP]:
                            hp_war = max(hp_war, rot_v[it - NHP][0])
                    em.op("vector", lambda e, p=it % 2, sl=slot: e.tensor_tensor(
                        out=hp[sl][:, :], in0=gsb[p][:, :], in1=usb[p][:, :], op=ALU.mult),
                        wait=[(st.sAC, upev_v[it]), (st.sPE, hp_war)],
                        inc=st.sDV)
                    mult_v[it] = st.sDV.n
                    mc, tau = divmod(it, NT)
                    nal = sum(1 for sg in HS_PLANS[mc] if not sg[4])
                    done = 0
                    for (lo, hi, c, dlo, rot, fi) in HS_PLANS[mc]:
                        if rot:
                            continue
                        done += 1
                        dsl = slice(dlo, dlo + (hi - lo))
                        col = slice(TOK * c + 512 * tau, TOK * c + 512 * tau + 512)
                        if fi:
                            em.op("vector", lambda e, sl=slot, l=lo, h=hi, ds=dsl, co=col:
                                  e.tensor_copy(hs[ds, co], hp[sl][l:h, :]),
                                  inc=(st.sDV if done == nal else None))
                        else:
                            em.op("vector", lambda e, sl=slot, l=lo, h=hi, ds=dsl, co=col:
                                  e.tensor_tensor(out=hs[ds, co], in0=hs[ds, co],
                                                  in1=hp[sl][l:h, :], op=ALU.add),
                                  inc=(st.sDV if done == nal else None))
                    if nal:
                        hsal_v[it] = st.sDV.n

                if 0 <= jt < IT and rot_v[jt] and "no_rot" not in st.expt:
                    mcj, tauj = divmod(jt, NT)
                    vv, parr = rot_v[jt]
                    rsegs = [sg for sg in HS_PLANS[mcj] if sg[4]]
                    for i, (lo, hi, c, dlo, rot, fi) in enumerate(rsegs):
                        assert not fi
                        dsl = slice(dlo, dlo + (hi - lo))
                        col = slice(TOK * c + 512 * tauj, TOK * c + 512 * tauj + 512)
                        em.op("vector", lambda e, p=parr, ds=dsl, co=col:
                              e.tensor_tensor(out=hs[ds, co], in0=hs[ds, co],
                                              in1=prot[p][ds, :], op=ALU.add),
                              wait=[(st.sPE, vv)],
                              inc=(st.sDV if i == len(rsegs) - 1 else None))
                    rotadd_v[parr] = st.sDV.n

            for t in range(NT):
                em.op("scalar", lambda e, tt=t: e.activation(
                    out=st.u2s[0:2, 512 * tt:512 * tt + 512], in_=pu2[tt][0:2, :],
                    func=AF.Identity, scale=0.5),
                    wait=[(st.sPE, u2_v[IT - 4 + t])], inc=st.sAC)
            em.run()
        _snap(st, tag + "_hs", hs, shape=[128, NHC * TOK], dt=BF16)
        _snap(st, tag + "_u2", st.u2s, shape=[2, TOK], dt=BF16)

        with contextlib.ExitStack() as es:
            def sb(name, shape, dt=BF16):
                return es.enter_context(nc.sbuf_tensor(_un(name), shape, dt))
            w2s = [sb(f"d_w2s{i}", [128, NHC * 128]) for i in range(2)]
            dds = [sb(f"d_dd{i}", [128, 512]) for i in range(2)]
            po = [es.enter_context(nc.psum_tensor(_un(f"d_po{i}"), [128, 512], F32)) for i in range(2)]
            em = Em(nc)
            wuse = [0, 0]
            evac = [0, 0]
            stv = [0, 0]
            for oc in range(NKC):
                par = oc % 2
                em.op("sync", lambda e, o=oc, p=par: e.dma_start(
                    out=w2s[p][:, :], in_=st.w2b.ap()[128 * o:128 * o + 128, :]),
                    wait=[(st.cW2, st.vW2), (st.sPE, wuse[par])],
                    inc=st.A[par], amt=16)
                wld = st.A[par].n
                for t in range(NT):
                    it = oc * NT + t
                    pp = it % 2
                    for kc in range(NHC + 1):
                        if kc < NHC:
                            kk = 128 if kc < NHC - 1 else 64
                            em.op("tensor", lambda e, p=par, k=kc, tt=t, b=pp, K=kk: e.matmul(
                                po[b][:, :], w2s[p][0:K, 128 * k:128 * k + 128],
                                hs[0:K, TOK * k + 512 * tt:TOK * k + 512 * tt + 512],
                                start=(k == 0), stop=False),
                                wait=[(st.A[par], wld),
                                      (st.sAC, evac[pp])])
                        else:
                            em.op("tensor", lambda e, o=oc, tt=t, b=pp: e.matmul(
                                po[b][:, :], st.b2s[0:2, 128 * o:128 * o + 128],
                                st.u2s[0:2, 512 * tt:512 * tt + 512],
                                start=False, stop=True),
                                wait=[(st.sCAo, st.v_odd_cast)], inc=st.sPE)
                    v_mm = st.sPE.n
                    if t == NT - 1:
                        wuse[par] = st.sPE.n
                    em.op("scalar", lambda e, b=pp: e.activation(
                        out=dds[b][:, :], in_=po[b][:, :], func=AF.Identity),
                        wait=[(st.sPE, v_mm), (st.S[pp], stv[pp])], inc=st.sAC)
                    evac[pp] = st.sAC.n
                    em.op("sync", lambda e, o=oc, tt=t, b=pp: e.dma_start(
                        out=st.ddelta.ap()[128 * o:128 * o + 128, 512 * tt:512 * tt + 512],
                        in_=dds[b][:, :]),
                        wait=[(st.sAC, evac[pp])], inc=st.S[pp], amt=16)
                    stv[pp] = st.S[pp].n
            em.run()
        _snap(st, tag + "_dd", st.ddelta)

    _epilogue_pass(st, delta_scale=0.5, src=src, dst=dst, write_ln2=write_ln2)


def _epilogue_pass(st, delta_scale, src, dst, write_ln2):
    nc = st.nc
    with contextlib.ExitStack() as es:
        def sb(name, shape, dt=BF16):
            return es.enter_context(nc.sbuf_tensor(_un(name), shape, dt))
        dtm = [sb(f"e_dtm{i}", [128, D]) for i in range(2)]
        rsub = [sb(f"e_rsub{i}", [128, D], F32) for i in range(2)]
        xnew = [sb(f"e_xnew{i}", [128, D], F32) for i in range(2)]
        l2t = [sb(f"e_l2t{i}", [128, D]) for i in range(2)]
        sbn = sb("e_sbn", [128, 48], F32)
        sagg = sb("e_sagg", [128, 2], F32)
        sstd = sb("e_sstd", [128, 1], F32)
        srstd = sb("e_srstd", [128, 1], F32)
        epss = sb("e_eps", [128, 1], F32)
        em = Em(nc)
        em.op("vector", lambda e: e.memset(epss[:, :], 1e-5), inc=st.sDV)
        used = [0, 0]
        stv = [0, 0]
        base0, base1 = st.S[0].n, st.S[1].n
        for t in range(NST):
            par = t % 2
            for ob in range(8):
                em.op("sync", lambda e, tt=t, o=ob, p=par: e.dma_start(
                    out=dtm[p][:, 512 * o:512 * o + 512],
                    in_=st.ddelta.ap()[512 * o:512 * o + 512, 128 * tt:128 * tt + 128],
                    transpose=True),
                    wait=[(st.sDV, used[par]), (st.S[0], base0), (st.S[1], base1)],
                    inc=st.A[par], amt=16)
            em.op("sync", lambda e, tt=t, p=par: e.dma_start(
                out=rsub[p][:, :], in_=src.ap()[128 * tt:128 * tt + 128, :]),
                inc=st.A[par], amt=16)
            v_ld = st.A[par].n
            em.op("vector", lambda e, p=par, sc=delta_scale: e.scalar_tensor_tensor(
                out=xnew[p][:, :], in0=dtm[p][:, :], scalar=sc, in1=rsub[p][:, :],
                op0=ALU.mult, op1=ALU.add),
                wait=[(st.A[par], v_ld), (st.S[par], stv[par])], inc=st.sDV)
            v_new = st.sDV.n
            em.op("sync", lambda e, tt=t, p=par: e.dma_start(
                out=dst.ap()[128 * tt:128 * tt + 128, :], in_=xnew[p][:, :]),
                wait=[(st.sDV, v_new)], inc=st.S[par], amt=16)
            if write_ln2:
                v_ln = _ln_chain(st, em, xnew[par], l2t[par],
                                 (sbn, sagg, sstd, srstd, epss),
                                 war_dst=[(st.S[par], stv[par])])
                em.op("sync", lambda e, tt=t, p=par: e.dma_start(
                    out=st.cc1i.ap()[128 * tt:128 * tt + 128, :], in_=l2t[p][:, :]),
                    wait=[(st.sDV, v_ln)], inc=st.S[par], amt=16)
                used[par] = v_ln
            else:
                used[par] = v_new
            stv[par] = st.S[par].n
        em.run()


def _cc1(st):
    nc = st.nc
    em = Em(nc)
    em.op("gpsimd", lambda e: e.collective_compute(
        "AllGather", ALU.bypass, replica_groups=PAIRS,
        ins=[st.cc1i.ap()], outs=[st.cc1o.ap()]),
        wait=[(st.S[0], st.S[0].n), (st.S[1], st.S[1].n)], inc=st.sCC, amt=1)
    v = st.sCC.n
    em.op("gpsimd", lambda e: None, wait=[(st.sCC, v)])
    em.run()


def _qkv(st):
    nc = st.nc
    NF = 12
    with contextlib.ExitStack() as es:
        def sb(name, shape, dt=BF16):
            return es.enter_context(nc.sbuf_tensor(_un(name), shape, dt))
        wq = [sb(f"q_wq{i}", [128, NKC * 128]) for i in range(NF)]
        rhs = [sb(f"q_rhs{i}", [128, NKC * 512]) for i in range(2)]
        stg = [sb(f"q_stg{i}", [128, 512]) for i in range(2)]
        pq = [es.enter_context(nc.psum_tensor(_un(f"q_pq{i}"), [128, 512], F32)) for i in range(2)]
        em = Em(nc)
        em.op("vector", lambda e: e.tensor_scalar(
            out=st.bqsc[:, :], in0=st.bqs[:, 0:16], scalar1=SCL, scalar2=None,
            op0=ALU.mult),
            wait=[(st.sSU, st.v_setup)], inc=st.sDV)
        v_bq = st.sDV.n
        grp_use = 0
        rhs_use = [0, 0]
        evac = [0, 0]
        stv = [0, 0]
        for fg in range(4):
            for i in range(NF):
                fcg = fg * NF + i
                em.op("sync", lambda e, f=fcg, ii=i: e.dma_start(
                    out=wq[ii][:, :], in_=st.wqb.ap()[128 * f:128 * f + 128, :]),
                    wait=[(st.cWQ, st.vWQ), (st.sPE, grp_use)],
                    inc=st.A[2], amt=16)
            wldv = st.A[2].n
            for g in range(8):
                par = g % 2
                for kc in range(NKC):
                    em.op("sync", lambda e, gg=g, k=kc, p=par: e.dma_start(
                        out=rhs[p][:, 512 * k:512 * k + 512],
                        in_=st.cc1o.ap()[512 * gg:512 * gg + 512, 128 * k:128 * k + 128],
                        transpose=True),
                        wait=[(st.sPE, rhs_use[par])], inc=st.A[par], amt=16)
                rldv = st.A[par].n
                for i in range(NF):
                    fcg = fg * NF + i
                    pp = (g * NF + i) % 2
                    for kc in range(NKC):
                        em.op("tensor", lambda e, ii=i, k=kc, p=par, b=pp: e.matmul(
                            pq[b][:, :], wq[ii][:, 128 * k:128 * k + 128],
                            rhs[p][:, 512 * k:512 * k + 512],
                            start=(k == 0), stop=(k == NKC - 1)),
                            wait=[(st.A[2], wldv), (st.A[par], rldv),
                                  (st.sAC, evac[pp])],
                            inc=(st.sPE if kc == NKC - 1 else None))
                    v_mm = st.sPE.n
                    if i == NF - 1:
                        rhs_use[par] = st.sPE.n
                        if g == 7:
                            grp_use = st.sPE.n
                    is_q = fcg < 16
                    bias_ap = (st.bqsc[:, fcg:fcg + 1] if is_q
                               else st.bqs[:, fcg:fcg + 1])
                    scale = SCL if is_q else 1.0
                    em.op("scalar", lambda e, b=pp, ba=bias_ap, sc=scale: e.activation(
                        out=stg[b][:, :], in_=pq[b][:, :], func=AF.Identity,
                        bias=ba, scale=sc),
                        wait=[(st.sPE, v_mm), (st.S[pp], stv[pp]), (st.sDV, v_bq)],
                        inc=st.sAC)
                    v_ev = st.sAC.n
                    evac[pp] = v_ev
                    if fcg < 16:
                        tgt, row = st.qs, fcg
                    elif fcg < 32:
                        tgt, row = st.ks, fcg - 16
                    else:
                        tgt, row = st.vsf, fcg - 32
                    em.op("sync", lambda e, tg=tgt, r=row, gg=g, b=pp: e.dma_start(
                        out=tg.ap()[128 * r:128 * r + 128, 512 * gg:512 * gg + 512],
                        in_=stg[b][:, :]),
                        wait=[(st.sAC, v_ev)], inc=st.S[pp], amt=16)
                    stv[pp] = st.S[pp].n
        em.run()


def _attention(st):
    nc = st.nc
    with contextlib.ExitStack() as es:
        def sb(name, shape, dt=BF16):
            return es.enter_context(nc.sbuf_tensor(_un(name), shape, dt))
        Ks = [sb(f"t_K{i}", [128, S]) for i in range(2)]
        K2 = [sb(f"t_K2{i}", [128, S]) for i in range(2)]
        Vh = [sb(f"t_V{i}", [128, NKC * 128]) for i in range(2)]
        Al = [sb(f"t_al{i}", [128, 512], F32) for i in range(2)]
        Ac = [sb(f"t_ac{i}", [128, 256], F32) for i in range(2)]
        Nk = [sb(f"t_nk{i}", [1, S], F32) for i in range(2)]
        Nkm = [sb(f"t_nkm{i}", [1, 1], F32) for i in range(2)]
        Nkmb = [sb(f"t_nkmb{i}", [1, 1]) for i in range(2)]
        Mk = [sb(f"t_mk{i}", [128, 1], F32) for i in range(2)]
        Qs = [sb(f"t_q{i}", [128, 512]) for i in range(2)]
        Ssb = [sb(f"t_S{i}", [128, S], F32) for i in range(2)]
        Psb = [sb(f"t_P{i}", [128, S]) for i in range(2)]
        Pt = [sb(f"t_pt{i}", [128, S]) for i in range(2)]
        Qd = sb("t_qd", [128, 128])
        Qn2 = [sb(f"t_qn2{i}", [128, 1], F32) for i in range(2)]
        Qn = [sb(f"t_qn{i}", [128, 1], F32) for i in range(2)]
        Ngm = [sb(f"t_ngm{i}", [128, 1], F32) for i in range(2)]
        Ls = [sb(f"t_l{i}", [128, 1], F32) for i in range(2)]
        Rl = [sb(f"t_rl{i}", [128, 1], F32) for i in range(2)]
        RlA = [sb(f"t_rlA{i}", [128, 1], F32) for i in range(2)]
        RlB = [sb(f"t_rlB{i}", [128, 1], F32) for i in range(2)]
        CtA = [sb(f"t_ctA{i}", [128, 128]) for i in range(2)]
        CtB = [sb(f"t_ctB{i}", [128, 128]) for i in range(2)]
        psS = [es.enter_context(nc.psum_tensor(f"t_psS{i}", [128, 512], F32)) for i in range(2)]
        ptp = [es.enter_context(nc.psum_tensor(_un(f"t_ptp{i}"), [128, 1024], BF16)) for i in range(2)]
        pctx = [es.enter_context(nc.psum_tensor(_un(f"t_pctx{i}"), [128, 512], F32)) for i in range(2)]
        pnk = es.enter_context(nc.psum_tensor(_un("t_pnk"), [128, 512], F32))
        pbc = es.enter_context(nc.psum_tensor(_un("t_pbc"), [128, 512], F32))

        em = Em(nc)
        job_pe = [0, 0]
        job_ac = [0, 0]
        job_dv = [0, 0]
        sS_war = [0, 0]
        ptp_war = [0, 0]
        pctx_war = [0, 0]
        Ssb_war = [0, 0]
        Psb_war = [0, 0]
        Pt_war = [0, 0]
        ct_war = [0, 0]
        nS = 0
        nkg = 0
        for hl in range(NJ):
            jp = hl % 2
            em.op("sync", lambda e, h=hl, p=jp: e.dma_start(
                out=Ks[p][:, :], in_=st.ks.ap()[128 * h:128 * h + 128, :]),
                wait=[(st.sPE, job_pe[jp]), (st.sAC, job_ac[jp]), (st.sDV, job_dv[jp]),
                      (st.S[0], st.S[0].n if hl == 0 else 0),
                      (st.S[1], st.S[1].n if hl == 0 else 0)],
                inc=st.A[jp], amt=16)
            em.op("sync", lambda e, h=hl, p=jp: e.dma_start(
                out=Al[p][:, :], in_=st.al2t.ap()[128 * h:128 * h + 128, :]),
                inc=st.A[jp], amt=16)
            em.op("sync", lambda e, h=hl, p=jp: e.dma_start(
                out=Ac[p][:, :], in_=st.actt.ap()[128 * h:128 * h + 128, :]),
                inc=st.A[jp], amt=16)
            for kc in range(NKC):
                em.op("sync", lambda e, h=hl, k=kc, p=jp: e.dma_start(
                    out=Vh[p][:, 128 * k:128 * k + 128],
                    in_=st.vsf.ap()[128 * h:128 * h + 128, 128 * k:128 * k + 128],
                    transpose=True),
                    inc=st.A[jp], amt=16)
            v_kld = st.A[jp].n
            em.op("scalar", lambda e, p=jp: e.activation(
                out=K2[p][:, :], in_=Ks[p][:, :], func=AF.Square),
                wait=[(st.A[jp], v_kld)], inc=st.sAC)
            v_k2 = st.sAC.n
            for kb in range(8):
                em.op("tensor", lambda e, p=jp, k=kb: e.matmul(
                    pnk[0:1, :], st.ones128s[:, :], K2[p][:, 512 * k:512 * k + 512],
                    start=True, stop=True),
                    wait=[(st.sAC, v_k2 if kb == 0 else st.sAC.n),
                          (st.sDV, st.v_memset)],
                    inc=st.sPE)
                v_nk = st.sPE.n
                em.op("scalar", lambda e, p=jp, k=kb: e.activation(
                    out=Nk[p][0:1, 512 * k:512 * k + 512], in_=pnk[0:1, :],
                    func=AF.Identity),
                    wait=[(st.sPE, v_nk)], inc=st.sAC)
            v_nkev = st.sAC.n
            em.op("vector", lambda e, p=jp: e.reduce_max(
                out=Nkm[p][0:1, :], in_=Nk[p][0:1, :], axis=AX.X),
                wait=[(st.sAC, v_nkev)], inc=st.sDV)
            v_max = st.sDV.n
            em.op("scalar", lambda e, p=jp: e.activation(
                out=Nkmb[p][0:1, :], in_=Nkm[p][0:1, :], func=AF.Sqrt),
                wait=[(st.sDV, v_max)], inc=st.sAC)
            v_sq = st.sAC.n
            em.op("tensor", lambda e, p=jp: e.matmul(
                pbc[:, 0:1], st.ones1s[0:1, :], Nkmb[p][0:1, 0:1],
                start=True, stop=True),
                wait=[(st.sAC, v_sq)], inc=st.sPE)
            v_bc = st.sPE.n
            em.op("scalar", lambda e, p=jp: e.activation(
                out=Mk[p][:, :], in_=pbc[:, 0:1], func=AF.Identity),
                wait=[(st.sPE, v_bc)], inc=st.sAC)
            v_mk = st.sAC.n

            q_use_pe = [job_pe[0], job_pe[1]]
            q_use_ac = [job_ac[0], job_ac[1]]
            for j in range(NSUP):
                E = j + 1
                qp = j % 2
                em.op("sync", lambda e, h=hl, jj=j, p=qp: e.dma_start(
                    out=Qs[p][:, :], in_=st.qs.ap()[128 * h:128 * h + 128, 512 * jj:512 * jj + 512]),
                    wait=[(st.sPE, q_use_pe[qp]), (st.sAC, q_use_ac[qp])],
                    inc=st.A[2 + qp], amt=16)
                v_qld = st.A[2 + qp].n
                for s_ in range(4):
                    sp = nS % 2
                    nS += 1
                    em.op("scalar", lambda e, p=qp, ss=s_, o=sp: e.activation(
                        out=Qd[:, :], in_=Qs[p][:, 128 * ss:128 * ss + 128],
                        func=AF.Square, accum_out=Qn2[o][:, :]),
                        wait=[(st.A[2 + qp], v_qld)], inc=st.sAC)
                    v_q2 = st.sAC.n
                    em.op("scalar", lambda e, o=sp: e.activation(
                        out=Qn[o][:, :], in_=Qn2[o][:, :], func=AF.Sqrt),
                        wait=[(st.sAC, v_q2)], inc=st.sAC)
                    v_qn = st.sAC.n
                    em.op("vector", lambda e, o=sp, p=jp: e.tensor_scalar(
                        out=Ngm[o][:, :], in0=Qn[o][:, :], scalar1=Mk[p][:, :],
                        scalar2=-1.0, op0=ALU.mult, op1=ALU.mult),
                        wait=[(st.sAC, max(v_qn, v_mk))], inc=st.sDV)
                    v_ngm = st.sDV.n
                    for kb in range(E):
                        bp = nS % 2 if E == 1 else (nS + kb) % 2
                        em.op("tensor", lambda e, p=qp, ss=s_, k=kb, b=bp, h=jp: e.matmul(
                            psS[b][:, :], Qs[p][:, 128 * ss:128 * ss + 128],
                            Ks[h][:, 512 * k:512 * k + 512], start=True, stop=True),
                            wait=[(st.sDV, sS_war[bp]), (st.A[jp], v_kld),
                                  (st.A[2 + qp], v_qld)], inc=st.sPE)
                        v_s = st.sPE.n
                        idx = j * 32 + s_ * 8 + kb
                        em.op("vector", lambda e, b=bp, k=kb, p=jp, ii=idx, o=sp: e.scalar_tensor_tensor(
                            out=Ssb[o][:, 512 * k:512 * k + 512], in0=psS[b][:, :],
                            scalar=Ac[p][:, ii:ii + 1], in1=Al[p][:, :],
                            op0=ALU.add, op1=ALU.add),
                            wait=[(st.sPE, v_s), (st.sAC, Ssb_war[sp]),
                                  (st.A[jp], v_kld), (st.sSU, st.v_setup)],
                            inc=st.sDV)
                        sS_war[bp] = st.sDV.n
                        if kb == j:
                            em.op("vector", lambda e, k=kb, ss=s_, o=sp: e.tensor_tensor(
                                out=Ssb[o][:, 512 * k:512 * k + 512],
                                in0=Ssb[o][:, 512 * k:512 * k + 512],
                                in1=st.maskws[:, 512 - 128 * ss:1024 - 128 * ss],
                                op=ALU.add), inc=st.sDV)
                    sttv = st.sDV.n
                    em.op("scalar", lambda e, o=sp, EE=E: e.activation(
                        out=Psb[o][:, 0:512 * EE], in_=Ssb[o][:, 0:512 * EE],
                        func=AF.Exp, bias=Ngm[o][:, :], accum_out=Ls[o][:, :]),
                        wait=[(st.sDV, max(sttv, v_ngm)),
                              (st.sPE, Psb_war[sp])],
                        inc=st.sAC)
                    v_exp = st.sAC.n
                    Ssb_war[sp] = v_exp
                    em.op("vector", lambda e, o=sp: e.reciprocal(Rl[o][:, :], Ls[o][:, :]),
                          wait=[(st.sAC, v_exp)], inc=st.sDV)
                    v_rc = st.sDV.n
                    em.op("vector", lambda e, o=sp: e.tensor_scalar(
                        out=RlA[o][:, :], in0=Rl[o][:, :], scalar1=st.mas[:, :],
                        scalar2=None, op0=ALU.mult),
                        wait=[(st.sDV, v_rc), (st.sSU, st.v_setup)])
                    em.op("vector", lambda e, o=sp: e.tensor_scalar(
                        out=RlB[o][:, :], in0=Rl[o][:, :], scalar1=st.mbs[:, :],
                        scalar2=None, op0=ALU.mult), inc=st.sDV)
                    v_rls = st.sDV.n
                    pv_first = True
                    for kg in range(E):
                        tp = nkg % 2
                        nkg += 1
                        for cc in range(4):
                            kc = 4 * kg + cc
                            em.op("tensor", lambda e, o=sp, k=kc, c=cc, b=tp: e.transpose(
                                ptp[b][:, 128 * c:128 * c + 128],
                                Psb[o][:, 128 * k:128 * k + 128], st.idents[:, :]),
                                wait=[(st.sAC, max(v_exp, ptp_war[tp]))],
                                inc=(st.sPE if cc == 3 else None))
                        v_tp = st.sPE.n
                        if kg == E - 1:
                            Psb_war[sp] = v_tp
                        em.op("scalar", lambda e, o=sp, g=kg, b=tp: e.activation(
                            out=Pt[o][:, 512 * g:512 * g + 512], in_=ptp[b][:, 0:512],
                            func=AF.Identity),
                            wait=[(st.sPE, v_tp), (st.sPE, Pt_war[sp])],
                            inc=st.sAC)
                        ptp_war[tp] = st.sAC.n
                        v_ptev = st.sAC.n
                        for cc in range(4):
                            kc = 4 * kg + cc
                            em.op("tensor", lambda e, o=sp, k=kc, p=jp, fi=pv_first,
                                  la=(kc == 4 * E - 1), c=sp: e.matmul(
                                pctx[c][:, 0:128], Pt[o][:, 128 * k:128 * k + 128],
                                Vh[p][:, 128 * k:128 * k + 128],
                                start=fi, stop=la),
                                wait=[(st.sAC, max(v_ptev, pctx_war[sp]))],
                                inc=(st.sPE if kc == 4 * E - 1 else None))
                            pv_first = False
                    v_pv = st.sPE.n
                    Pt_war[sp] = v_pv
                    em.op("scalar", lambda e, o=sp: e.activation(
                        out=CtA[o][:, :], in_=pctx[o][:, 0:128], func=AF.Identity,
                        scale=RlA[o][:, :]),
                        wait=[(st.sPE, v_pv), (st.sDV, v_rls), (st.S[sp], ct_war[sp])],
                        inc=st.sAC)
                    em.op("scalar", lambda e, o=sp: e.activation(
                        out=CtB[o][:, :], in_=pctx[o][:, 0:128], func=AF.Identity,
                        scale=RlB[o][:, :]), inc=st.sAC)
                    v_ct = st.sAC.n
                    pctx_war[sp] = v_ct
                    rowb = 512 * j + 128 * s_
                    em.op("sync", lambda e, h=hl, r=rowb, o=sp: e.dma_start(
                        out=st.cc2i.ap()[r:r + 128, 128 * h:128 * h + 128],
                        in_=CtA[o][:, :]),
                        wait=[(st.sAC, v_ct)], inc=st.S[sp], amt=16)
                    em.op("sync", lambda e, h=hl, r=rowb, o=sp: e.dma_start(
                        out=st.cc2i.ap()[r:r + 128, 2048 + 128 * h:2048 + 128 * h + 128],
                        in_=CtB[o][:, :]),
                        inc=st.S[sp], amt=16)
                    ct_war[sp] = st.S[sp].n
                q_use_pe[qp] = st.sPE.n
                q_use_ac[qp] = st.sAC.n
            job_pe[jp] = st.sPE.n
            job_ac[jp] = st.sAC.n
            job_dv[jp] = st.sDV.n
        em.run()


def _cc2(st):
    nc = st.nc
    em = Em(nc)
    em.op("gpsimd", lambda e: e.collective_compute(
        "ReduceScatter", ALU.add, replica_groups=PAIRS,
        ins=[st.cc2i.ap()], outs=[st.cc2o.ap()]),
        wait=[(st.S[0], st.S[0].n), (st.S[1], st.S[1].n)], inc=st.sCC, amt=1)
    v = st.sCC.n
    em.op("gpsimd", lambda e: None, wait=[(st.sCC, v)])
    em.run()


def _outproj(st):
    nc = st.nc
    with contextlib.ExitStack() as es:
        def sb(name, shape, dt=BF16):
            return es.enter_context(nc.sbuf_tensor(_un(name), shape, dt))
        ctxf = sb("o_ctxf", [128, NKC * 1024])
        wo = [sb(f"o_wo{i}", [128, NKC * 128]) for i in range(2)]
        dds = [sb(f"o_dd{i}", [128, 512]) for i in range(2)]
        po = [es.enter_context(nc.psum_tensor(_un(f"o_po{i}"), [128, 512], F32)) for i in range(2)]
        em = Em(nc)
        wuse = [0, 0]
        evac = [0, 0]
        stv = [0, 0]
        ctx_use = 0
        for th in range(2):
            for tl in range(2):
                for kc in range(NKC):
                    em.op("sync", lambda e, t=th, l=tl, k=kc: e.dma_start(
                        out=ctxf[:, 1024 * k + 512 * l:1024 * k + 512 * l + 512],
                        in_=st.cc2o.ap()[512 * (2 * t + l):512 * (2 * t + l) + 512,
                                         128 * k:128 * k + 128],
                        transpose=True),
                        wait=[(st.sPE, ctx_use)], inc=st.A[2], amt=16)
            v_cld = st.A[2].n
            for oc in range(32):
                par = oc % 2
                em.op("sync", lambda e, o=oc, p=par: e.dma_start(
                    out=wo[p][:, :], in_=st.wob.ap()[128 * o:128 * o + 128, :]),
                    wait=[(st.cWO, st.vWO), (st.sPE, wuse[par])],
                    inc=st.A[par], amt=16)
                v_wld = st.A[par].n
                for tl in range(2):
                    pp = (oc * 2 + tl) % 2
                    for kc in range(NKC):
                        em.op("tensor", lambda e, p=par, k=kc, l=tl, b=pp: e.matmul(
                            po[b][:, :], wo[p][:, 128 * k:128 * k + 128],
                            ctxf[:, 1024 * k + 512 * l:1024 * k + 512 * l + 512],
                            start=(k == 0), stop=(k == NKC - 1)),
                            wait=[(st.A[2], v_cld), (st.A[par], v_wld),
                                  (st.sAC, evac[pp])],
                            inc=(st.sPE if kc == NKC - 1 else None))
                    v_mm = st.sPE.n
                    if tl == 1:
                        wuse[par] = st.sPE.n
                        if oc == 31:
                            ctx_use = st.sPE.n
                    em.op("scalar", lambda e, o=oc, b=pp: e.activation(
                        out=dds[b][:, :], in_=po[b][:, :], func=AF.Identity,
                        bias=st.bos[:, o:o + 1]),
                        wait=[(st.sPE, v_mm), (st.S[pp], stv[pp]),
                              (st.sSU, st.v_setup)], inc=st.sAC)
                    evac[pp] = st.sAC.n
                    em.op("sync", lambda e, o=oc, t=th, l=tl, b=pp: e.dma_start(
                        out=st.ddelta.ap()[128 * o:128 * o + 128,
                                           512 * (2 * t + l):512 * (2 * t + l) + 512],
                        in_=dds[b][:, :]),
                        wait=[(st.sAC, evac[pp])], inc=st.S[pp], amt=16)
                    stv[pp] = st.S[pp].n
        em.run()


def _final(st):
    nc = st.nc
    em = Em(nc)
    w = [(st.S[0], st.S[0].n), (st.S[1], st.S[1].n)]
    em.op("sync", lambda e: None, wait=w)
    em.op("gpsimd", lambda e: None, wait=w)
    for tname, tout in st.taps.items():
        src = getattr(st, tname)
        em.op("gpsimd", lambda e, s=src, d=tout: e.dma_start(out=d.ap(), in_=s.ap()),
              inc=st.sCAo, amt=16)
    if st.taps:
        em.op("gpsimd", lambda e: None, wait=[(st.sCAo, st.sCAo.n)])
    em.run()


def _prep_inputs(c, I):
    half = c % 2
    b = c // 2
    f32 = np.float32
    bf16 = ml_dtypes.bfloat16

    x = np.ascontiguousarray(I["L0_attn_hidden_states"][b, half * TOK:(half + 1) * TOK, :],
                             dtype=f32)

    heads = [16 * half + i for i in range(16)]
    wqkv = I["L0_attn_w_qkv"]
    cols = []
    for base in (0, D, 2 * D):
        for h in heads:
            cols.extend(range(base + 128 * h, base + 128 * h + 128))
    cols = np.asarray(cols)
    wsl = np.asarray(wqkv)[:, cols]
    wqkvt = np.ascontiguousarray(
        wsl.reshape(NKC, 128, 48, 128).transpose(2, 1, 0, 3).reshape(48 * 128, NKC * 128),
        dtype=f32)
    bq = np.asarray(I["L0_attn_b_qkv"])[cols]
    bqkvt = np.ascontiguousarray(bq.reshape(48, 128).T, dtype=f32)

    wout = np.asarray(I["L0_attn_w_out"])
    woutt = np.ascontiguousarray(
        wout.reshape(NKC, 128, 32, 128).transpose(2, 1, 0, 3).reshape(32 * 128, NKC * 128),
        dtype=f32)
    boutt = np.ascontiguousarray(np.asarray(I["L0_attn_b_out"]).reshape(32, 128).T, dtype=f32)

    w1q = np.asarray(I["L0_ffn_w1q"])
    w3q = np.asarray(I["L0_ffn_w3q"])
    w2q = np.asarray(I["L0_ffn_w2q"])
    w1t = np.ascontiguousarray(
        w1q.reshape(NMC, 128, NQC, 128).transpose(0, 3, 2, 1).reshape(NMC * 128, DQ),
        dtype=f32)
    w3t = np.ascontiguousarray(
        w3q.reshape(NMC, 128, NQC, 128).transpose(0, 3, 2, 1).reshape(NMC * 128, DQ),
        dtype=f32)
    w2p = np.concatenate([w2q, np.zeros((D, NHC * 128 - HQ), dtype=w2q.dtype)], axis=1)
    w2t = np.ascontiguousarray(
        w2p.reshape(32, 128, NHC, 128).transpose(0, 3, 2, 1).reshape(32 * 128, NHC * 128),
        dtype=f32)

    b1t = np.ascontiguousarray(np.asarray(I["L0_ffn_b1"]).T, dtype=f32)
    b3t = np.ascontiguousarray(np.asarray(I["L0_ffn_b3"]).T, dtype=f32)
    b2t = np.ascontiguousarray(np.asarray(I["L0_ffn_b2"]).T, dtype=f32)
    a1t = np.ascontiguousarray(
        np.asarray(I["L0_ffn_a1"]).reshape(2, NKC, 128).transpose(2, 1, 0).reshape(128, 64),
        dtype=f32)
    a3t = np.ascontiguousarray(
        np.asarray(I["L0_ffn_a3"]).reshape(2, NKC, 128).transpose(2, 1, 0).reshape(128, 64),
        dtype=f32)
    a2t = np.ascontiguousarray(
        np.asarray(I["L0_ffn_a2"]).reshape(2, NMC, 128).transpose(2, 1, 0).reshape(128, 172),
        dtype=f32)

    identt = np.eye(128, dtype=bf16)
    p64t = np.roll(np.eye(128, dtype=np.float32), 64, axis=0).astype(bf16)

    fp = np.arange(1024)
    pp = np.arange(128)
    maskwt = np.where(fp[None, :] <= pp[:, None] + 512, 0.0, NEG).astype(f32)

    slopes = _alibi_slopes(32)
    al2t = np.zeros((NJ * 128, 512), dtype=f32)
    actt = np.zeros((NJ * 128, 256), dtype=f32)
    f5 = np.arange(512, dtype=f32)
    for hl in range(NJ):
        sl = slopes[16 * half + hl]
        al2t[128 * hl:128 * hl + 128, :] = sl * (f5[None, :] - pp[:, None].astype(f32))
        tab = np.zeros(256, dtype=f32)
        for j in range(8):
            for s_ in range(4):
                for kb in range(j + 1):
                    tab[j * 32 + s_ * 8 + kb] = sl * (512.0 * (kb - j) - 128.0 * s_)
        actt[128 * hl:128 * hl + 128, :] = tab[None, :]

    mat = np.full((128, 1), 1.0 if half == 0 else 0.0, dtype=f32)
    mbt = np.full((128, 1), 1.0 if half == 1 else 0.0, dtype=f32)

    return {
        "x": x, "w1t": w1t, "w3t": w3t, "w2t": w2t, "wqkvt": wqkvt, "woutt": woutt,
        "b1t": b1t, "b3t": b3t, "b2t": b2t, "a1t": a1t, "a3t": a3t, "a2t": a2t,
        "bqkvt": bqkvt, "boutt": boutt, "identt": identt, "p64t": p64t,
        "maskwt": maskwt, "al2t": al2t, "actt": actt, "mat": mat, "mbt": mbt,
    }


def kernel(**inputs):
    nc = build_nc()
    in_maps = [_prep_inputs(c, inputs) for c in range(NCORE)]
    res = run_bass_kernel_spmd(nc, in_maps, core_ids=list(range(NCORE)))
    out = np.empty((4, S, D), dtype=np.float32)
    for c in range(NCORE):
        out[c // 2, (c % 2) * TOK:(c % 2 + 1) * TOK, :] = res.results[c]["out"]
    return out


# version 05
# speedup vs baseline: 1.0753x; 1.0753x over previous
import math
import contextlib
import numpy as np
import ml_dtypes
import concourse.bass as bass
import concourse.mybir as mybir
from concourse.bass_utils import run_bass_kernel_spmd

F32 = mybir.dt.float32
BF16 = mybir.dt.bfloat16
AF = mybir.ActivationFunctionType
ALU = mybir.AluOpType
AX = mybir.AxisListType

NCORE = 8
TOK = 2048
D = 4096
S = 4096
NKC = 32
HFF = 11008
NMC = 86
DQ = 1024
NQC = 8
HQ = 2752
NHC = 22
NT = 4
NST = 16
NJ = 16
NSUP = 8
SCL = 1.0 / math.sqrt(128.0)
PAIRS = [[0, 1], [2, 3], [4, 5], [6, 7]]
NEG = -1.0e9


def _alibi_slopes(n):
    p = 2 ** math.floor(math.log2(n))
    base = 2.0 ** (-(2.0 ** -(math.log2(p) - 3)))
    sl = [base ** (i + 1) for i in range(p)]
    if p != n:
        eb = 2.0 ** (-(2.0 ** -(math.log2(2 * p) - 3)))
        sl += [eb ** (2 * i + 1) for i in range(min(p, n - p))]
    return np.asarray(sl, dtype=np.float32)


def _hs_plans():
    plans = []
    first = {}
    for mc in range(NMC):
        h0 = 128 * mc
        bounds = {0, 64, 128}
        r = (-h0) % HQ
        if 0 < r < 128:
            bounds.add(r)
        bounds = sorted(bounds)
        segs = []
        for lo, hi in zip(bounds[:-1], bounds[1:]):
            hp = (h0 + lo) % HQ
            c, dlo = hp // 128, hp % 128
            rot = (dlo != lo)
            if rot:
                assert dlo == (lo + 64) % 128, (mc, lo, dlo)
            key = (c, dlo, hi - lo)
            isfirst = key not in first and (c, dlo) not in {(k[0], k[1]) for k in first}
            kk = (c, dlo)
            isfirst = kk not in first
            if isfirst:
                first[kk] = mc
                assert not rot, f"first write must be aligned mc={mc} seg={lo}"
            segs.append((lo, hi, c, dlo, rot, isfirst))
        plans.append(segs)
    merged = []
    for segs in plans:
        out = []
        for sg in segs:
            if out:
                lo, hi, c, dlo, rot, fi = out[-1]
                lo2, hi2, c2, dlo2, rot2, fi2 = sg
                if (not rot and not rot2 and c == c2 and hi == lo2
                        and dlo + (hi - lo) == dlo2 and fi == fi2):
                    out[-1] = (lo, hi2, c, dlo, rot, fi)
                    continue
            out.append(sg)
        merged.append(out)
    return merged


HS_PLANS = _hs_plans()

_UID = [0]


def _un(name):
    _UID[0] += 1
    return f"{name}_{_UID[0]}"


class Sem:
    def __init__(self, es, nc, name):
        self.h = es.enter_context(nc.semaphore(name))
        self.n = 0
        self.name = name

    def bump(self, amt):
        self.n += amt
        return self.n


class Em:
    ENGS = ("sync", "tensor", "scalar", "vector", "gpsimd")

    def __init__(self, nc):
        self.nc = nc
        self.ops = {e: [] for e in self.ENGS}
        self.lastwait = {e: {} for e in self.ENGS}

    def op(self, eng, fn, wait=(), inc=None, amt=1):
        w = []
        lw = self.lastwait[eng]
        for sem, v in wait:
            if sem is None or v is None or v <= 0:
                continue
            if lw.get(id(sem), 0) >= v:
                continue
            lw[id(sem)] = v
            w.append((sem, v))
        ret = inc.bump(amt) if inc is not None else None
        self.ops[eng].append((w, fn, inc, amt))
        return ret

    def run(self):
        with self.nc.Block() as blk:
            for eng in self.ENGS:
                if not self.ops[eng]:
                    continue

                def body(e, _ops=self.ops[eng]):
                    for w, fn, inc, amt in _ops:
                        for sem, v in w:
                            e.wait_ge(sem.h, v)
                        ins = fn(e)
                        if inc is not None:
                            ins.then_inc(inc.h, amt)
                getattr(blk, eng)(body)


class St:
    pass


def build_nc(taps=(), snaps=(), expt=()):
    nc = bass.Bass("TRN2", target_bir_lowering=False, num_devices=8)
    st = St()
    st.nc = nc
    st.want_snaps = set(snaps)
    st.snap_outs = {}
    st.expt = set(expt)

    def dri(name, shape):
        return nc.dram_tensor(name, shape, F32, kind="ExternalInput")

    def drib(name, shape):
        return nc.dram_tensor(name, shape, BF16, kind="ExternalInput")

    st.xin = dri("x", [TOK, D])
    st.w1t = dri("w1t", [NMC * 128, DQ])
    st.w3t = dri("w3t", [NMC * 128, DQ])
    st.w2t = dri("w2t", [NKC * 128, NHC * 128])
    st.wqkvt = dri("wqkvt", [48 * 128, NKC * 128])
    st.woutt = dri("woutt", [32 * 128, NKC * 128])
    st.b1t = dri("b1t", [2, HFF])
    st.b3t = dri("b3t", [2, HFF])
    st.b2t = dri("b2t", [2, D])
    st.a1t = dri("a1t", [128, 64])
    st.a3t = dri("a3t", [128, 64])
    st.a2t = dri("a2t", [128, 172])
    st.bqkvt = dri("bqkvt", [128, 48])
    st.boutt = dri("boutt", [128, 32])
    st.identt = drib("identt", [128, 128])
    st.p64t = drib("p64t", [128, 128])
    st.maskwt = dri("maskwt", [128, 1024])
    st.al2t = dri("al2t", [NJ * 128, 512])
    st.actt = dri("actt", [NJ * 128, 256])
    st.mat = dri("mat", [128, 1])
    st.mbt = dri("mbt", [128, 1])

    st.out = nc.dram_tensor("out", [TOK, D], F32, kind="ExternalOutput")

    def drs(name, shape, dt=BF16):
        return nc.dram_tensor(name, shape, dt)

    st.w1b = drs("w1b", [NMC * 128, DQ])
    st.w3b = drs("w3b", [NMC * 128, DQ])
    st.b1b = drs("b1b", [2, HFF])
    st.b3b = drs("b3b", [2, HFF])
    st.w2b = drs("w2b", [NKC * 128, NHC * 128])
    st.wqb = drs("wqb", [48 * 128, NKC * 128])
    st.wob = drs("wob", [32 * 128, NKC * 128])
    st.lntm = drs("lntm", [TOK, D])
    st.ddelta = drs("ddelta", [D, TOK])
    st.x1res = drs("x1res", [TOK, D], F32)
    st.x2res = drs("x2res", [TOK, D], F32)
    st.cc1i = drs("cc1i", [TOK, D])
    st.cc1o = drs("cc1o", [2 * TOK, D])
    st.xlnf = drs("xlnf", [D, 2 * TOK])
    st.vst = drs("vst", [128, NJ * S // 128 * 128])
    st.qs = drs("qs", [NJ * 128, S])
    st.ks = drs("ks", [NJ * 128, S])
    st.vsf = drs("vsf", [NJ * 128, S])
    st.cc2i = drs("cc2i", [2 * TOK, 2 * D // 2])
    st.cc2o = drs("cc2o", [TOK, 2 * D // 2])

    st.taps = {}
    for tname in taps:
        src = getattr(st, tname)
        shp = list(src.shape)
        dt = src.dtype if hasattr(src, "dtype") else F32
        st.taps[tname] = nc.dram_tensor("tap_" + tname, shp, dt, kind="ExternalOutput")

    with contextlib.ExitStack() as es:
        st.es = es
        st.A = [Sem(es, nc, f"sA{i}") for i in range(4)]
        st.S = [Sem(es, nc, f"sS{i}") for i in range(2)]
        st.sSU = Sem(es, nc, "sSU")
        st.cW13 = Sem(es, nc, "cW13")
        st.cW2 = Sem(es, nc, "cW2")
        st.cWQ = Sem(es, nc, "cWQ")
        st.cWO = Sem(es, nc, "cWO")
        st.cAS = Sem(es, nc, "cAS")
        st.sCAo = Sem(es, nc, "sCAo")
        st.sLB = Sem(es, nc, "sLB")
        st.sCC = Sem(es, nc, "sCC")
        st.sPE = Sem(es, nc, "sPE")
        st.sAC = Sem(es, nc, "sAC")
        st.sDV = Sem(es, nc, "sDV")

        def sb(name, shape, dt=BF16):
            return es.enter_context(nc.sbuf_tensor(_un(name), shape, dt))

        st.b2s = sb("b2s", [2, D])
        st.a1s = sb("a1s", [128, 64])
        st.a3s = sb("a3s", [128, 64])
        st.a2s = sb("a2s", [128, 172])
        st.bqs = sb("bqs", [128, 48], F32)
        st.bqsc = sb("bqsc", [128, 16], F32)
        st.bos = sb("bos", [128, 32], F32)
        st.maskws = sb("maskws", [128, 1024], F32)
        st.idents = sb("idents", [128, 128])
        st.p64s = sb("p64s", [128, 128])
        st.ones1s = sb("ones1s", [1, 128])
        st.ones128s = sb("ones128s", [128, 1])
        st.mas = sb("mas", [128, 1], F32)
        st.mbs = sb("mbs", [128, 1], F32)
        st.u1s = sb("u1s", [2, TOK])
        st.u3s = sb("u3s", [2, TOK])
        st.u2s = sb("u2s", [2, TOK])

        _ffn_pass(st, src=st.xin, dst=st.x1res, write_ln2=True, tag="f1",
                  pre_emit=lambda em: _setup_and_casts(st, em))
        _cc1(st)
        _stage_xlnf(st)
        _qkv(st)
        _attention(st)
        _cc2(st)
        _outproj(st)
        _epilogue_pass(st, delta_scale=1.0, src=st.x1res, dst=st.x2res, write_ln2=False)
        _ffn_pass(st, src=st.x2res, dst=st.out, write_ln2=False, tag="f3")
        _final(st)
    return nc


def _snap(st, tag, src, shape=None, dt=None):
    if tag not in st.want_snaps:
        return
    nc = st.nc
    if shape is None:
        shape = list(src.shape)
    if dt is None:
        dt = src.dtype
    out = nc.dram_tensor("snap_" + tag, shape, dt, kind="ExternalOutput")
    st.snap_outs[tag] = out
    em = Em(nc)
    em.op("gpsimd", lambda e: e.dma_start(out=out.ap(), in_=(src.ap() if hasattr(src, "ap") else src[:, :])),
          wait=[(st.S[0], st.S[0].n), (st.S[1], st.S[1].n),
                (st.sDV, st.sDV.n), (st.sAC, st.sAC.n),
                (st.sPE, st.sPE.n), (st.sCAo, st.sCAo.n)],
          inc=st.sCC, amt=16)
    em.op("gpsimd", lambda e: None, wait=[(st.sCC, st.sCC.n)])
    em.run()


def _setup_and_casts(st, em):

    for dst, src in [(st.bqs, st.bqkvt), (st.bos, st.boutt), (st.maskws, st.maskwt),
                     (st.idents, st.identt), (st.p64s, st.p64t),
                     (st.mas, st.mat), (st.mbs, st.mbt)]:
        em.op("sync", lambda e, d=dst, s=src: e.dma_start(out=d[:, :], in_=s.ap()),
              inc=st.sSU, amt=16)
    st.v_setup = st.sSU.n

    em.op("vector", lambda e: e.memset(st.ones1s[:, :], 1.0))
    em.op("vector", lambda e: e.memset(st.ones128s[:, :], 1.0), inc=st.sDV)
    st.v_memset = st.sDV.n

    for dst, src in [(st.a1s, st.a1t), (st.a3s, st.a3t), (st.a2s, st.a2t)]:
        em.op("gpsimd", lambda e, d=dst, s=src: e.dma_start(
            out=d[:, :], in_=s.ap(), max_dma_last_dim=8192), inc=st.cAS, amt=16)
    st.v_small_cast = st.cAS.n
    em.op("gpsimd", lambda e: e.dma_start(out=st.b2s[:, :], in_=st.b2t.ap(),
                                          max_dma_last_dim=8192),
          inc=st.sCAo, amt=16)
    for dst, src in [(st.b1b, st.b1t), (st.b3b, st.b3t)]:
        em.op("gpsimd", lambda e, d=dst, s=src: e.dma_start(
            out=d.ap(), in_=s.ap(), max_dma_last_dim=8192), inc=st.sCAo, amt=16)
    st.v_odd_cast = st.sCAo.n

    def cast_chunks(sem, dst, src, nrows, nchunks):
        step = nrows // nchunks
        assert step % 16 == 0
        for i in range(nchunks):
            r0, r1 = i * step, (i + 1) * step if i < nchunks - 1 else nrows
            em.op("gpsimd", lambda e, a=r0, b=r1, d=dst, s=src: e.dma_start(
                out=d.ap()[a:b, :], in_=s.ap()[a:b, :], max_dma_last_dim=8192),
                inc=sem, amt=16)
        return sem.n

    st.vW13 = cast_chunks(st.cW13, st.w1b, st.w1t, NMC * 128, 8)
    st.vW13 = cast_chunks(st.cW13, st.w3b, st.w3t, NMC * 128, 8)
    st.vW2 = cast_chunks(st.cW2, st.w2b, st.w2t, NKC * 128, 4)
    st.vWQ = cast_chunks(st.cWQ, st.wqb, st.wqkvt, 48 * 128, 8)
    st.vWO = cast_chunks(st.cWO, st.wob, st.woutt, 32 * 128, 4)


def _ln_chain(st, em, src_ap_fn, dst_sb, stats, war_dst=()):
    sbn, sagg, sstd, srstd, epss = stats
    src = src_ap_fn
    for k in range(8):
        if k < 7:
            em.op("vector", lambda e, kk=k: e.bn_stats(sbn[:, 6 * kk:6 * kk + 6],
                                                       src[:, 512 * kk:512 * kk + 512]))
        else:
            em.op("vector", lambda e, kk=k: e.bn_stats(sbn[:, 6 * kk:6 * kk + 6],
                                                       src[:, 512 * kk:512 * kk + 512]),
                  inc=st.sDV)
    v_bn = st.sDV.n
    em.op("vector", lambda e: e.bn_aggr(sagg[:, :], sbn[:, :]),
          wait=[(st.sDV, v_bn)], inc=st.sDV)
    v_agg = st.sDV.n
    em.op("scalar", lambda e: e.activation(out=sstd[:, :], in_=sagg[:, 1:2],
                                           func=AF.Sqrt, bias=epss[:, :]),
          wait=[(st.sDV, v_agg)], inc=st.sAC)
    v_sqrt = st.sAC.n
    em.op("vector", lambda e: e.reciprocal(srstd[:, :], sstd[:, :]),
          wait=[(st.sAC, v_sqrt)], inc=st.sDV)
    v_rec = st.sDV.n
    em.op("vector", lambda e: e.tensor_scalar(out=dst_sb[:, :], in0=src[:, :],
                                              scalar1=sagg[:, 0:1], scalar2=srstd[:, :],
                                              op0=ALU.subtract, op1=ALU.mult),
          wait=[(st.sDV, v_rec)] + list(war_dst), inc=st.sDV)
    return st.sDV.n


def _ffn_pass(st, src, dst, write_ln2, tag="f1", pre_emit=None):
    nc = st.nc
    is_out = (dst is st.out)

    with contextlib.ExitStack() as es:
        def sb(name, shape, dt=BF16):
            return es.enter_context(nc.sbuf_tensor(_un(name), shape, dt))
        xsub = [sb(f"a_xsub{i}", [128, D], F32) for i in range(2)]
        xlnt = [sb(f"a_xlnt{i}", [128, D]) for i in range(2)]
        sbn = sb("a_sbn", [128, 48], F32)
        sagg = sb("a_sagg", [128, 2], F32)
        sstd = sb("a_sstd", [128, 1], F32)
        srstd = sb("a_srstd", [128, 1], F32)
        epss = sb("a_eps", [128, 1], F32)
        em = Em(nc)
        if pre_emit is not None:
            pre_emit(em)
        em.op("vector", lambda e: e.memset(epss[:, :], 1e-5), inc=st.sDV)
        norm_v = [0, 0]
        store_v = [0, 0]
        for t in range(NST):
            par = t % 2
            em.op("sync", lambda e, tt=t, p=par: e.dma_start(
                out=xsub[p][:, :], in_=src.ap()[128 * tt:128 * tt + 128, :]),
                wait=[(st.sDV, norm_v[par])],
                inc=st.A[par], amt=16)
            v_ld = st.A[par].n
            em.op("vector", lambda e, p=par: e.bn_stats(sbn[:, 0:6], xsub[p][:, 0:512]),
                  wait=[(st.A[par], v_ld)])
            for k in range(1, 8):
                if k < 7:
                    em.op("vector", lambda e, kk=k, p=par: e.bn_stats(
                        sbn[:, 6 * kk:6 * kk + 6], xsub[p][:, 512 * kk:512 * kk + 512]))
                else:
                    em.op("vector", lambda e, kk=k, p=par: e.bn_stats(
                        sbn[:, 6 * kk:6 * kk + 6], xsub[p][:, 512 * kk:512 * kk + 512]),
                        inc=st.sDV)
            v_bn = st.sDV.n
            em.op("vector", lambda e: e.bn_aggr(sagg[:, :], sbn[:, :]),
                  wait=[(st.sDV, v_bn)], inc=st.sDV)
            v_agg = st.sDV.n
            em.op("scalar", lambda e: e.activation(out=sstd[:, :], in_=sagg[:, 1:2],
                                                   func=AF.Sqrt, bias=epss[:, :]),
                  wait=[(st.sDV, v_agg)], inc=st.sAC)
            v_sq = st.sAC.n
            em.op("vector", lambda e: e.reciprocal(srstd[:, :], sstd[:, :]),
                  wait=[(st.sAC, v_sq)], inc=st.sDV)
            v_rec = st.sDV.n
            em.op("vector", lambda e, p=par: e.tensor_scalar(
                out=xlnt[p][:, :], in0=xsub[p][:, :], scalar1=sagg[:, 0:1],
                scalar2=srstd[:, :], op0=ALU.subtract, op1=ALU.mult),
                wait=[(st.sDV, v_rec), (st.S[par], store_v[par])], inc=st.sDV)
            norm_v[par] = st.sDV.n
            em.op("sync", lambda e, tt=t, p=par: e.dma_start(
                out=st.lntm.ap()[128 * tt:128 * tt + 128, :], in_=xlnt[p][:, :]),
                wait=[(st.sDV, norm_v[par])], inc=st.S[par], amt=16)
            store_v[par] = st.S[par].n
        em.run()
    _snap(st, tag + "_lntm", st.lntm)

    with contextlib.ExitStack() as esp:
        def sbp(name, shape, dt=BF16):
            return esp.enter_context(nc.sbuf_tensor(_un(name), shape, dt))
        xs = sbp("p_xs", [128, NQC * TOK])
        hs = sbp("p_hs", [128, NHC * TOK])

        with contextlib.ExitStack() as es:
            def sb(name, shape, dt=BF16):
                return es.enter_context(nc.sbuf_tensor(_un(name), shape, dt))
            xfm = sb("b_xfm", [128, NKC * 512])
            t1 = sb("b_t1", [128, 512])
            t2 = sb("b_t2", [128, 512])
            pu1 = [es.enter_context(nc.psum_tensor(_un(f"b_pu1{i}"), [128, 512], F32)) for i in range(2)]
            pu3 = [es.enter_context(nc.psum_tensor(_un(f"b_pu3{i}"), [128, 512], F32)) for i in range(2)]
            em = Em(nc)
            pe_used = 0
            dv_used = 0
            evac_v = [0, 0]
            for t in range(NT):
                par = t % 2
                for kc in range(NKC):
                    em.op("sync", lambda e, tt=t, k=kc: e.dma_start(
                        out=xfm[:, 512 * k:512 * k + 512],
                        in_=st.lntm.ap()[512 * tt:512 * tt + 512, 128 * k:128 * k + 128],
                        transpose=True),
                        wait=[(st.sPE, pe_used), (st.sDV, dv_used),
                              (st.S[0], st.S[0].n if kc == 0 and t == 0 else 0),
                              (st.S[1], st.S[1].n if kc == 0 and t == 0 else 0)],
                        inc=st.A[0], amt=16)
                ldv = st.A[0].n
                for kq in range(NQC):
                    em.op("vector", lambda e, k=kq: e.tensor_tensor(
                        out=t1[:, :], in0=xfm[:, 512 * k:512 * k + 512],
                        in1=xfm[:, 512 * (k + 8):512 * (k + 8) + 512], op=ALU.add),
                        wait=[(st.A[0], ldv)])
                    em.op("vector", lambda e, k=kq: e.tensor_tensor(
                        out=t2[:, :], in0=xfm[:, 512 * (k + 16):512 * (k + 16) + 512],
                        in1=xfm[:, 512 * (k + 24):512 * (k + 24) + 512], op=ALU.add))
                    em.op("vector", lambda e, k=kq, tt=t: e.tensor_tensor(
                        out=xs[:, TOK * k + 512 * tt:TOK * k + 512 * tt + 512],
                        in0=t1[:, :], in1=t2[:, :], op=ALU.add),
                        inc=(st.sDV if k == NQC - 1 else None))
                dv_used = st.sDV.n
                for (pu, asb) in ((pu1[par], st.a1s), (pu3[par], st.a3s)):
                    for kc in range(NKC):
                        em.op("tensor", lambda e, k=kc, b=pu, a=asb: e.matmul(
                            b[0:2, :], a[:, 2 * k:2 * k + 2],
                            xfm[:, 512 * k:512 * k + 512],
                            start=(k == 0), stop=(k == NKC - 1)),
                            wait=[(st.A[0], ldv), (st.cAS, st.v_small_cast),
                                  (st.sAC, evac_v[par])],
                            inc=(st.sPE if kc == NKC - 1 else None))
                pe_used = st.sPE.n
                v_mm = st.sPE.n
                em.op("scalar", lambda e, tt=t, p=par: e.activation(
                    out=st.u1s[0:2, 512 * tt:512 * tt + 512], in_=pu1[p][0:2, :],
                    func=AF.Identity, scale=0.5),
                    wait=[(st.sPE, v_mm)], inc=st.sAC)
                em.op("scalar", lambda e, tt=t, p=par: e.activation(
                    out=st.u3s[0:2, 512 * tt:512 * tt + 512], in_=pu3[p][0:2, :],
                    func=AF.Identity, scale=0.5),
                    inc=st.sAC)
                evac_v[par] = st.sAC.n
            em.run()
        _snap(st, tag + "_xs", xs, shape=[128, NQC * TOK], dt=BF16)
        _snap(st, tag + "_u1", st.u1s, shape=[2, TOK], dt=BF16)
        _snap(st, tag + "_u3", st.u3s, shape=[2, TOK], dt=BF16)

        with contextlib.ExitStack() as es:
            def sb(name, shape, dt=BF16):
                return es.enter_context(nc.sbuf_tensor(_un(name), shape, dt))
            w1s = [sb(f"c_w1s{i}", [128, DQ]) for i in range(2)]
            w3s = [sb(f"c_w3s{i}", [128, DQ]) for i in range(2)]
            b1p = sb("c_b1p", [2, HFF])
            b3p = sb("c_b3p", [2, HFF])
            gsb = [sb(f"c_g{i}", [128, 512]) for i in range(2)]
            usb = [sb(f"c_u{i}", [128, 512]) for i in range(2)]
            NHP = 6
            hp = [sb(f"c_hp{i}", [128, 512]) for i in range(NHP)]
            pg = es.enter_context(nc.psum_tensor(_un("c_pg"), [128, 512], F32))
            pup = es.enter_context(nc.psum_tensor(_un("c_pup"), [128, 512], F32))
            pu2 = [es.enter_context(nc.psum_tensor(_un(f"c_pu2{i}"), [128, 512], F32)) for i in range(NT)]
            prot = [es.enter_context(nc.psum_tensor(_un(f"c_prot{i}"), [128, 512], F32)) for i in range(2)]

            em = Em(nc)
            IT = NMC * NT
            em.op("sync", lambda e: e.dma_start(out=b1p[:, :], in_=st.b1b.ap()),
                  wait=[(st.sCAo, st.v_odd_cast)], inc=st.sLB, amt=16)
            em.op("sync", lambda e: e.dma_start(out=b3p[:, :], in_=st.b3b.ap()),
                  inc=st.sLB, amt=16)
            v_lb = st.sLB.n
            wld_v = [0] * NMC
            wuse_v = [0, 0]
            silu_v = [0] * IT
            upev_v = [0] * IT
            mult_v = [0] * IT
            hsal_v = [0] * IT
            u2_v = [0] * IT
            rot_v = [0] * IT
            rotadd_v = [0, 0]
            g_war = 0
            u_war = 0
            nrot = 0

            for it in range(IT + 2):
                if it < IT:
                    mc, tau = divmod(it, NT)
                    par = mc % 2
                    if tau == 0:
                        for (wbuf, wsrc) in ((w1s[par], st.w1b),
                                             (w3s[par], st.w3b)):
                            em.op("sync", lambda e, d=wbuf, s=wsrc, m=mc: e.dma_start(
                                out=d[:, :], in_=s.ap()[128 * m:128 * m + 128, :]),
                                wait=[(st.cW13, st.vW13),
                                      (st.sPE, wuse_v[par])],
                                inc=st.A[par], amt=16)
                        wld_v[mc] = st.A[par].n
                    for kc in range(NQC + 1):
                        if kc < NQC:
                            em.op("tensor", lambda e, p=par, k=kc, tt=tau: e.matmul(
                                pg[:, :], w1s[p][:, 128 * k:128 * k + 128],
                                xs[:, TOK * k + 512 * tt:TOK * k + 512 * tt + 512],
                                start=(k == 0), stop=False),
                                wait=[(st.A[par], wld_v[mc]),
                                      (st.sAC, silu_v[it - 1] if it >= 1 else 0)])
                        else:
                            em.op("tensor", lambda e, m=mc, tt=tau: e.matmul(
                                pg[:, :], b1p[0:2, 128 * m:128 * m + 128],
                                st.u1s[0:2, 512 * tt:512 * tt + 512],
                                start=False, stop=True),
                                wait=[(st.sLB, v_lb)], inc=st.sPE)
                    v_g = st.sPE.n
                    for kc in range(NQC + 1):
                        if kc < NQC:
                            em.op("tensor", lambda e, p=par, k=kc, tt=tau: e.matmul(
                                pup[:, :], w3s[p][:, 128 * k:128 * k + 128],
                                xs[:, TOK * k + 512 * tt:TOK * k + 512 * tt + 512],
                                start=(k == 0), stop=False),
                                wait=[(st.sAC, upev_v[it - 1] if it >= 1 else 0)])
                        else:
                            em.op("tensor", lambda e, m=mc, tt=tau: e.matmul(
                                pup[:, :], b3p[0:2, 128 * m:128 * m + 128],
                                st.u3s[0:2, 512 * tt:512 * tt + 512],
                                start=False, stop=True), inc=st.sPE)
                    v_u = st.sPE.n
                    if tau == NT - 1:
                        wuse_v[par] = st.sPE.n

                jt = it - 2
                if 0 <= jt < IT:
                    mcj, tauj = divmod(jt, NT)
                    if "no_u2" in st.expt:
                        em.op("tensor", lambda e: None, wait=[(st.sDV, mult_v[jt])])
                        u2_v[jt] = st.sPE.n
                    else:
                        em.op("tensor", lambda e, m=mcj, tt=tauj, sl=jt % NHP: e.matmul(
                            pu2[tt][0:2, :], st.a2s[:, 2 * m:2 * m + 2], hp[sl][:, :],
                            start=(m == 0), stop=(m == NMC - 1)),
                            wait=[(st.sDV, mult_v[jt])], inc=st.sPE)
                        u2_v[jt] = st.sPE.n
                    if any(sg[4] for sg in HS_PLANS[mcj]) and "no_rot" not in st.expt:
                        parr = nrot % 2
                        em.op("tensor", lambda e, sl=jt % NHP, p=parr: e.matmul(
                            prot[p][:, :], st.p64s[:, :], hp[sl][:, :],
                            start=True, stop=True),
                            wait=[(st.sDV, rotadd_v[parr]), (st.sSU, st.v_setup)],
                            inc=st.sPE)
                        rot_v[jt] = (st.sPE.n, parr)
                        nrot += 1

                if it < IT:
                    em.op("scalar", lambda e, p=it % 2: e.activation(
                        out=gsb[p][:, :], in_=pg[:, :], func=AF.Silu),
                        wait=[(st.sPE, v_g), (st.sDV, mult_v[it - 2] if it >= 2 else 0)],
                        inc=st.sAC)
                    silu_v[it] = st.sAC.n
                    em.op("scalar", lambda e, p=it % 2: e.activation(
                        out=usb[p][:, :], in_=pup[:, :], func=AF.Identity),
                        wait=[(st.sPE, v_u)], inc=st.sAC)
                    upev_v[it] = st.sAC.n
                    slot = it % NHP
                    hp_war = 0
                    if it >= NHP:
                        hp_war = u2_v[it - NHP]
                        if rot_v[it - NHP]:
                            hp_war = max(hp_war, rot_v[it - NHP][0])
                    em.op("vector", lambda e, p=it % 2, sl=slot: e.tensor_tensor(
                        out=hp[sl][:, :], in0=gsb[p][:, :], in1=usb[p][:, :], op=ALU.mult),
                        wait=[(st.sAC, upev_v[it]), (st.sPE, hp_war)],
                        inc=st.sDV)
                    mult_v[it] = st.sDV.n
                    mc, tau = divmod(it, NT)
                    nal = sum(1 for sg in HS_PLANS[mc] if not sg[4])
                    done = 0
                    for (lo, hi, c, dlo, rot, fi) in HS_PLANS[mc]:
                        if rot:
                            continue
                        done += 1
                        dsl = slice(dlo, dlo + (hi - lo))
                        col = slice(TOK * c + 512 * tau, TOK * c + 512 * tau + 512)
                        if fi:
                            em.op("vector", lambda e, sl=slot, l=lo, h=hi, ds=dsl, co=col:
                                  e.tensor_copy(hs[ds, co], hp[sl][l:h, :]),
                                  inc=(st.sDV if done == nal else None))
                        else:
                            em.op("vector", lambda e, sl=slot, l=lo, h=hi, ds=dsl, co=col:
                                  e.tensor_tensor(out=hs[ds, co], in0=hs[ds, co],
                                                  in1=hp[sl][l:h, :], op=ALU.add),
                                  inc=(st.sDV if done == nal else None))
                    if nal:
                        hsal_v[it] = st.sDV.n

                if 0 <= jt < IT and rot_v[jt] and "no_rot" not in st.expt:
                    mcj, tauj = divmod(jt, NT)
                    vv, parr = rot_v[jt]
                    rsegs = [sg for sg in HS_PLANS[mcj] if sg[4]]
                    for i, (lo, hi, c, dlo, rot, fi) in enumerate(rsegs):
                        assert not fi
                        dsl = slice(dlo, dlo + (hi - lo))
                        col = slice(TOK * c + 512 * tauj, TOK * c + 512 * tauj + 512)
                        em.op("vector", lambda e, p=parr, ds=dsl, co=col:
                              e.tensor_tensor(out=hs[ds, co], in0=hs[ds, co],
                                              in1=prot[p][ds, :], op=ALU.add),
                              wait=[(st.sPE, vv)],
                              inc=(st.sDV if i == len(rsegs) - 1 else None))
                    rotadd_v[parr] = st.sDV.n

            for t in range(NT):
                em.op("scalar", lambda e, tt=t: e.activation(
                    out=st.u2s[0:2, 512 * tt:512 * tt + 512], in_=pu2[tt][0:2, :],
                    func=AF.Identity, scale=0.5),
                    wait=[(st.sPE, u2_v[IT - 4 + t])], inc=st.sAC)
            em.run()
        _snap(st, tag + "_hs", hs, shape=[128, NHC * TOK], dt=BF16)
        _snap(st, tag + "_u2", st.u2s, shape=[2, TOK], dt=BF16)

        with contextlib.ExitStack() as es:
            def sb(name, shape, dt=BF16):
                return es.enter_context(nc.sbuf_tensor(_un(name), shape, dt))
            w2s = [sb(f"d_w2s{i}", [128, NHC * 128]) for i in range(2)]
            dds = [sb(f"d_dd{i}", [128, 512]) for i in range(2)]
            po = [es.enter_context(nc.psum_tensor(_un(f"d_po{i}"), [128, 512], F32)) for i in range(2)]
            em = Em(nc)
            wuse = [0, 0]
            evac = [0, 0]
            stv = [0, 0]
            for oc in range(NKC):
                par = oc % 2
                em.op("sync", lambda e, o=oc, p=par: e.dma_start(
                    out=w2s[p][:, :], in_=st.w2b.ap()[128 * o:128 * o + 128, :]),
                    wait=[(st.cW2, st.vW2), (st.sPE, wuse[par])],
                    inc=st.A[par], amt=16)
                wld = st.A[par].n
                for t in range(NT):
                    it = oc * NT + t
                    pp = it % 2
                    for kc in range(NHC + 1):
                        if kc < NHC:
                            kk = 128 if kc < NHC - 1 else 64
                            em.op("tensor", lambda e, p=par, k=kc, tt=t, b=pp, K=kk: e.matmul(
                                po[b][:, :], w2s[p][0:K, 128 * k:128 * k + 128],
                                hs[0:K, TOK * k + 512 * tt:TOK * k + 512 * tt + 512],
                                start=(k == 0), stop=False),
                                wait=[(st.A[par], wld),
                                      (st.sAC, evac[pp])])
                        else:
                            em.op("tensor", lambda e, o=oc, tt=t, b=pp: e.matmul(
                                po[b][:, :], st.b2s[0:2, 128 * o:128 * o + 128],
                                st.u2s[0:2, 512 * tt:512 * tt + 512],
                                start=False, stop=True),
                                wait=[(st.sCAo, st.v_odd_cast)], inc=st.sPE)
                    v_mm = st.sPE.n
                    if t == NT - 1:
                        wuse[par] = st.sPE.n
                    em.op("scalar", lambda e, b=pp: e.activation(
                        out=dds[b][:, :], in_=po[b][:, :], func=AF.Identity),
                        wait=[(st.sPE, v_mm), (st.S[pp], stv[pp])], inc=st.sAC)
                    evac[pp] = st.sAC.n
                    em.op("sync", lambda e, o=oc, tt=t, b=pp: e.dma_start(
                        out=st.ddelta.ap()[128 * o:128 * o + 128, 512 * tt:512 * tt + 512],
                        in_=dds[b][:, :]),
                        wait=[(st.sAC, evac[pp])], inc=st.S[pp], amt=16)
                    stv[pp] = st.S[pp].n
            em.run()
        _snap(st, tag + "_dd", st.ddelta)

    _epilogue_pass(st, delta_scale=0.5, src=src, dst=dst, write_ln2=write_ln2)


def _epilogue_pass(st, delta_scale, src, dst, write_ln2):
    nc = st.nc
    with contextlib.ExitStack() as es:
        def sb(name, shape, dt=BF16):
            return es.enter_context(nc.sbuf_tensor(_un(name), shape, dt))
        dtm = [sb(f"e_dtm{i}", [128, D]) for i in range(2)]
        rsub = [sb(f"e_rsub{i}", [128, D], F32) for i in range(2)]
        xnew = [sb(f"e_xnew{i}", [128, D], F32) for i in range(2)]
        l2t = [sb(f"e_l2t{i}", [128, D]) for i in range(2)]
        sbn = sb("e_sbn", [128, 48], F32)
        sagg = sb("e_sagg", [128, 2], F32)
        sstd = sb("e_sstd", [128, 1], F32)
        srstd = sb("e_srstd", [128, 1], F32)
        epss = sb("e_eps", [128, 1], F32)
        em = Em(nc)
        em.op("vector", lambda e: e.memset(epss[:, :], 1e-5), inc=st.sDV)
        used = [0, 0]
        stv = [0, 0]
        base0, base1 = st.S[0].n, st.S[1].n
        for t in range(NST):
            par = t % 2
            for ob in range(8):
                em.op("sync", lambda e, tt=t, o=ob, p=par: e.dma_start(
                    out=dtm[p][:, 512 * o:512 * o + 512],
                    in_=st.ddelta.ap()[512 * o:512 * o + 512, 128 * tt:128 * tt + 128],
                    transpose=True),
                    wait=[(st.sDV, used[par]), (st.S[0], base0), (st.S[1], base1)],
                    inc=st.A[par], amt=16)
            em.op("sync", lambda e, tt=t, p=par: e.dma_start(
                out=rsub[p][:, :], in_=src.ap()[128 * tt:128 * tt + 128, :]),
                inc=st.A[par], amt=16)
            v_ld = st.A[par].n
            em.op("vector", lambda e, p=par, sc=delta_scale: e.scalar_tensor_tensor(
                out=xnew[p][:, :], in0=dtm[p][:, :], scalar=sc, in1=rsub[p][:, :],
                op0=ALU.mult, op1=ALU.add),
                wait=[(st.A[par], v_ld), (st.S[par], stv[par])], inc=st.sDV)
            v_new = st.sDV.n
            em.op("sync", lambda e, tt=t, p=par: e.dma_start(
                out=dst.ap()[128 * tt:128 * tt + 128, :], in_=xnew[p][:, :]),
                wait=[(st.sDV, v_new)], inc=st.S[par], amt=16)
            if write_ln2:
                v_ln = _ln_chain(st, em, xnew[par], l2t[par],
                                 (sbn, sagg, sstd, srstd, epss),
                                 war_dst=[(st.S[par], stv[par])])
                em.op("sync", lambda e, tt=t, p=par: e.dma_start(
                    out=st.cc1i.ap()[128 * tt:128 * tt + 128, :], in_=l2t[p][:, :]),
                    wait=[(st.sDV, v_ln)], inc=st.S[par], amt=16)
                used[par] = v_ln
            else:
                used[par] = v_new
            stv[par] = st.S[par].n
        em.run()


def _cc1(st):
    nc = st.nc
    em = Em(nc)
    em.op("gpsimd", lambda e: e.collective_compute(
        "AllGather", ALU.bypass, replica_groups=PAIRS,
        ins=[st.cc1i.ap()], outs=[st.cc1o.ap()]),
        wait=[(st.S[0], st.S[0].n), (st.S[1], st.S[1].n)], inc=st.sCC, amt=1)
    v = st.sCC.n
    em.op("gpsimd", lambda e: None, wait=[(st.sCC, v)])
    em.run()


def _stage_xlnf(st):
    nc = st.nc
    with contextlib.ExitStack() as es:
        buf = [es.enter_context(nc.sbuf_tensor(_un("sx_b"), [128, 2 * TOK], BF16))
               for i in range(2)]
        em = Em(nc)
        stv = [0, 0]
        for kc in range(NKC):
            par = kc % 2
            for hb in range(2):
                em.op("sync", lambda e, k=kc, h=hb, p=par: e.dma_start(
                    out=buf[p][:, TOK * h:TOK * h + TOK],
                    in_=st.cc1o.ap()[TOK * h:TOK * h + TOK, 128 * k:128 * k + 128],
                    transpose=True),
                    wait=[(st.S[par], stv[par])],
                    inc=st.A[par], amt=16)
            v_ld = st.A[par].n
            em.op("sync", lambda e, k=kc, p=par: e.dma_start(
                out=st.xlnf.ap()[128 * k:128 * k + 128, :], in_=buf[p][:, :]),
                wait=[(st.A[par], v_ld)], inc=st.S[par], amt=16)
            stv[par] = st.S[par].n
        em.run()


def _qkv(st):
    nc = st.nc
    NF = 12
    with contextlib.ExitStack() as es:
        def sb(name, shape, dt=BF16):
            return es.enter_context(nc.sbuf_tensor(_un(name), shape, dt))
        wq = [sb(f"q_wq{i}", [128, NKC * 128]) for i in range(NF)]
        rhs = [sb(f"q_rhs{i}", [128, NKC * 512]) for i in range(2)]
        stg = [sb(f"q_stg{i}", [128, 512]) for i in range(2)]
        vstg = [sb(f"q_vstg{i}", [128, 512]) for i in range(2)]
        pq = [es.enter_context(nc.psum_tensor(_un(f"q_pq{i}"), [128, 512], F32)) for i in range(2)]
        ptv = [es.enter_context(nc.psum_tensor(_un(f"q_ptv{i}"), [128, 1024], BF16)) for i in range(2)]
        em = Em(nc)
        em.op("vector", lambda e: e.tensor_scalar(
            out=st.bqsc[:, :], in0=st.bqs[:, 0:16], scalar1=SCL, scalar2=None,
            op0=ALU.mult),
            wait=[(st.sSU, st.v_setup)], inc=st.sDV)
        v_bq = st.sDV.n
        grp_use = 0
        rhs_use = [0, 0]
        evac = [0, 0]
        stv = [0, 0]
        vtev = [0, 0]
        for fg in range(4):
            for i in range(NF):
                fcg = fg * NF + i
                em.op("sync", lambda e, f=fcg, ii=i: e.dma_start(
                    out=wq[ii][:, :], in_=st.wqb.ap()[128 * f:128 * f + 128, :]),
                    wait=[(st.cWQ, st.vWQ), (st.sPE, grp_use)],
                    inc=st.A[2], amt=16)
            wldv = st.A[2].n
            for g in range(8):
                par = g % 2
                em.op("sync", lambda e, gg=g, p=par: e.dma_start(
                    out=rhs[p][:, :],
                    in_=bass.AP(st.xlnf, 512 * gg,
                                [[2 * TOK, 128], [128 * 2 * TOK, NKC], [1, 512]])),
                    wait=[(st.sPE, rhs_use[par])], inc=st.A[par], amt=16)
                rldv = st.A[par].n
                for i in range(NF):
                    fcg = fg * NF + i
                    pp = (g * NF + i) % 2
                    for kc in range(NKC):
                        em.op("tensor", lambda e, ii=i, k=kc, p=par, b=pp: e.matmul(
                            pq[b][:, :], wq[ii][:, 128 * k:128 * k + 128],
                            rhs[p][:, 512 * k:512 * k + 512],
                            start=(k == 0), stop=(k == NKC - 1)),
                            wait=[(st.A[2], wldv), (st.A[par], rldv),
                                  (st.sAC, evac[pp])],
                            inc=(st.sPE if kc == NKC - 1 else None))
                    v_mm = st.sPE.n
                    if i == NF - 1:
                        rhs_use[par] = st.sPE.n
                        if g == 7:
                            grp_use = st.sPE.n
                    is_q = fcg < 16
                    bias_ap = (st.bqsc[:, fcg:fcg + 1] if is_q
                               else st.bqs[:, fcg:fcg + 1])
                    scale = SCL if is_q else 1.0
                    em.op("scalar", lambda e, b=pp, ba=bias_ap, sc=scale: e.activation(
                        out=stg[b][:, :], in_=pq[b][:, :], func=AF.Identity,
                        bias=ba, scale=sc),
                        wait=[(st.sPE, v_mm), (st.S[pp], stv[pp]), (st.sDV, v_bq)],
                        inc=st.sAC)
                    v_ev = st.sAC.n
                    evac[pp] = v_ev
                    if fcg < 32:
                        if fcg < 16:
                            tgt, row = st.qs, fcg
                        else:
                            tgt, row = st.ks, fcg - 16
                        em.op("sync", lambda e, tg=tgt, r=row, gg=g, b=pp: e.dma_start(
                            out=tg.ap()[128 * r:128 * r + 128, 512 * gg:512 * gg + 512],
                            in_=stg[b][:, :]),
                            wait=[(st.sAC, v_ev)], inc=st.S[pp], amt=16)
                        stv[pp] = st.S[pp].n
                    else:
                        hl = fcg - 32
                        for s4 in range(4):
                            em.op("tensor", lambda e, b=pp, s=s4: e.transpose(
                                ptv[b][:, 128 * s:128 * s + 128],
                                stg[b][:, 128 * s:128 * s + 128], st.idents[:, :]),
                                wait=[(st.sAC, max(v_ev, vtev[pp])),
                                      (st.sSU, st.v_setup)],
                                inc=(st.sPE if s4 == 3 else None))
                        v_tp = st.sPE.n
                        em.op("scalar", lambda e, b=pp: e.activation(
                            out=vstg[b][:, :], in_=ptv[b][:, 0:512], func=AF.Identity),
                            wait=[(st.sPE, v_tp), (st.S[pp], stv[pp])], inc=st.sAC)
                        vtev[pp] = st.sAC.n
                        em.op("sync", lambda e, h=hl, gg=g, b=pp: e.dma_start(
                            out=bass.AP(st.vst, 4096 * h + 512 * gg,
                                        [[NJ * 4096, 128], [128, 4], [1, 128]]),
                            in_=vstg[b][:, :]),
                            wait=[(st.sAC, vtev[pp])], inc=st.S[pp], amt=16)
                        stv[pp] = st.S[pp].n
        em.run()


def _attention(st):
    nc = st.nc
    with contextlib.ExitStack() as es:
        def sb(name, shape, dt=BF16):
            return es.enter_context(nc.sbuf_tensor(_un(name), shape, dt))
        Ks = [sb(f"t_K{i}", [128, S]) for i in range(2)]
        K2 = [sb(f"t_K2{i}", [128, S]) for i in range(2)]
        Vh = [sb(f"t_V{i}", [128, NKC * 128]) for i in range(2)]
        Al = [sb(f"t_al{i}", [128, 512], F32) for i in range(2)]
        Ac = [sb(f"t_ac{i}", [128, 256], F32) for i in range(2)]
        Nk = [sb(f"t_nk{i}", [1, S], F32) for i in range(2)]
        Nkm = [sb(f"t_nkm{i}", [1, 1], F32) for i in range(2)]
        Nkmb = [sb(f"t_nkmb{i}", [1, 1]) for i in range(2)]
        Mk = [sb(f"t_mk{i}", [128, 1], F32) for i in range(2)]
        Qs = [sb(f"t_q{i}", [128, 512]) for i in range(2)]
        Ssb = [sb(f"t_S{i}", [128, S], F32) for i in range(2)]
        Psb = [sb(f"t_P{i}", [128, S]) for i in range(4)]
        Pt = [sb(f"t_pt{i}", [128, S]) for i in range(2)]
        Qd = sb("t_qd", [128, 128])
        Qn2 = [sb(f"t_qn2{i}", [128, 1], F32) for i in range(4)]
        Qn = [sb(f"t_qn{i}", [128, 1], F32) for i in range(4)]
        Ngm = [sb(f"t_ngm{i}", [128, 1], F32) for i in range(4)]
        Ls = [sb(f"t_l{i}", [128, 1], F32) for i in range(4)]
        Rl = [sb(f"t_rl{i}", [128, 1], F32) for i in range(4)]
        RlA = [sb(f"t_rlA{i}", [128, 1], F32) for i in range(4)]
        RlB = [sb(f"t_rlB{i}", [128, 1], F32) for i in range(4)]
        CtA = [sb(f"t_ctA{i}", [128, 128]) for i in range(2)]
        CtB = [sb(f"t_ctB{i}", [128, 128]) for i in range(2)]
        psS = [es.enter_context(nc.psum_tensor(f"t_psS{i}", [128, 512], F32)) for i in range(2)]
        ptp = [es.enter_context(nc.psum_tensor(_un(f"t_ptp{i}"), [128, 1024], BF16)) for i in range(2)]
        pctx = [es.enter_context(nc.psum_tensor(_un(f"t_pctx{i}"), [128, 512], F32)) for i in range(2)]
        pnk = es.enter_context(nc.psum_tensor(_un("t_pnk"), [128, 512], F32))
        pbc = es.enter_context(nc.psum_tensor(_un("t_pbc"), [128, 512], F32))

        em = Em(nc)
        job_pe = [0, 0]
        job_ac = [0, 0]
        job_dv = [0, 0]
        sS_war = [0, 0]
        ptp_war = [0, 0]
        pctx_war = [0, 0]
        Ssb_war = [0, 0]
        Psb_war = [0, 0, 0, 0]
        Pt_war = [0, 0]
        ct_war = [0, 0]
        nS = 0
        nkg = 0
        for hl in range(NJ):
            jp = hl % 2
            em.op("sync", lambda e, h=hl, p=jp: e.dma_start(
                out=Ks[p][:, :], in_=st.ks.ap()[128 * h:128 * h + 128, :]),
                wait=[(st.sPE, job_pe[jp]), (st.sAC, job_ac[jp]), (st.sDV, job_dv[jp]),
                      (st.S[0], st.S[0].n if hl == 0 else 0),
                      (st.S[1], st.S[1].n if hl == 0 else 0)],
                inc=st.A[jp], amt=16)
            em.op("sync", lambda e, h=hl, p=jp: e.dma_start(
                out=Al[p][:, :], in_=st.al2t.ap()[128 * h:128 * h + 128, :]),
                inc=st.A[jp], amt=16)
            em.op("sync", lambda e, h=hl, p=jp: e.dma_start(
                out=Ac[p][:, :], in_=st.actt.ap()[128 * h:128 * h + 128, :]),
                inc=st.A[jp], amt=16)
            em.op("sync", lambda e, h=hl, p=jp: e.dma_start(
                out=Vh[p][:, :],
                in_=bass.AP(st.vst, 4096 * h,
                            [[NJ * 4096, 128], [1, 4096]])),
                inc=st.A[jp], amt=16)
            v_kld = st.A[jp].n
            em.op("scalar", lambda e, p=jp: e.activation(
                out=K2[p][:, :], in_=Ks[p][:, :], func=AF.Square),
                wait=[(st.A[jp], v_kld)], inc=st.sAC)
            v_k2 = st.sAC.n
            for kb in range(8):
                em.op("tensor", lambda e, p=jp, k=kb: e.matmul(
                    pnk[0:1, :], st.ones128s[:, :], K2[p][:, 512 * k:512 * k + 512],
                    start=True, stop=True),
                    wait=[(st.sAC, v_k2 if kb == 0 else st.sAC.n),
                          (st.sDV, st.v_memset)],
                    inc=st.sPE)
                v_nk = st.sPE.n
                em.op("scalar", lambda e, p=jp, k=kb: e.activation(
                    out=Nk[p][0:1, 512 * k:512 * k + 512], in_=pnk[0:1, :],
                    func=AF.Identity),
                    wait=[(st.sPE, v_nk)], inc=st.sAC)
            v_nkev = st.sAC.n
            em.op("vector", lambda e, p=jp: e.reduce_max(
                out=Nkm[p][0:1, :], in_=Nk[p][0:1, :], axis=AX.X),
                wait=[(st.sAC, v_nkev)], inc=st.sDV)
            v_max = st.sDV.n
            em.op("scalar", lambda e, p=jp: e.activation(
                out=Nkmb[p][0:1, :], in_=Nkm[p][0:1, :], func=AF.Sqrt),
                wait=[(st.sDV, v_max)], inc=st.sAC)
            v_sq = st.sAC.n
            em.op("tensor", lambda e, p=jp: e.matmul(
                pbc[:, 0:1], st.ones1s[0:1, :], Nkmb[p][0:1, 0:1],
                start=True, stop=True),
                wait=[(st.sAC, v_sq)], inc=st.sPE)
            v_bc = st.sPE.n
            em.op("scalar", lambda e, p=jp: e.activation(
                out=Mk[p][:, :], in_=pbc[:, 0:1], func=AF.Identity),
                wait=[(st.sPE, v_bc)], inc=st.sAC)
            v_mk = st.sAC.n

            q_use_pe = [job_pe[0], job_pe[1]]
            q_use_ac = [job_ac[0], job_ac[1]]
            negm_prev = [0, 0, 0, 0]
            expd_prev = [0, 0, 0, 0]
            rc_prev = [0, 0, 0, 0]
            ctxe_prev = [0, 0, 0, 0]
            for j in range(NSUP):
                E = j + 1
                qp = j % 2
                em.op("sync", lambda e, h=hl, jj=j, p=qp: e.dma_start(
                    out=Qs[p][:, :], in_=st.qs.ap()[128 * h:128 * h + 128, 512 * jj:512 * jj + 512]),
                    wait=[(st.sPE, q_use_pe[qp]), (st.sAC, q_use_ac[qp])],
                    inc=st.A[2 + qp], amt=16)
                v_qld = st.A[2 + qp].n
                exp_v = [0, 0, 0, 0]
                rl_v = [0, 0, 0, 0]
                sttv = [0, 0, 0, 0]
                for s_ in range(4):
                    sp2 = s_ % 2
                    em.op("scalar", lambda e, p=qp, ss=s_: e.activation(
                        out=Qd[:, :], in_=Qs[p][:, 128 * ss:128 * ss + 128],
                        func=AF.Square, accum_out=Qn2[ss][:, :]),
                        wait=[(st.A[2 + qp], v_qld), (st.sDV, negm_prev[s_])],
                        inc=st.sAC)
                    v_q2 = st.sAC.n
                    em.op("scalar", lambda e, ss=s_: e.activation(
                        out=Qn[ss][:, :], in_=Qn2[ss][:, :], func=AF.Sqrt),
                        wait=[(st.sAC, v_q2)], inc=st.sAC)
                    v_qn = st.sAC.n
                    em.op("vector", lambda e, ss=s_, p=jp: e.tensor_scalar(
                        out=Ngm[ss][:, :], in0=Qn[ss][:, :], scalar1=Mk[p][:, :],
                        scalar2=-1.0, op0=ALU.mult, op1=ALU.mult),
                        wait=[(st.sAC, max(v_qn, v_mk, expd_prev[s_]))], inc=st.sDV)
                    negm_prev[s_] = st.sDV.n
                    for kb in range(E):
                        bp = nS % 2 if E == 1 else (nS + kb) % 2
                        em.op("tensor", lambda e, p=qp, ss=s_, k=kb, b=bp, h=jp: e.matmul(
                            psS[b][:, :], Qs[p][:, 128 * ss:128 * ss + 128],
                            Ks[h][:, 512 * k:512 * k + 512], start=True, stop=True),
                            wait=[(st.sDV, sS_war[bp]), (st.A[jp], v_kld),
                                  (st.A[2 + qp], v_qld)], inc=st.sPE)
                        v_s = st.sPE.n
                        idx = j * 32 + s_ * 8 + kb
                        em.op("vector", lambda e, b=bp, k=kb, p=jp, ii=idx, o=sp2: e.scalar_tensor_tensor(
                            out=Ssb[o][:, 512 * k:512 * k + 512], in0=psS[b][:, :],
                            scalar=Ac[p][:, ii:ii + 1], in1=Al[p][:, :],
                            op0=ALU.add, op1=ALU.add),
                            wait=[(st.sPE, v_s), (st.sAC, Ssb_war[sp2]),
                                  (st.A[jp], v_kld), (st.sSU, st.v_setup)],
                            inc=st.sDV)
                        sS_war[bp] = st.sDV.n
                        if kb == j:
                            em.op("vector", lambda e, k=kb, ss=s_, o=sp2: e.tensor_tensor(
                                out=Ssb[o][:, 512 * k:512 * k + 512],
                                in0=Ssb[o][:, 512 * k:512 * k + 512],
                                in1=st.maskws[:, 512 - 128 * ss:1024 - 128 * ss],
                                op=ALU.add), inc=st.sDV)
                    sttv[s_] = st.sDV.n
                    em.op("scalar", lambda e, ss=s_, o=sp2, EE=E: e.activation(
                        out=Psb[ss][:, 0:512 * EE], in_=Ssb[o][:, 0:512 * EE],
                        func=AF.Exp, bias=Ngm[ss][:, :], accum_out=Ls[ss][:, :]),
                        wait=[(st.sDV, max(sttv[s_], negm_prev[s_], rc_prev[s_])),
                              (st.sPE, Psb_war[s_])],
                        inc=st.sAC)
                    exp_v[s_] = st.sAC.n
                    Ssb_war[sp2] = exp_v[s_]
                    nS += 1
                for s_ in range(4):
                    em.op("vector", lambda e, ss=s_: e.reciprocal(Rl[ss][:, :], Ls[ss][:, :]),
                          wait=[(st.sAC, exp_v[s_])], inc=st.sDV)
                    v_rc = st.sDV.n
                    rc_prev[s_] = v_rc
                    em.op("vector", lambda e, ss=s_: e.tensor_scalar(
                        out=RlA[ss][:, :], in0=Rl[ss][:, :], scalar1=st.mas[:, :],
                        scalar2=None, op0=ALU.mult),
                        wait=[(st.sDV, v_rc), (st.sSU, st.v_setup),
                              (st.sAC, ctxe_prev[s_])])
                    em.op("vector", lambda e, ss=s_: e.tensor_scalar(
                        out=RlB[ss][:, :], in0=Rl[ss][:, :], scalar1=st.mbs[:, :],
                        scalar2=None, op0=ALU.mult), inc=st.sDV)
                    rl_v[s_] = st.sDV.n
                for s_ in range(4):
                    sp2 = s_ % 2
                    scp = (4 * j + s_) % 2
                    pv_first = True
                    for kg in range(E):
                        tp = nkg % 2
                        nkg += 1
                        for cc in range(4):
                            kc = 4 * kg + cc
                            em.op("tensor", lambda e, ss=s_, k=kc, c=cc, b=tp: e.transpose(
                                ptp[b][:, 128 * c:128 * c + 128],
                                Psb[ss][:, 128 * k:128 * k + 128], st.idents[:, :]),
                                wait=[(st.sAC, max(exp_v[s_], ptp_war[tp]))],
                                inc=(st.sPE if cc == 3 else None))
                        v_tp = st.sPE.n
                        if kg == E - 1:
                            Psb_war[s_] = v_tp
                        em.op("scalar", lambda e, g=kg, b=tp, o=sp2: e.activation(
                            out=Pt[o][:, 512 * g:512 * g + 512], in_=ptp[b][:, 0:512],
                            func=AF.Identity),
                            wait=[(st.sPE, max(v_tp, Pt_war[sp2]))],
                            inc=st.sAC)
                        ptp_war[tp] = st.sAC.n
                        v_ptev = st.sAC.n
                        for cc in range(4):
                            kc = 4 * kg + cc
                            em.op("tensor", lambda e, k=kc, p=jp, fi=pv_first, o=sp2,
                                  la=(kc == 4 * E - 1): e.matmul(
                                pctx[o][:, 0:128], Pt[o][:, 128 * k:128 * k + 128],
                                Vh[p][:, 128 * k:128 * k + 128],
                                start=fi, stop=la),
                                wait=[(st.sAC, max(v_ptev, pctx_war[sp2]))],
                                inc=(st.sPE if kc == 4 * E - 1 else None))
                            pv_first = False
                    v_pv = st.sPE.n
                    Pt_war[sp2] = v_pv
                    em.op("scalar", lambda e, ss=s_, o=sp2: e.activation(
                        out=CtA[o][:, :], in_=pctx[o][:, 0:128], func=AF.Identity,
                        scale=RlA[ss][:, :]),
                        wait=[(st.sPE, v_pv), (st.sDV, rl_v[s_]),
                              (st.S[scp], ct_war[scp])],
                        inc=st.sAC)
                    em.op("scalar", lambda e, ss=s_, o=sp2: e.activation(
                        out=CtB[o][:, :], in_=pctx[o][:, 0:128], func=AF.Identity,
                        scale=RlB[ss][:, :]), inc=st.sAC)
                    v_ct = st.sAC.n
                    pctx_war[sp2] = v_ct
                    ctxe_prev[s_] = v_ct
                    rowb = 512 * j + 128 * s_
                    em.op("sync", lambda e, h=hl, r=rowb, o=sp2: e.dma_start(
                        out=st.cc2i.ap()[r:r + 128, 128 * h:128 * h + 128],
                        in_=CtA[o][:, :]),
                        wait=[(st.sAC, v_ct)], inc=st.S[scp], amt=16)
                    em.op("sync", lambda e, h=hl, r=rowb, o=sp2: e.dma_start(
                        out=st.cc2i.ap()[r:r + 128, 2048 + 128 * h:2048 + 128 * h + 128],
                        in_=CtB[o][:, :]),
                        inc=st.S[scp], amt=16)
                    ct_war[scp] = st.S[scp].n
                q_use_pe[qp] = st.sPE.n
                q_use_ac[qp] = st.sAC.n
            job_pe[jp] = st.sPE.n
            job_ac[jp] = st.sAC.n
            job_dv[jp] = st.sDV.n
        em.run()


def _cc2(st):
    nc = st.nc
    em = Em(nc)
    em.op("gpsimd", lambda e: e.collective_compute(
        "ReduceScatter", ALU.add, replica_groups=PAIRS,
        ins=[st.cc2i.ap()], outs=[st.cc2o.ap()]),
        wait=[(st.S[0], st.S[0].n), (st.S[1], st.S[1].n)], inc=st.sCC, amt=1)
    v = st.sCC.n
    em.op("gpsimd", lambda e: None, wait=[(st.sCC, v)])
    em.run()


def _outproj(st):
    nc = st.nc
    with contextlib.ExitStack() as es:
        def sb(name, shape, dt=BF16):
            return es.enter_context(nc.sbuf_tensor(_un(name), shape, dt))
        ctxf = sb("o_ctxf", [128, NKC * 1024])
        wo = [sb(f"o_wo{i}", [128, NKC * 128]) for i in range(2)]
        dds = [sb(f"o_dd{i}", [128, 512]) for i in range(2)]
        po = [es.enter_context(nc.psum_tensor(_un(f"o_po{i}"), [128, 512], F32)) for i in range(2)]
        em = Em(nc)
        wuse = [0, 0]
        evac = [0, 0]
        stv = [0, 0]
        ctx_use = 0
        for th in range(2):
            for tl in range(2):
                for kc in range(NKC):
                    em.op("sync", lambda e, t=th, l=tl, k=kc: e.dma_start(
                        out=ctxf[:, 1024 * k + 512 * l:1024 * k + 512 * l + 512],
                        in_=st.cc2o.ap()[512 * (2 * t + l):512 * (2 * t + l) + 512,
                                         128 * k:128 * k + 128],
                        transpose=True),
                        wait=[(st.sPE, ctx_use)], inc=st.A[2], amt=16)
            v_cld = st.A[2].n
            for oc in range(32):
                par = oc % 2
                em.op("sync", lambda e, o=oc, p=par: e.dma_start(
                    out=wo[p][:, :], in_=st.wob.ap()[128 * o:128 * o + 128, :]),
                    wait=[(st.cWO, st.vWO), (st.sPE, wuse[par])],
                    inc=st.A[par], amt=16)
                v_wld = st.A[par].n
                for tl in range(2):
                    pp = (oc * 2 + tl) % 2
                    for kc in range(NKC):
                        em.op("tensor", lambda e, p=par, k=kc, l=tl, b=pp: e.matmul(
                            po[b][:, :], wo[p][:, 128 * k:128 * k + 128],
                            ctxf[:, 1024 * k + 512 * l:1024 * k + 512 * l + 512],
                            start=(k == 0), stop=(k == NKC - 1)),
                            wait=[(st.A[2], v_cld), (st.A[par], v_wld),
                                  (st.sAC, evac[pp])],
                            inc=(st.sPE if kc == NKC - 1 else None))
                    v_mm = st.sPE.n
                    if tl == 1:
                        wuse[par] = st.sPE.n
                        if oc == 31:
                            ctx_use = st.sPE.n
                    em.op("scalar", lambda e, o=oc, b=pp: e.activation(
                        out=dds[b][:, :], in_=po[b][:, :], func=AF.Identity,
                        bias=st.bos[:, o:o + 1]),
                        wait=[(st.sPE, v_mm), (st.S[pp], stv[pp]),
                              (st.sSU, st.v_setup)], inc=st.sAC)
                    evac[pp] = st.sAC.n
                    em.op("sync", lambda e, o=oc, t=th, l=tl, b=pp: e.dma_start(
                        out=st.ddelta.ap()[128 * o:128 * o + 128,
                                           512 * (2 * t + l):512 * (2 * t + l) + 512],
                        in_=dds[b][:, :]),
                        wait=[(st.sAC, evac[pp])], inc=st.S[pp], amt=16)
                    stv[pp] = st.S[pp].n
        em.run()


def _final(st):
    nc = st.nc
    em = Em(nc)
    w = [(st.S[0], st.S[0].n), (st.S[1], st.S[1].n)]
    em.op("sync", lambda e: None, wait=w)
    em.op("gpsimd", lambda e: None, wait=w)
    for tname, tout in st.taps.items():
        src = getattr(st, tname)
        em.op("gpsimd", lambda e, s=src, d=tout: e.dma_start(out=d.ap(), in_=s.ap()),
              inc=st.sCAo, amt=16)
    if st.taps:
        em.op("gpsimd", lambda e: None, wait=[(st.sCAo, st.sCAo.n)])
    em.run()


def _prep_inputs(c, I):
    half = c % 2
    b = c // 2
    f32 = np.float32
    bf16 = ml_dtypes.bfloat16

    x = np.ascontiguousarray(I["L0_attn_hidden_states"][b, half * TOK:(half + 1) * TOK, :],
                             dtype=f32)

    heads = [16 * half + i for i in range(16)]
    wqkv = I["L0_attn_w_qkv"]
    cols = []
    for base in (0, D, 2 * D):
        for h in heads:
            cols.extend(range(base + 128 * h, base + 128 * h + 128))
    cols = np.asarray(cols)
    wsl = np.asarray(wqkv)[:, cols]
    wqkvt = np.ascontiguousarray(
        wsl.reshape(NKC, 128, 48, 128).transpose(2, 1, 0, 3).reshape(48 * 128, NKC * 128),
        dtype=f32)
    bq = np.asarray(I["L0_attn_b_qkv"])[cols]
    bqkvt = np.ascontiguousarray(bq.reshape(48, 128).T, dtype=f32)

    wout = np.asarray(I["L0_attn_w_out"])
    woutt = np.ascontiguousarray(
        wout.reshape(NKC, 128, 32, 128).transpose(2, 1, 0, 3).reshape(32 * 128, NKC * 128),
        dtype=f32)
    boutt = np.ascontiguousarray(np.asarray(I["L0_attn_b_out"]).reshape(32, 128).T, dtype=f32)

    w1q = np.asarray(I["L0_ffn_w1q"])
    w3q = np.asarray(I["L0_ffn_w3q"])
    w2q = np.asarray(I["L0_ffn_w2q"])
    w1t = np.ascontiguousarray(
        w1q.reshape(NMC, 128, NQC, 128).transpose(0, 3, 2, 1).reshape(NMC * 128, DQ),
        dtype=f32)
    w3t = np.ascontiguousarray(
        w3q.reshape(NMC, 128, NQC, 128).transpose(0, 3, 2, 1).reshape(NMC * 128, DQ),
        dtype=f32)
    w2p = np.concatenate([w2q, np.zeros((D, NHC * 128 - HQ), dtype=w2q.dtype)], axis=1)
    w2t = np.ascontiguousarray(
        w2p.reshape(32, 128, NHC, 128).transpose(0, 3, 2, 1).reshape(32 * 128, NHC * 128),
        dtype=f32)

    b1t = np.ascontiguousarray(np.asarray(I["L0_ffn_b1"]).T, dtype=f32)
    b3t = np.ascontiguousarray(np.asarray(I["L0_ffn_b3"]).T, dtype=f32)
    b2t = np.ascontiguousarray(np.asarray(I["L0_ffn_b2"]).T, dtype=f32)
    a1t = np.ascontiguousarray(
        np.asarray(I["L0_ffn_a1"]).reshape(2, NKC, 128).transpose(2, 1, 0).reshape(128, 64),
        dtype=f32)
    a3t = np.ascontiguousarray(
        np.asarray(I["L0_ffn_a3"]).reshape(2, NKC, 128).transpose(2, 1, 0).reshape(128, 64),
        dtype=f32)
    a2t = np.ascontiguousarray(
        np.asarray(I["L0_ffn_a2"]).reshape(2, NMC, 128).transpose(2, 1, 0).reshape(128, 172),
        dtype=f32)

    identt = np.eye(128, dtype=bf16)
    p64t = np.roll(np.eye(128, dtype=np.float32), 64, axis=0).astype(bf16)

    fp = np.arange(1024)
    pp = np.arange(128)
    maskwt = np.where(fp[None, :] <= pp[:, None] + 512, 0.0, NEG).astype(f32)

    slopes = _alibi_slopes(32)
    al2t = np.zeros((NJ * 128, 512), dtype=f32)
    actt = np.zeros((NJ * 128, 256), dtype=f32)
    f5 = np.arange(512, dtype=f32)
    for hl in range(NJ):
        sl = slopes[16 * half + hl]
        al2t[128 * hl:128 * hl + 128, :] = sl * (f5[None, :] - pp[:, None].astype(f32))
        tab = np.zeros(256, dtype=f32)
        for j in range(8):
            for s_ in range(4):
                for kb in range(j + 1):
                    tab[j * 32 + s_ * 8 + kb] = sl * (512.0 * (kb - j) - 128.0 * s_)
        actt[128 * hl:128 * hl + 128, :] = tab[None, :]

    mat = np.full((128, 1), 1.0 if half == 0 else 0.0, dtype=f32)
    mbt = np.full((128, 1), 1.0 if half == 1 else 0.0, dtype=f32)

    return {
        "x": x, "w1t": w1t, "w3t": w3t, "w2t": w2t, "wqkvt": wqkvt, "woutt": woutt,
        "b1t": b1t, "b3t": b3t, "b2t": b2t, "a1t": a1t, "a3t": a3t, "a2t": a2t,
        "bqkvt": bqkvt, "boutt": boutt, "identt": identt, "p64t": p64t,
        "maskwt": maskwt, "al2t": al2t, "actt": actt, "mat": mat, "mbt": mbt,
    }


def kernel(**inputs):
    nc = build_nc()
    in_maps = [_prep_inputs(c, inputs) for c in range(NCORE)]
    res = run_bass_kernel_spmd(nc, in_maps, core_ids=list(range(NCORE)))
    out = np.empty((4, S, D), dtype=np.float32)
    for c in range(NCORE):
        out[c // 2, (c % 2) * TOK:(c % 2 + 1) * TOK, :] = res.results[c]["out"]
    return out


# version 06
# speedup vs baseline: 1.2362x; 1.1496x over previous
import math
import contextlib
import numpy as np
import ml_dtypes
import concourse.bass as bass
import concourse.mybir as mybir
from concourse.bass_utils import run_bass_kernel_spmd

F32 = mybir.dt.float32
BF16 = mybir.dt.bfloat16
AF = mybir.ActivationFunctionType
ALU = mybir.AluOpType
AX = mybir.AxisListType

NCORE = 8
TOK = 2048
D = 4096
S = 4096
NKC = 32
HFF = 11008
NMC = 86
DQ = 1024
NQC = 8
HQ = 2752
NHC = 22
NT = 4
NST = 16
NJ = 16
NSUP = 8
SCL = 1.0 / math.sqrt(128.0)
PAIRS = [[0, 1], [2, 3], [4, 5], [6, 7]]
NEG = -1.0e9


def _alibi_slopes(n):
    p = 2 ** math.floor(math.log2(n))
    base = 2.0 ** (-(2.0 ** -(math.log2(p) - 3)))
    sl = [base ** (i + 1) for i in range(p)]
    if p != n:
        eb = 2.0 ** (-(2.0 ** -(math.log2(2 * p) - 3)))
        sl += [eb ** (2 * i + 1) for i in range(min(p, n - p))]
    return np.asarray(sl, dtype=np.float32)


def _hs_plans():
    plans = []
    first = {}
    for mc in range(NMC):
        h0 = 128 * mc
        bounds = {0, 64, 128}
        r = (-h0) % HQ
        if 0 < r < 128:
            bounds.add(r)
        bounds = sorted(bounds)
        segs = []
        for lo, hi in zip(bounds[:-1], bounds[1:]):
            hp = (h0 + lo) % HQ
            c, dlo = hp // 128, hp % 128
            rot = (dlo != lo)
            if rot:
                assert dlo == (lo + 64) % 128, (mc, lo, dlo)
            key = (c, dlo, hi - lo)
            isfirst = key not in first and (c, dlo) not in {(k[0], k[1]) for k in first}
            kk = (c, dlo)
            isfirst = kk not in first
            if isfirst:
                first[kk] = mc
                assert not rot, f"first write must be aligned mc={mc} seg={lo}"
            segs.append((lo, hi, c, dlo, rot, isfirst))
        plans.append(segs)
    merged = []
    for segs in plans:
        out = []
        for sg in segs:
            if out:
                lo, hi, c, dlo, rot, fi = out[-1]
                lo2, hi2, c2, dlo2, rot2, fi2 = sg
                if (not rot and not rot2 and c == c2 and hi == lo2
                        and dlo + (hi - lo) == dlo2 and fi == fi2):
                    out[-1] = (lo, hi2, c, dlo, rot, fi)
                    continue
            out.append(sg)
        merged.append(out)
    return merged


HS_PLANS = _hs_plans()

_UID = [0]


def _un(name):
    _UID[0] += 1
    return f"{name}_{_UID[0]}"


class Sem:
    def __init__(self, es, nc, name):
        self.h = es.enter_context(nc.semaphore(name))
        self.n = 0
        self.name = name

    def bump(self, amt):
        self.n += amt
        return self.n


class Em:
    ENGS = ("sync", "tensor", "scalar", "vector", "gpsimd")

    def __init__(self, nc):
        self.nc = nc
        self.ops = {e: [] for e in self.ENGS}
        self.lastwait = {e: {} for e in self.ENGS}

    def op(self, eng, fn, wait=(), inc=None, amt=1):
        w = []
        lw = self.lastwait[eng]
        for sem, v in wait:
            if sem is None or v is None or v <= 0:
                continue
            if lw.get(id(sem), 0) >= v:
                continue
            lw[id(sem)] = v
            w.append((sem, v))
        ret = inc.bump(amt) if inc is not None else None
        self.ops[eng].append((w, fn, inc, amt))
        return ret

    def run(self):
        with self.nc.Block() as blk:
            for eng in self.ENGS:
                if not self.ops[eng]:
                    continue

                def body(e, _ops=self.ops[eng]):
                    for w, fn, inc, amt in _ops:
                        for sem, v in w:
                            e.wait_ge(sem.h, v)
                        ins = fn(e)
                        if inc is not None:
                            ins.then_inc(inc.h, amt)
                getattr(blk, eng)(body)


class St:
    pass


def build_nc(taps=(), snaps=(), expt=()):
    nc = bass.Bass("TRN2", target_bir_lowering=False, num_devices=8)
    st = St()
    st.nc = nc
    st.want_snaps = set(snaps)
    st.snap_outs = {}
    st.expt = set(expt)

    def dri(name, shape):
        return nc.dram_tensor(name, shape, F32, kind="ExternalInput")

    def drib(name, shape):
        return nc.dram_tensor(name, shape, BF16, kind="ExternalInput")

    st.xin = dri("x", [TOK, D])
    st.w1t = dri("w1t", [NMC * 128, DQ])
    st.w3t = dri("w3t", [NMC * 128, DQ])
    st.w2t = dri("w2t", [NKC * 128, NHC * 128])
    st.wqkvt = dri("wqkvt", [48 * 128, NKC * 128])
    st.woutt = dri("woutt", [32 * 128, NKC * 128])
    st.b1t = dri("b1t", [2, HFF])
    st.b3t = dri("b3t", [2, HFF])
    st.b2t = dri("b2t", [2, D])
    st.a1t = dri("a1t", [128, 64])
    st.a3t = dri("a3t", [128, 64])
    st.a2t = dri("a2t", [128, 172])
    st.bqkvt = dri("bqkvt", [128, 48])
    st.boutt = dri("boutt", [128, 32])
    st.identt = drib("identt", [128, 128])
    st.p64t = drib("p64t", [128, 128])
    st.maskwt = dri("maskwt", [128, 1024])
    st.al2t = dri("al2t", [NJ * 128, 512])
    st.actt = dri("actt", [NJ * 128, 256])
    st.mat = dri("mat", [128, 1])
    st.mbt = dri("mbt", [128, 1])

    st.out = nc.dram_tensor("out", [TOK, D], F32, kind="ExternalOutput")

    def drs(name, shape, dt=BF16):
        return nc.dram_tensor(name, shape, dt)

    st.w1b = drs("w1b", [NMC * 128, DQ])
    st.w3b = drs("w3b", [NMC * 128, DQ])
    st.b1b = drs("b1b", [2, HFF])
    st.b3b = drs("b3b", [2, HFF])
    st.w2b = drs("w2b", [NKC * 128, NHC * 128])
    st.wqb = drs("wqb", [48 * 128, NKC * 128])
    st.wob = drs("wob", [32 * 128, NKC * 128])
    st.lntm = drs("lntm", [TOK, D])
    st.ddelta = drs("ddelta", [D, TOK])
    st.x1res = drs("x1res", [TOK, D], F32)
    st.x2res = drs("x2res", [TOK, D], F32)
    st.cc1i = drs("cc1i", [TOK, D])
    st.cc1o = drs("cc1o", [2 * TOK, D])
    st.xlnf = drs("xlnf", [D, 2 * TOK])
    st.vst = drs("vst", [128, NJ * S // 128 * 128])
    st.qs = drs("qs", [NJ * 128, S])
    st.ks = drs("ks", [NJ * 128, S])
    st.vsf = drs("vsf", [NJ * 128, S])
    st.cc2i = drs("cc2i", [2 * TOK, 2 * D // 2])
    st.cc2o = drs("cc2o", [TOK, 2 * D // 2])

    st.taps = {}
    for tname in taps:
        src = getattr(st, tname)
        shp = list(src.shape)
        dt = src.dtype if hasattr(src, "dtype") else F32
        st.taps[tname] = nc.dram_tensor("tap_" + tname, shp, dt, kind="ExternalOutput")

    with contextlib.ExitStack() as es:
        st.es = es
        st.A = [Sem(es, nc, f"sA{i}") for i in range(4)]
        st.S = [Sem(es, nc, f"sS{i}") for i in range(2)]
        st.sSU = Sem(es, nc, "sSU")
        st.cW13 = Sem(es, nc, "cW13")
        st.cW2 = Sem(es, nc, "cW2")
        st.cWQ = Sem(es, nc, "cWQ")
        st.cWO = Sem(es, nc, "cWO")
        st.cAS = Sem(es, nc, "cAS")
        st.sCAo = Sem(es, nc, "sCAo")
        st.sLB = Sem(es, nc, "sLB")
        st.sCC = Sem(es, nc, "sCC")
        st.sPE = Sem(es, nc, "sPE")
        st.sAC = Sem(es, nc, "sAC")
        st.sDV = Sem(es, nc, "sDV")

        def sb(name, shape, dt=BF16):
            return es.enter_context(nc.sbuf_tensor(_un(name), shape, dt))

        st.b2s = sb("b2s", [2, D])
        st.a1s = sb("a1s", [128, 64])
        st.a3s = sb("a3s", [128, 64])
        st.a2s = sb("a2s", [128, 172])
        st.bqs = sb("bqs", [128, 48], F32)
        st.bqsc = sb("bqsc", [128, 16], F32)
        st.bos = sb("bos", [128, 32], F32)
        st.maskws = sb("maskws", [128, 1024], F32)
        st.idents = sb("idents", [128, 128])
        st.p64s = sb("p64s", [128, 128])
        st.ones1s = sb("ones1s", [1, 128])
        st.ones128s = sb("ones128s", [128, 1])
        st.mas = sb("mas", [128, 1], F32)
        st.mbs = sb("mbs", [128, 1], F32)
        st.u1s = sb("u1s", [2, TOK])
        st.u3s = sb("u3s", [2, TOK])
        st.u2s = sb("u2s", [2, TOK])

        _ffn_pass(st, src=st.xin, dst=st.x1res, write_ln2=True, tag="f1",
                  pre_emit=lambda em: _setup_and_casts(st, em))
        _cc1(st)
        _stage_xlnf(st)
        _qkv(st)
        _attention(st)
        _cc2(st)
        _outproj(st)
        _epilogue_pass(st, delta_scale=1.0, src=st.x1res, dst=st.x2res, write_ln2=False)
        _ffn_pass(st, src=st.x2res, dst=st.out, write_ln2=False, tag="f3")
        _final(st)
    return nc


def _snap(st, tag, src, shape=None, dt=None):
    if tag not in st.want_snaps:
        return
    nc = st.nc
    if shape is None:
        shape = list(src.shape)
    if dt is None:
        dt = src.dtype
    out = nc.dram_tensor("snap_" + tag, shape, dt, kind="ExternalOutput")
    st.snap_outs[tag] = out
    em = Em(nc)
    em.op("gpsimd", lambda e: e.dma_start(out=out.ap(), in_=(src.ap() if hasattr(src, "ap") else src[:, :])),
          wait=[(st.S[0], st.S[0].n), (st.S[1], st.S[1].n),
                (st.sDV, st.sDV.n), (st.sAC, st.sAC.n),
                (st.sPE, st.sPE.n), (st.sCAo, st.sCAo.n)],
          inc=st.sCC, amt=16)
    em.op("gpsimd", lambda e: None, wait=[(st.sCC, st.sCC.n)])
    em.run()


def _setup_and_casts(st, em):

    for dst, src in [(st.bqs, st.bqkvt), (st.bos, st.boutt), (st.maskws, st.maskwt),
                     (st.idents, st.identt), (st.p64s, st.p64t),
                     (st.mas, st.mat), (st.mbs, st.mbt)]:
        em.op("sync", lambda e, d=dst, s=src: e.dma_start(out=d[:, :], in_=s.ap()),
              inc=st.sSU, amt=16)
    st.v_setup = st.sSU.n

    em.op("vector", lambda e: e.memset(st.ones1s[:, :], 1.0))
    em.op("vector", lambda e: e.memset(st.ones128s[:, :], 1.0), inc=st.sDV)
    st.v_memset = st.sDV.n

    for dst, src in [(st.a1s, st.a1t), (st.a3s, st.a3t), (st.a2s, st.a2t)]:
        em.op("gpsimd", lambda e, d=dst, s=src: e.dma_start(
            out=d[:, :], in_=s.ap(), max_dma_last_dim=8192), inc=st.cAS, amt=16)
    st.v_small_cast = st.cAS.n
    em.op("gpsimd", lambda e: e.dma_start(out=st.b2s[:, :], in_=st.b2t.ap(),
                                          max_dma_last_dim=8192),
          inc=st.sCAo, amt=16)
    for dst, src in [(st.b1b, st.b1t), (st.b3b, st.b3t)]:
        em.op("gpsimd", lambda e, d=dst, s=src: e.dma_start(
            out=d.ap(), in_=s.ap(), max_dma_last_dim=8192), inc=st.sCAo, amt=16)
    st.v_odd_cast = st.sCAo.n

    def cast_chunks(sem, dst, src, nrows, nchunks):
        step = nrows // nchunks
        assert step % 16 == 0
        for i in range(nchunks):
            r0, r1 = i * step, (i + 1) * step if i < nchunks - 1 else nrows
            em.op("gpsimd", lambda e, a=r0, b=r1, d=dst, s=src: e.dma_start(
                out=d.ap()[a:b, :], in_=s.ap()[a:b, :], max_dma_last_dim=8192),
                inc=sem, amt=16)
        return sem.n

    st.vW13 = cast_chunks(st.cW13, st.w1b, st.w1t, NMC * 128, 8)
    st.vW13 = cast_chunks(st.cW13, st.w3b, st.w3t, NMC * 128, 8)
    st.vW2 = cast_chunks(st.cW2, st.w2b, st.w2t, NKC * 128, 4)
    st.vWQ = cast_chunks(st.cWQ, st.wqb, st.wqkvt, 48 * 128, 8)
    st.vWO = cast_chunks(st.cWO, st.wob, st.woutt, 32 * 128, 4)


def _ln_chain(st, em, src_ap_fn, dst_sb, stats, war_dst=()):
    sbn, sagg, sstd, srstd, epss = stats
    src = src_ap_fn
    for k in range(8):
        if k < 7:
            em.op("vector", lambda e, kk=k: e.bn_stats(sbn[:, 6 * kk:6 * kk + 6],
                                                       src[:, 512 * kk:512 * kk + 512]))
        else:
            em.op("vector", lambda e, kk=k: e.bn_stats(sbn[:, 6 * kk:6 * kk + 6],
                                                       src[:, 512 * kk:512 * kk + 512]),
                  inc=st.sDV)
    v_bn = st.sDV.n
    em.op("vector", lambda e: e.bn_aggr(sagg[:, :], sbn[:, :]),
          wait=[(st.sDV, v_bn)], inc=st.sDV)
    v_agg = st.sDV.n
    em.op("scalar", lambda e: e.activation(out=sstd[:, :], in_=sagg[:, 1:2],
                                           func=AF.Sqrt, bias=epss[:, :]),
          wait=[(st.sDV, v_agg)], inc=st.sAC)
    v_sqrt = st.sAC.n
    em.op("vector", lambda e: e.reciprocal(srstd[:, :], sstd[:, :]),
          wait=[(st.sAC, v_sqrt)], inc=st.sDV)
    v_rec = st.sDV.n
    em.op("vector", lambda e: e.tensor_scalar(out=dst_sb[:, :], in0=src[:, :],
                                              scalar1=sagg[:, 0:1], scalar2=srstd[:, :],
                                              op0=ALU.subtract, op1=ALU.mult),
          wait=[(st.sDV, v_rec)] + list(war_dst), inc=st.sDV)
    return st.sDV.n


def _ffn_pass(st, src, dst, write_ln2, tag="f1", pre_emit=None):
    nc = st.nc
    is_out = (dst is st.out)

    with contextlib.ExitStack() as es:
        def sb(name, shape, dt=BF16):
            return es.enter_context(nc.sbuf_tensor(_un(name), shape, dt))
        xsub = [sb(f"a_xsub{i}", [128, D], F32) for i in range(2)]
        xlnt = [sb(f"a_xlnt{i}", [128, D]) for i in range(2)]
        sbn = sb("a_sbn", [128, 48], F32)
        sagg = sb("a_sagg", [128, 2], F32)
        sstd = sb("a_sstd", [128, 1], F32)
        srstd = sb("a_srstd", [128, 1], F32)
        epss = sb("a_eps", [128, 1], F32)
        em = Em(nc)
        if pre_emit is not None:
            pre_emit(em)
        em.op("vector", lambda e: e.memset(epss[:, :], 1e-5), inc=st.sDV)
        norm_v = [0, 0]
        store_v = [0, 0]
        for t in range(NST):
            par = t % 2
            em.op("sync", lambda e, tt=t, p=par: e.dma_start(
                out=xsub[p][:, :], in_=src.ap()[128 * tt:128 * tt + 128, :]),
                wait=[(st.sDV, norm_v[par])],
                inc=st.A[par], amt=16)
            v_ld = st.A[par].n
            em.op("vector", lambda e, p=par: e.bn_stats(sbn[:, 0:6], xsub[p][:, 0:512]),
                  wait=[(st.A[par], v_ld)])
            for k in range(1, 8):
                if k < 7:
                    em.op("vector", lambda e, kk=k, p=par: e.bn_stats(
                        sbn[:, 6 * kk:6 * kk + 6], xsub[p][:, 512 * kk:512 * kk + 512]))
                else:
                    em.op("vector", lambda e, kk=k, p=par: e.bn_stats(
                        sbn[:, 6 * kk:6 * kk + 6], xsub[p][:, 512 * kk:512 * kk + 512]),
                        inc=st.sDV)
            v_bn = st.sDV.n
            em.op("vector", lambda e: e.bn_aggr(sagg[:, :], sbn[:, :]),
                  wait=[(st.sDV, v_bn)], inc=st.sDV)
            v_agg = st.sDV.n
            em.op("scalar", lambda e: e.activation(out=sstd[:, :], in_=sagg[:, 1:2],
                                                   func=AF.Sqrt, bias=epss[:, :]),
                  wait=[(st.sDV, v_agg)], inc=st.sAC)
            v_sq = st.sAC.n
            em.op("vector", lambda e: e.reciprocal(srstd[:, :], sstd[:, :]),
                  wait=[(st.sAC, v_sq)], inc=st.sDV)
            v_rec = st.sDV.n
            em.op("vector", lambda e, p=par: e.tensor_scalar(
                out=xlnt[p][:, :], in0=xsub[p][:, :], scalar1=sagg[:, 0:1],
                scalar2=srstd[:, :], op0=ALU.subtract, op1=ALU.mult),
                wait=[(st.sDV, v_rec), (st.S[par], store_v[par])], inc=st.sDV)
            norm_v[par] = st.sDV.n
            em.op("sync", lambda e, tt=t, p=par: e.dma_start(
                out=st.lntm.ap()[128 * tt:128 * tt + 128, :], in_=xlnt[p][:, :]),
                wait=[(st.sDV, norm_v[par])], inc=st.S[par], amt=16)
            store_v[par] = st.S[par].n
        em.run()
    _snap(st, tag + "_lntm", st.lntm)

    with contextlib.ExitStack() as esp:
        def sbp(name, shape, dt=BF16):
            return esp.enter_context(nc.sbuf_tensor(_un(name), shape, dt))
        xs = sbp("p_xs", [128, NQC * TOK])
        hs = sbp("p_hs", [128, NHC * TOK])

        with contextlib.ExitStack() as es:
            def sb(name, shape, dt=BF16):
                return es.enter_context(nc.sbuf_tensor(_un(name), shape, dt))
            xfm = sb("b_xfm", [128, NKC * 512])
            t1 = sb("b_t1", [128, 512])
            t2 = sb("b_t2", [128, 512])
            pu1 = [es.enter_context(nc.psum_tensor(_un(f"b_pu1{i}"), [128, 512], F32)) for i in range(2)]
            pu3 = [es.enter_context(nc.psum_tensor(_un(f"b_pu3{i}"), [128, 512], F32)) for i in range(2)]
            em = Em(nc)
            pe_used = 0
            dv_used = 0
            evac_v = [0, 0]
            for t in range(NT):
                par = t % 2
                for kc in range(NKC):
                    em.op("sync", lambda e, tt=t, k=kc: e.dma_start(
                        out=xfm[:, 512 * k:512 * k + 512],
                        in_=st.lntm.ap()[512 * tt:512 * tt + 512, 128 * k:128 * k + 128],
                        transpose=True),
                        wait=[(st.sPE, pe_used), (st.sDV, dv_used),
                              (st.S[0], st.S[0].n if kc == 0 and t == 0 else 0),
                              (st.S[1], st.S[1].n if kc == 0 and t == 0 else 0)],
                        inc=st.A[0], amt=16)
                ldv = st.A[0].n
                for kq in range(NQC):
                    em.op("vector", lambda e, k=kq: e.tensor_tensor(
                        out=t1[:, :], in0=xfm[:, 512 * k:512 * k + 512],
                        in1=xfm[:, 512 * (k + 8):512 * (k + 8) + 512], op=ALU.add),
                        wait=[(st.A[0], ldv)])
                    em.op("vector", lambda e, k=kq: e.tensor_tensor(
                        out=t2[:, :], in0=xfm[:, 512 * (k + 16):512 * (k + 16) + 512],
                        in1=xfm[:, 512 * (k + 24):512 * (k + 24) + 512], op=ALU.add))
                    em.op("vector", lambda e, k=kq, tt=t: e.tensor_tensor(
                        out=xs[:, TOK * k + 512 * tt:TOK * k + 512 * tt + 512],
                        in0=t1[:, :], in1=t2[:, :], op=ALU.add),
                        inc=(st.sDV if k == NQC - 1 else None))
                dv_used = st.sDV.n
                for (pu, asb) in ((pu1[par], st.a1s), (pu3[par], st.a3s)):
                    for kc in range(NKC):
                        em.op("tensor", lambda e, k=kc, b=pu, a=asb: e.matmul(
                            b[0:2, :], a[:, 2 * k:2 * k + 2],
                            xfm[:, 512 * k:512 * k + 512],
                            start=(k == 0), stop=(k == NKC - 1)),
                            wait=[(st.A[0], ldv), (st.cAS, st.v_small_cast),
                                  (st.sAC, evac_v[par])],
                            inc=(st.sPE if kc == NKC - 1 else None))
                pe_used = st.sPE.n
                v_mm = st.sPE.n
                em.op("scalar", lambda e, tt=t, p=par: e.activation(
                    out=st.u1s[0:2, 512 * tt:512 * tt + 512], in_=pu1[p][0:2, :],
                    func=AF.Identity, scale=0.5),
                    wait=[(st.sPE, v_mm)], inc=st.sAC)
                em.op("scalar", lambda e, tt=t, p=par: e.activation(
                    out=st.u3s[0:2, 512 * tt:512 * tt + 512], in_=pu3[p][0:2, :],
                    func=AF.Identity, scale=0.5),
                    inc=st.sAC)
                evac_v[par] = st.sAC.n
            em.run()
        _snap(st, tag + "_xs", xs, shape=[128, NQC * TOK], dt=BF16)
        _snap(st, tag + "_u1", st.u1s, shape=[2, TOK], dt=BF16)
        _snap(st, tag + "_u3", st.u3s, shape=[2, TOK], dt=BF16)

        with contextlib.ExitStack() as es:
            def sb(name, shape, dt=BF16):
                return es.enter_context(nc.sbuf_tensor(_un(name), shape, dt))
            w1s = [sb(f"c_w1s{i}", [128, DQ]) for i in range(2)]
            w3s = [sb(f"c_w3s{i}", [128, DQ]) for i in range(2)]
            b1p = sb("c_b1p", [2, HFF])
            b3p = sb("c_b3p", [2, HFF])
            gsb = [sb(f"c_g{i}", [128, 512]) for i in range(2)]
            usb = [sb(f"c_u{i}", [128, 512]) for i in range(2)]
            NHP = 6
            hp = [sb(f"c_hp{i}", [128, 512]) for i in range(NHP)]
            pg = es.enter_context(nc.psum_tensor(_un("c_pg"), [128, 512], F32))
            pup = es.enter_context(nc.psum_tensor(_un("c_pup"), [128, 512], F32))
            pu2 = [es.enter_context(nc.psum_tensor(_un(f"c_pu2{i}"), [128, 512], F32)) for i in range(NT)]
            prot = [es.enter_context(nc.psum_tensor(_un(f"c_prot{i}"), [128, 512], F32)) for i in range(2)]

            em = Em(nc)
            IT = NMC * NT
            em.op("sync", lambda e: e.dma_start(out=b1p[:, :], in_=st.b1b.ap()),
                  wait=[(st.sCAo, st.v_odd_cast)], inc=st.sLB, amt=16)
            em.op("sync", lambda e: e.dma_start(out=b3p[:, :], in_=st.b3b.ap()),
                  inc=st.sLB, amt=16)
            v_lb = st.sLB.n
            wld_v = [0] * NMC
            wuse_v = [0, 0]
            silu_v = [0] * IT
            upev_v = [0] * IT
            mult_v = [0] * IT
            hsal_v = [0] * IT
            u2_v = [0] * IT
            rot_v = [0] * IT
            rotadd_v = [0, 0]
            g_war = 0
            u_war = 0
            nrot = 0

            for it in range(IT + 2):
                if it < IT:
                    mc, tau = divmod(it, NT)
                    par = mc % 2
                    if tau == 0:
                        for (wbuf, wsrc) in ((w1s[par], st.w1b),
                                             (w3s[par], st.w3b)):
                            em.op("sync", lambda e, d=wbuf, s=wsrc, m=mc: e.dma_start(
                                out=d[:, :], in_=s.ap()[128 * m:128 * m + 128, :]),
                                wait=[(st.cW13, st.vW13),
                                      (st.sPE, wuse_v[par])],
                                inc=st.A[par], amt=16)
                        wld_v[mc] = st.A[par].n
                    for kc in range(NQC + 1):
                        if kc < NQC:
                            em.op("tensor", lambda e, p=par, k=kc, tt=tau: e.matmul(
                                pg[:, :], w1s[p][:, 128 * k:128 * k + 128],
                                xs[:, TOK * k + 512 * tt:TOK * k + 512 * tt + 512],
                                start=(k == 0), stop=False),
                                wait=[(st.A[par], wld_v[mc]),
                                      (st.sAC, silu_v[it - 1] if it >= 1 else 0)])
                        else:
                            em.op("tensor", lambda e, m=mc, tt=tau: e.matmul(
                                pg[:, :], b1p[0:2, 128 * m:128 * m + 128],
                                st.u1s[0:2, 512 * tt:512 * tt + 512],
                                start=False, stop=True),
                                wait=[(st.sLB, v_lb)], inc=st.sPE)
                    v_g = st.sPE.n
                    for kc in range(NQC + 1):
                        if kc < NQC:
                            em.op("tensor", lambda e, p=par, k=kc, tt=tau: e.matmul(
                                pup[:, :], w3s[p][:, 128 * k:128 * k + 128],
                                xs[:, TOK * k + 512 * tt:TOK * k + 512 * tt + 512],
                                start=(k == 0), stop=False),
                                wait=[(st.sAC, upev_v[it - 1] if it >= 1 else 0)])
                        else:
                            em.op("tensor", lambda e, m=mc, tt=tau: e.matmul(
                                pup[:, :], b3p[0:2, 128 * m:128 * m + 128],
                                st.u3s[0:2, 512 * tt:512 * tt + 512],
                                start=False, stop=True), inc=st.sPE)
                    v_u = st.sPE.n
                    if tau == NT - 1:
                        wuse_v[par] = st.sPE.n

                jt = it - 2
                if 0 <= jt < IT:
                    mcj, tauj = divmod(jt, NT)
                    if "no_u2" in st.expt:
                        em.op("tensor", lambda e: None, wait=[(st.sDV, mult_v[jt])])
                        u2_v[jt] = st.sPE.n
                    else:
                        em.op("tensor", lambda e, m=mcj, tt=tauj, sl=jt % NHP: e.matmul(
                            pu2[tt][0:2, :], st.a2s[:, 2 * m:2 * m + 2], hp[sl][:, :],
                            start=(m == 0), stop=(m == NMC - 1)),
                            wait=[(st.sDV, mult_v[jt])], inc=st.sPE)
                        u2_v[jt] = st.sPE.n
                    if any(sg[4] for sg in HS_PLANS[mcj]) and "no_rot" not in st.expt:
                        parr = nrot % 2
                        em.op("tensor", lambda e, sl=jt % NHP, p=parr: e.matmul(
                            prot[p][:, :], st.p64s[:, :], hp[sl][:, :],
                            start=True, stop=True),
                            wait=[(st.sDV, rotadd_v[parr]), (st.sSU, st.v_setup)],
                            inc=st.sPE)
                        rot_v[jt] = (st.sPE.n, parr)
                        nrot += 1

                if it < IT:
                    em.op("scalar", lambda e, p=it % 2: e.activation(
                        out=gsb[p][:, :], in_=pg[:, :], func=AF.Silu),
                        wait=[(st.sPE, v_g), (st.sDV, mult_v[it - 2] if it >= 2 else 0)],
                        inc=st.sAC)
                    silu_v[it] = st.sAC.n
                    em.op("scalar", lambda e, p=it % 2: e.activation(
                        out=usb[p][:, :], in_=pup[:, :], func=AF.Identity),
                        wait=[(st.sPE, v_u)], inc=st.sAC)
                    upev_v[it] = st.sAC.n
                    slot = it % NHP
                    hp_war = 0
                    if it >= NHP:
                        hp_war = u2_v[it - NHP]
                        if rot_v[it - NHP]:
                            hp_war = max(hp_war, rot_v[it - NHP][0])
                    em.op("vector", lambda e, p=it % 2, sl=slot: e.tensor_tensor(
                        out=hp[sl][:, :], in0=gsb[p][:, :], in1=usb[p][:, :], op=ALU.mult),
                        wait=[(st.sAC, upev_v[it]), (st.sPE, hp_war)],
                        inc=st.sDV)
                    mult_v[it] = st.sDV.n
                    mc, tau = divmod(it, NT)
                    nal = sum(1 for sg in HS_PLANS[mc] if not sg[4])
                    done = 0
                    for (lo, hi, c, dlo, rot, fi) in HS_PLANS[mc]:
                        if rot:
                            continue
                        done += 1
                        dsl = slice(dlo, dlo + (hi - lo))
                        col = slice(TOK * c + 512 * tau, TOK * c + 512 * tau + 512)
                        if fi:
                            em.op("vector", lambda e, sl=slot, l=lo, h=hi, ds=dsl, co=col:
                                  e.tensor_copy(hs[ds, co], hp[sl][l:h, :]),
                                  inc=(st.sDV if done == nal else None))
                        else:
                            em.op("vector", lambda e, sl=slot, l=lo, h=hi, ds=dsl, co=col:
                                  e.tensor_tensor(out=hs[ds, co], in0=hs[ds, co],
                                                  in1=hp[sl][l:h, :], op=ALU.add),
                                  inc=(st.sDV if done == nal else None))
                    if nal:
                        hsal_v[it] = st.sDV.n

                if 0 <= jt < IT and rot_v[jt] and "no_rot" not in st.expt:
                    mcj, tauj = divmod(jt, NT)
                    vv, parr = rot_v[jt]
                    rsegs = [sg for sg in HS_PLANS[mcj] if sg[4]]
                    for i, (lo, hi, c, dlo, rot, fi) in enumerate(rsegs):
                        assert not fi
                        dsl = slice(dlo, dlo + (hi - lo))
                        col = slice(TOK * c + 512 * tauj, TOK * c + 512 * tauj + 512)
                        em.op("vector", lambda e, p=parr, ds=dsl, co=col:
                              e.tensor_tensor(out=hs[ds, co], in0=hs[ds, co],
                                              in1=prot[p][ds, :], op=ALU.add),
                              wait=[(st.sPE, vv)],
                              inc=(st.sDV if i == len(rsegs) - 1 else None))
                    rotadd_v[parr] = st.sDV.n

            for t in range(NT):
                em.op("scalar", lambda e, tt=t: e.activation(
                    out=st.u2s[0:2, 512 * tt:512 * tt + 512], in_=pu2[tt][0:2, :],
                    func=AF.Identity, scale=0.5),
                    wait=[(st.sPE, u2_v[IT - 4 + t])], inc=st.sAC)
            em.run()
        _snap(st, tag + "_hs", hs, shape=[128, NHC * TOK], dt=BF16)
        _snap(st, tag + "_u2", st.u2s, shape=[2, TOK], dt=BF16)

        with contextlib.ExitStack() as es:
            def sb(name, shape, dt=BF16):
                return es.enter_context(nc.sbuf_tensor(_un(name), shape, dt))
            w2s = [sb(f"d_w2s{i}", [128, NHC * 128]) for i in range(2)]
            dds = [sb(f"d_dd{i}", [128, 512]) for i in range(2)]
            po = [es.enter_context(nc.psum_tensor(_un(f"d_po{i}"), [128, 512], F32)) for i in range(2)]
            em = Em(nc)
            wuse = [0, 0]
            evac = [0, 0]
            stv = [0, 0]
            for oc in range(NKC):
                par = oc % 2
                em.op("sync", lambda e, o=oc, p=par: e.dma_start(
                    out=w2s[p][:, :], in_=st.w2b.ap()[128 * o:128 * o + 128, :]),
                    wait=[(st.cW2, st.vW2), (st.sPE, wuse[par])],
                    inc=st.A[par], amt=16)
                wld = st.A[par].n
                for t in range(NT):
                    it = oc * NT + t
                    pp = it % 2
                    for kc in range(NHC + 1):
                        if kc < NHC:
                            kk = 128 if kc < NHC - 1 else 64
                            em.op("tensor", lambda e, p=par, k=kc, tt=t, b=pp, K=kk: e.matmul(
                                po[b][:, :], w2s[p][0:K, 128 * k:128 * k + 128],
                                hs[0:K, TOK * k + 512 * tt:TOK * k + 512 * tt + 512],
                                start=(k == 0), stop=False),
                                wait=[(st.A[par], wld),
                                      (st.sAC, evac[pp])])
                        else:
                            em.op("tensor", lambda e, o=oc, tt=t, b=pp: e.matmul(
                                po[b][:, :], st.b2s[0:2, 128 * o:128 * o + 128],
                                st.u2s[0:2, 512 * tt:512 * tt + 512],
                                start=False, stop=True),
                                wait=[(st.sCAo, st.v_odd_cast)], inc=st.sPE)
                    v_mm = st.sPE.n
                    if t == NT - 1:
                        wuse[par] = st.sPE.n
                    em.op("scalar", lambda e, b=pp: e.activation(
                        out=dds[b][:, :], in_=po[b][:, :], func=AF.Identity),
                        wait=[(st.sPE, v_mm), (st.S[pp], stv[pp])], inc=st.sAC)
                    evac[pp] = st.sAC.n
                    em.op("sync", lambda e, o=oc, tt=t, b=pp: e.dma_start(
                        out=st.ddelta.ap()[128 * o:128 * o + 128, 512 * tt:512 * tt + 512],
                        in_=dds[b][:, :]),
                        wait=[(st.sAC, evac[pp])], inc=st.S[pp], amt=16)
                    stv[pp] = st.S[pp].n
            em.run()
        _snap(st, tag + "_dd", st.ddelta)

    _epilogue_pass(st, delta_scale=0.5, src=src, dst=dst, write_ln2=write_ln2)


def _epilogue_pass(st, delta_scale, src, dst, write_ln2):
    nc = st.nc
    with contextlib.ExitStack() as es:
        def sb(name, shape, dt=BF16):
            return es.enter_context(nc.sbuf_tensor(_un(name), shape, dt))
        dtm = [sb(f"e_dtm{i}", [128, D]) for i in range(2)]
        rsub = [sb(f"e_rsub{i}", [128, D], F32) for i in range(2)]
        xnew = [sb(f"e_xnew{i}", [128, D], F32) for i in range(2)]
        l2t = [sb(f"e_l2t{i}", [128, D]) for i in range(2)]
        sbn = sb("e_sbn", [128, 48], F32)
        sagg = sb("e_sagg", [128, 2], F32)
        sstd = sb("e_sstd", [128, 1], F32)
        srstd = sb("e_srstd", [128, 1], F32)
        epss = sb("e_eps", [128, 1], F32)
        em = Em(nc)
        em.op("vector", lambda e: e.memset(epss[:, :], 1e-5), inc=st.sDV)
        used = [0, 0]
        stv = [0, 0]
        base0, base1 = st.S[0].n, st.S[1].n
        for t in range(NST):
            par = t % 2
            for ob in range(8):
                em.op("sync", lambda e, tt=t, o=ob, p=par: e.dma_start(
                    out=dtm[p][:, 512 * o:512 * o + 512],
                    in_=st.ddelta.ap()[512 * o:512 * o + 512, 128 * tt:128 * tt + 128],
                    transpose=True),
                    wait=[(st.sDV, used[par]), (st.S[0], base0), (st.S[1], base1)],
                    inc=st.A[par], amt=16)
            em.op("sync", lambda e, tt=t, p=par: e.dma_start(
                out=rsub[p][:, :], in_=src.ap()[128 * tt:128 * tt + 128, :]),
                inc=st.A[par], amt=16)
            v_ld = st.A[par].n
            em.op("vector", lambda e, p=par, sc=delta_scale: e.scalar_tensor_tensor(
                out=xnew[p][:, :], in0=dtm[p][:, :], scalar=sc, in1=rsub[p][:, :],
                op0=ALU.mult, op1=ALU.add),
                wait=[(st.A[par], v_ld), (st.S[par], stv[par])], inc=st.sDV)
            v_new = st.sDV.n
            em.op("sync", lambda e, tt=t, p=par: e.dma_start(
                out=dst.ap()[128 * tt:128 * tt + 128, :], in_=xnew[p][:, :]),
                wait=[(st.sDV, v_new)], inc=st.S[par], amt=16)
            if write_ln2:
                v_ln = _ln_chain(st, em, xnew[par], l2t[par],
                                 (sbn, sagg, sstd, srstd, epss),
                                 war_dst=[(st.S[par], stv[par])])
                em.op("sync", lambda e, tt=t, p=par: e.dma_start(
                    out=st.cc1i.ap()[128 * tt:128 * tt + 128, :], in_=l2t[p][:, :]),
                    wait=[(st.sDV, v_ln)], inc=st.S[par], amt=16)
                used[par] = v_ln
            else:
                used[par] = v_new
            stv[par] = st.S[par].n
        em.run()


def _cc1(st):
    nc = st.nc
    em = Em(nc)
    em.op("gpsimd", lambda e: e.collective_compute(
        "AllGather", ALU.bypass, replica_groups=PAIRS,
        ins=[st.cc1i.ap()], outs=[st.cc1o.ap()]),
        wait=[(st.S[0], st.S[0].n), (st.S[1], st.S[1].n)], inc=st.sCC, amt=1)
    v = st.sCC.n
    em.op("gpsimd", lambda e: None, wait=[(st.sCC, v)])
    em.run()


def _stage_xlnf(st):
    nc = st.nc
    with contextlib.ExitStack() as es:
        buf = [es.enter_context(nc.sbuf_tensor(_un("sx_b"), [128, 2 * TOK], BF16))
               for i in range(2)]
        em = Em(nc)
        stv = [0, 0]
        for kc in range(NKC):
            par = kc % 2
            for hb in range(2):
                em.op("sync", lambda e, k=kc, h=hb, p=par: e.dma_start(
                    out=buf[p][:, TOK * h:TOK * h + TOK],
                    in_=st.cc1o.ap()[TOK * h:TOK * h + TOK, 128 * k:128 * k + 128],
                    transpose=True),
                    wait=[(st.S[par], stv[par])],
                    inc=st.A[par], amt=16)
            v_ld = st.A[par].n
            em.op("sync", lambda e, k=kc, p=par: e.dma_start(
                out=st.xlnf.ap()[128 * k:128 * k + 128, :], in_=buf[p][:, :]),
                wait=[(st.A[par], v_ld)], inc=st.S[par], amt=16)
            stv[par] = st.S[par].n
        em.run()


def _qkv(st):
    nc = st.nc
    NF = 12
    with contextlib.ExitStack() as es:
        def sb(name, shape, dt=BF16):
            return es.enter_context(nc.sbuf_tensor(_un(name), shape, dt))
        wq = [sb(f"q_wq{i}", [128, NKC * 128]) for i in range(NF)]
        rhs = [sb(f"q_rhs{i}", [128, NKC * 512]) for i in range(2)]
        stg = [sb(f"q_stg{i}", [128, 512]) for i in range(2)]
        vstg = [sb(f"q_vstg{i}", [128, 512]) for i in range(2)]
        pq = [es.enter_context(nc.psum_tensor(_un(f"q_pq{i}"), [128, 512], F32)) for i in range(2)]
        ptv = [es.enter_context(nc.psum_tensor(_un(f"q_ptv{i}"), [128, 1024], BF16)) for i in range(2)]
        em = Em(nc)
        em.op("vector", lambda e: e.tensor_scalar(
            out=st.bqsc[:, :], in0=st.bqs[:, 0:16], scalar1=SCL, scalar2=None,
            op0=ALU.mult),
            wait=[(st.sSU, st.v_setup)], inc=st.sDV)
        v_bq = st.sDV.n
        grp_use = 0
        rhs_use = [0, 0]
        evac = [0, 0]
        stv = [0, 0]
        vtev = [0, 0]
        for fg in range(4):
            for i in range(NF):
                fcg = fg * NF + i
                em.op("sync", lambda e, f=fcg, ii=i: e.dma_start(
                    out=wq[ii][:, :], in_=st.wqb.ap()[128 * f:128 * f + 128, :]),
                    wait=[(st.cWQ, st.vWQ), (st.sPE, grp_use)],
                    inc=st.A[2], amt=16)
            wldv = st.A[2].n
            for g in range(8):
                par = g % 2
                em.op("sync", lambda e, gg=g, p=par: e.dma_start(
                    out=rhs[p][:, :],
                    in_=bass.AP(st.xlnf, 512 * gg,
                                [[2 * TOK, 128], [128 * 2 * TOK, NKC], [1, 512]])),
                    wait=[(st.sPE, rhs_use[par])], inc=st.A[par], amt=16)
                rldv = st.A[par].n
                for i in range(NF):
                    fcg = fg * NF + i
                    pp = (g * NF + i) % 2
                    for kc in range(NKC):
                        em.op("tensor", lambda e, ii=i, k=kc, p=par, b=pp: e.matmul(
                            pq[b][:, :], wq[ii][:, 128 * k:128 * k + 128],
                            rhs[p][:, 512 * k:512 * k + 512],
                            start=(k == 0), stop=(k == NKC - 1)),
                            wait=[(st.A[2], wldv), (st.A[par], rldv),
                                  (st.sAC, evac[pp])],
                            inc=(st.sPE if kc == NKC - 1 else None))
                    v_mm = st.sPE.n
                    if i == NF - 1:
                        rhs_use[par] = st.sPE.n
                        if g == 7:
                            grp_use = st.sPE.n
                    is_q = fcg < 16
                    bias_ap = (st.bqsc[:, fcg:fcg + 1] if is_q
                               else st.bqs[:, fcg:fcg + 1])
                    scale = SCL if is_q else 1.0
                    em.op("scalar", lambda e, b=pp, ba=bias_ap, sc=scale: e.activation(
                        out=stg[b][:, :], in_=pq[b][:, :], func=AF.Identity,
                        bias=ba, scale=sc),
                        wait=[(st.sPE, v_mm), (st.S[pp], stv[pp]), (st.sDV, v_bq)],
                        inc=st.sAC)
                    v_ev = st.sAC.n
                    evac[pp] = v_ev
                    if fcg < 32:
                        if fcg < 16:
                            tgt, row = st.qs, fcg
                        else:
                            tgt, row = st.ks, fcg - 16
                        em.op("sync", lambda e, tg=tgt, r=row, gg=g, b=pp: e.dma_start(
                            out=tg.ap()[128 * r:128 * r + 128, 512 * gg:512 * gg + 512],
                            in_=stg[b][:, :]),
                            wait=[(st.sAC, v_ev)], inc=st.S[pp], amt=16)
                        stv[pp] = st.S[pp].n
                    else:
                        hl = fcg - 32
                        for s4 in range(4):
                            em.op("tensor", lambda e, b=pp, s=s4: e.transpose(
                                ptv[b][:, 128 * s:128 * s + 128],
                                stg[b][:, 128 * s:128 * s + 128], st.idents[:, :]),
                                wait=[(st.sAC, max(v_ev, vtev[pp])),
                                      (st.sSU, st.v_setup)],
                                inc=(st.sPE if s4 == 3 else None))
                        v_tp = st.sPE.n
                        em.op("scalar", lambda e, b=pp: e.activation(
                            out=vstg[b][:, :], in_=ptv[b][:, 0:512], func=AF.Identity),
                            wait=[(st.sPE, v_tp), (st.S[pp], stv[pp])], inc=st.sAC)
                        vtev[pp] = st.sAC.n
                        em.op("sync", lambda e, h=hl, gg=g, b=pp: e.dma_start(
                            out=bass.AP(st.vst, 4096 * h + 512 * gg,
                                        [[NJ * 4096, 128], [128, 4], [1, 128]]),
                            in_=vstg[b][:, :]),
                            wait=[(st.sAC, vtev[pp])], inc=st.S[pp], amt=16)
                        stv[pp] = st.S[pp].n
        em.run()


def _attention(st):
    nc = st.nc
    with contextlib.ExitStack() as es:
        def sb(name, shape, dt=BF16):
            return es.enter_context(nc.sbuf_tensor(_un(name), shape, dt))
        Ks = [sb(f"t_K{i}", [128, S]) for i in range(2)]
        K2 = [sb("t_K2", [128, S])]
        Vh = [sb(f"t_V{i}", [128, NKC * 128]) for i in range(2)]
        Al = [sb(f"t_al{i}", [128, 512], F32) for i in range(2)]
        Ac = [sb(f"t_ac{i}", [128, 256], F32) for i in range(2)]
        Nk = [sb(f"t_nk{i}", [1, S], F32) for i in range(2)]
        Nkm = [sb(f"t_nkm{i}", [1, 1], F32) for i in range(2)]
        Nkmb = [sb(f"t_nkmb{i}", [1, 1]) for i in range(2)]
        Mk = [sb(f"t_mk{i}", [128, 1], F32) for i in range(2)]
        Qs = [sb(f"t_q{i}", [128, S]) for i in range(2)]
        Qn2a = [sb(f"t_qn2a{i}", [128, 32], F32) for i in range(2)]
        Qna = [sb(f"t_qna{i}", [128, 32], F32) for i in range(2)]
        Ngma = [sb(f"t_ngma{i}", [128, 32], F32) for i in range(2)]
        Ssb = [sb(f"t_S{i}", [128, S], F32) for i in range(2)]
        Psb = [sb(f"t_P{i}", [128, S]) for i in range(4)]
        Pt = [sb(f"t_pt{i}", [128, S]) for i in range(2)]
        Qd = sb("t_qd", [128, 128])
        Ls = [sb(f"t_l{i}", [128, 1], F32) for i in range(4)]
        Rl = [sb(f"t_rl{i}", [128, 1], F32) for i in range(4)]
        RlA = [sb(f"t_rlA{i}", [128, 1], F32) for i in range(4)]
        RlB = [sb(f"t_rlB{i}", [128, 1], F32) for i in range(4)]
        CtA = [sb(f"t_ctA{i}", [128, 128]) for i in range(2)]
        CtB = [sb(f"t_ctB{i}", [128, 128]) for i in range(2)]
        psS = [es.enter_context(nc.psum_tensor(f"t_psS{i}", [128, 512], F32)) for i in range(2)]
        ptp = [es.enter_context(nc.psum_tensor(_un(f"t_ptp{i}"), [128, 1024], BF16)) for i in range(2)]
        pctx = [es.enter_context(nc.psum_tensor(_un(f"t_pctx{i}"), [128, 512], F32)) for i in range(2)]
        pnk = es.enter_context(nc.psum_tensor(_un("t_pnk"), [128, 512], F32))
        pbc = es.enter_context(nc.psum_tensor(_un("t_pbc"), [128, 512], F32))

        em = Em(nc)
        job_pe = [0, 0]
        job_ac = [0, 0]
        job_dv = [0, 0]
        sS_war = [0, 0]
        ptp_war = [0, 0]
        pctx_war = [0, 0]
        Ssb_war = [0, 0]
        Psb_war = [0, 0, 0, 0]
        Pt_war = [0, 0]
        ct_war = [0, 0]
        nS = 0
        nkg = 0
        nk_prev = [0]
        for hl in range(NJ):
            jp = hl % 2
            em.op("sync", lambda e, h=hl, p=jp: e.dma_start(
                out=Ks[p][:, :], in_=st.ks.ap()[128 * h:128 * h + 128, :]),
                wait=[(st.sPE, job_pe[jp]), (st.sAC, job_ac[jp]), (st.sDV, job_dv[jp]),
                      (st.S[0], st.S[0].n if hl == 0 else 0),
                      (st.S[1], st.S[1].n if hl == 0 else 0)],
                inc=st.A[jp], amt=16)
            em.op("sync", lambda e, h=hl, p=jp: e.dma_start(
                out=Al[p][:, :], in_=st.al2t.ap()[128 * h:128 * h + 128, :]),
                inc=st.A[jp], amt=16)
            em.op("sync", lambda e, h=hl, p=jp: e.dma_start(
                out=Ac[p][:, :], in_=st.actt.ap()[128 * h:128 * h + 128, :]),
                inc=st.A[jp], amt=16)
            em.op("sync", lambda e, h=hl, p=jp: e.dma_start(
                out=Vh[p][:, :],
                in_=bass.AP(st.vst, 4096 * h,
                            [[NJ * 4096, 128], [1, 4096]])),
                inc=st.A[jp], amt=16)
            em.op("sync", lambda e, h=hl, p=jp: e.dma_start(
                out=Qs[p][:, :], in_=st.qs.ap()[128 * h:128 * h + 128, :]),
                inc=st.A[jp], amt=16)
            v_kld = st.A[jp].n
            em.op("scalar", lambda e, p=jp: e.activation(
                out=K2[0][:, :], in_=Ks[p][:, :], func=AF.Square),
                wait=[(st.A[jp], v_kld), (st.sPE, nk_prev[0])], inc=st.sAC)
            v_k2 = st.sAC.n
            for kb in range(8):
                em.op("tensor", lambda e, k=kb: e.matmul(
                    pnk[0:1, :], st.ones128s[:, :], K2[0][:, 512 * k:512 * k + 512],
                    start=True, stop=True),
                    wait=[(st.sAC, v_k2 if kb == 0 else st.sAC.n),
                          (st.sDV, st.v_memset)],
                    inc=st.sPE)
                v_nk = st.sPE.n
                em.op("scalar", lambda e, p=jp, k=kb: e.activation(
                    out=Nk[p][0:1, 512 * k:512 * k + 512], in_=pnk[0:1, :],
                    func=AF.Identity),
                    wait=[(st.sPE, v_nk)], inc=st.sAC)
            v_nkev = st.sAC.n
            nk_prev[0] = st.sPE.n
            em.op("vector", lambda e, p=jp: e.reduce_max(
                out=Nkm[p][0:1, :], in_=Nk[p][0:1, :], axis=AX.X),
                wait=[(st.sAC, v_nkev)], inc=st.sDV)
            v_max = st.sDV.n
            em.op("scalar", lambda e, p=jp: e.activation(
                out=Nkmb[p][0:1, :], in_=Nkm[p][0:1, :], func=AF.Sqrt),
                wait=[(st.sDV, v_max)], inc=st.sAC)
            v_sq = st.sAC.n
            em.op("tensor", lambda e, p=jp: e.matmul(
                pbc[:, 0:1], st.ones1s[0:1, :], Nkmb[p][0:1, 0:1],
                start=True, stop=True),
                wait=[(st.sAC, v_sq)], inc=st.sPE)
            v_bc = st.sPE.n
            em.op("scalar", lambda e, p=jp: e.activation(
                out=Mk[p][:, :], in_=pbc[:, 0:1], func=AF.Identity),
                wait=[(st.sPE, v_bc)], inc=st.sAC)
            v_mk = st.sAC.n
            for t32 in range(32):
                em.op("scalar", lambda e, p=jp, t=t32: e.activation(
                    out=Qd[:, :], in_=Qs[p][:, 128 * t:128 * t + 128],
                    func=AF.Square, accum_out=Qn2a[p][:, t:t + 1]),
                    wait=[(st.A[jp], v_kld),
                          (st.sDV, job_dv[jp])],
                    inc=(st.sAC if t32 == 31 else None))
            v_q2a = st.sAC.n
            em.op("scalar", lambda e, p=jp: e.activation(
                out=Qna[p][:, :], in_=Qn2a[p][:, :], func=AF.Sqrt),
                wait=[(st.sAC, v_q2a)], inc=st.sAC)
            v_qna = st.sAC.n
            em.op("vector", lambda e, p=jp: e.tensor_scalar(
                out=Ngma[p][:, :], in0=Qna[p][:, :], scalar1=Mk[p][:, :],
                scalar2=-1.0, op0=ALU.mult, op1=ALU.mult),
                wait=[(st.sAC, max(v_qna, v_mk))], inc=st.sDV)
            v_ngma = st.sDV.n

            q_use_pe = [job_pe[0], job_pe[1]]
            q_use_ac = [job_ac[0], job_ac[1]]
            rc_prev = [0, 0, 0, 0]
            for j in range(NSUP):
                E = j + 1
                qp = j % 2
                exp_v = [0, 0, 0, 0]
                rl_v = [0, 0, 0, 0]
                sttv = [0, 0, 0, 0]
                for s_ in range(4):
                    sp2 = s_ % 2
                    for kb in range(E):
                        bp = nS % 2 if E == 1 else (nS + kb) % 2
                        em.op("tensor", lambda e, jj=j, ss=s_, k=kb, b=bp, h=jp: e.matmul(
                            psS[b][:, :], Qs[h][:, 512 * jj + 128 * ss:512 * jj + 128 * ss + 128],
                            Ks[h][:, 512 * k:512 * k + 512], start=True, stop=True),
                            wait=[(st.sDV, sS_war[bp]), (st.A[jp], v_kld)],
                            inc=st.sPE)
                        v_s = st.sPE.n
                        idx = j * 32 + s_ * 8 + kb
                        em.op("vector", lambda e, b=bp, k=kb, p=jp, ii=idx, o=sp2: e.scalar_tensor_tensor(
                            out=Ssb[o][:, 512 * k:512 * k + 512], in0=psS[b][:, :],
                            scalar=Ac[p][:, ii:ii + 1], in1=Al[p][:, :],
                            op0=ALU.add, op1=ALU.add),
                            wait=[(st.sPE, v_s), (st.sAC, Ssb_war[sp2]),
                                  (st.A[jp], v_kld), (st.sSU, st.v_setup)],
                            inc=st.sDV)
                        sS_war[bp] = st.sDV.n
                        if kb == j:
                            em.op("vector", lambda e, k=kb, ss=s_, o=sp2: e.tensor_tensor(
                                out=Ssb[o][:, 512 * k:512 * k + 512],
                                in0=Ssb[o][:, 512 * k:512 * k + 512],
                                in1=st.maskws[:, 512 - 128 * ss:1024 - 128 * ss],
                                op=ALU.add), inc=st.sDV)
                    sttv[s_] = st.sDV.n
                    em.op("scalar", lambda e, jj=j, ss=s_, o=sp2, EE=E, p=jp: e.activation(
                        out=Psb[ss][:, 0:512 * EE], in_=Ssb[o][:, 0:512 * EE],
                        func=AF.Exp, bias=Ngma[p][:, 4 * jj + ss:4 * jj + ss + 1],
                        accum_out=Ls[ss][:, :]),
                        wait=[(st.sDV, max(sttv[s_], v_ngma, rc_prev[s_])),
                              (st.sPE, Psb_war[s_])],
                        inc=st.sAC)
                    exp_v[s_] = st.sAC.n
                    Ssb_war[sp2] = exp_v[s_]
                    nS += 1
                for s_ in range(4):
                    em.op("vector", lambda e, ss=s_: e.reciprocal(Rl[ss][:, :], Ls[ss][:, :]),
                          wait=[(st.sAC, exp_v[s_])], inc=st.sDV)
                    v_rc = st.sDV.n
                    rc_prev[s_] = v_rc
                    em.op("vector", lambda e, ss=s_: e.tensor_scalar(
                        out=RlA[ss][:, :], in0=Rl[ss][:, :], scalar1=st.mas[:, :],
                        scalar2=None, op0=ALU.mult),
                        wait=[(st.sDV, v_rc), (st.sSU, st.v_setup)])
                    em.op("vector", lambda e, ss=s_: e.tensor_scalar(
                        out=RlB[ss][:, :], in0=Rl[ss][:, :], scalar1=st.mbs[:, :],
                        scalar2=None, op0=ALU.mult), inc=st.sDV)
                    rl_v[s_] = st.sDV.n
                for s_ in range(4):
                    sp2 = s_ % 2
                    scp = (4 * j + s_) % 2
                    pv_first = True
                    for kg in range(E):
                        tp = nkg % 2
                        nkg += 1
                        for cc in range(4):
                            kc = 4 * kg + cc
                            em.op("tensor", lambda e, ss=s_, k=kc, c=cc, b=tp: e.transpose(
                                ptp[b][:, 128 * c:128 * c + 128],
                                Psb[ss][:, 128 * k:128 * k + 128], st.idents[:, :]),
                                wait=[(st.sAC, exp_v[s_]), (st.sDV, ptp_war[tp])],
                                inc=(st.sPE if cc == 3 else None))
                        v_tp = st.sPE.n
                        if kg == E - 1:
                            Psb_war[s_] = v_tp
                        em.op("vector", lambda e, g=kg, b=tp, o=sp2: e.tensor_copy(
                            Pt[o][:, 512 * g:512 * g + 512], ptp[b][:, 0:512]),
                            wait=[(st.sPE, max(v_tp, Pt_war[sp2]))],
                            inc=st.sDV)
                        ptp_war[tp] = st.sDV.n
                        v_ptev = st.sDV.n
                        for cc in range(4):
                            kc = 4 * kg + cc
                            em.op("tensor", lambda e, k=kc, p=jp, fi=pv_first, o=sp2,
                                  la=(kc == 4 * E - 1): e.matmul(
                                pctx[o][:, 0:128], Pt[o][:, 128 * k:128 * k + 128],
                                Vh[p][:, 128 * k:128 * k + 128],
                                start=fi, stop=la),
                                wait=[(st.sDV, max(v_ptev, pctx_war[sp2]))],
                                inc=(st.sPE if kc == 4 * E - 1 else None))
                            pv_first = False
                    v_pv = st.sPE.n
                    Pt_war[sp2] = v_pv
                    em.op("vector", lambda e, ss=s_, o=sp2: e.tensor_scalar(
                        out=CtA[o][:, :], in0=pctx[o][:, 0:128], scalar1=RlA[ss][:, :],
                        scalar2=None, op0=ALU.mult),
                        wait=[(st.sPE, v_pv),
                              (st.S[scp], ct_war[scp])],
                        inc=st.sDV)
                    em.op("vector", lambda e, ss=s_, o=sp2: e.tensor_scalar(
                        out=CtB[o][:, :], in0=pctx[o][:, 0:128], scalar1=RlB[ss][:, :],
                        scalar2=None, op0=ALU.mult), inc=st.sDV)
                    v_ct = st.sDV.n
                    pctx_war[sp2] = v_ct
                    rowb = 512 * j + 128 * s_
                    em.op("sync", lambda e, h=hl, r=rowb, o=sp2: e.dma_start(
                        out=st.cc2i.ap()[r:r + 128, 128 * h:128 * h + 128],
                        in_=CtA[o][:, :]),
                        wait=[(st.sDV, v_ct)], inc=st.S[scp], amt=16)
                    em.op("sync", lambda e, h=hl, r=rowb, o=sp2: e.dma_start(
                        out=st.cc2i.ap()[r:r + 128, 2048 + 128 * h:2048 + 128 * h + 128],
                        in_=CtB[o][:, :]),
                        inc=st.S[scp], amt=16)
                    ct_war[scp] = st.S[scp].n
                q_use_pe[qp] = st.sPE.n
                q_use_ac[qp] = st.sAC.n
            job_pe[jp] = st.sPE.n
            job_ac[jp] = st.sAC.n
            job_dv[jp] = st.sDV.n
        em.run()


def _cc2(st):
    nc = st.nc
    em = Em(nc)
    em.op("gpsimd", lambda e: e.collective_compute(
        "ReduceScatter", ALU.add, replica_groups=PAIRS,
        ins=[st.cc2i.ap()], outs=[st.cc2o.ap()]),
        wait=[(st.S[0], st.S[0].n), (st.S[1], st.S[1].n)], inc=st.sCC, amt=1)
    v = st.sCC.n
    em.op("gpsimd", lambda e: None, wait=[(st.sCC, v)])
    em.run()


def _outproj(st):
    nc = st.nc
    with contextlib.ExitStack() as es:
        def sb(name, shape, dt=BF16):
            return es.enter_context(nc.sbuf_tensor(_un(name), shape, dt))
        ctxf = sb("o_ctxf", [128, NKC * 1024])
        wo = [sb(f"o_wo{i}", [128, NKC * 128]) for i in range(2)]
        dds = [sb(f"o_dd{i}", [128, 512]) for i in range(2)]
        po = [es.enter_context(nc.psum_tensor(_un(f"o_po{i}"), [128, 512], F32)) for i in range(2)]
        em = Em(nc)
        wuse = [0, 0]
        evac = [0, 0]
        stv = [0, 0]
        ctx_use = 0
        for th in range(2):
            for tl in range(2):
                for kc in range(NKC):
                    em.op("sync", lambda e, t=th, l=tl, k=kc: e.dma_start(
                        out=ctxf[:, 1024 * k + 512 * l:1024 * k + 512 * l + 512],
                        in_=st.cc2o.ap()[512 * (2 * t + l):512 * (2 * t + l) + 512,
                                         128 * k:128 * k + 128],
                        transpose=True),
                        wait=[(st.sPE, ctx_use)], inc=st.A[2], amt=16)
            v_cld = st.A[2].n
            for oc in range(32):
                par = oc % 2
                em.op("sync", lambda e, o=oc, p=par: e.dma_start(
                    out=wo[p][:, :], in_=st.wob.ap()[128 * o:128 * o + 128, :]),
                    wait=[(st.cWO, st.vWO), (st.sPE, wuse[par])],
                    inc=st.A[par], amt=16)
                v_wld = st.A[par].n
                for tl in range(2):
                    pp = (oc * 2 + tl) % 2
                    for kc in range(NKC):
                        em.op("tensor", lambda e, p=par, k=kc, l=tl, b=pp: e.matmul(
                            po[b][:, :], wo[p][:, 128 * k:128 * k + 128],
                            ctxf[:, 1024 * k + 512 * l:1024 * k + 512 * l + 512],
                            start=(k == 0), stop=(k == NKC - 1)),
                            wait=[(st.A[2], v_cld), (st.A[par], v_wld),
                                  (st.sAC, evac[pp])],
                            inc=(st.sPE if kc == NKC - 1 else None))
                    v_mm = st.sPE.n
                    if tl == 1:
                        wuse[par] = st.sPE.n
                        if oc == 31:
                            ctx_use = st.sPE.n
                    em.op("scalar", lambda e, o=oc, b=pp: e.activation(
                        out=dds[b][:, :], in_=po[b][:, :], func=AF.Identity,
                        bias=st.bos[:, o:o + 1]),
                        wait=[(st.sPE, v_mm), (st.S[pp], stv[pp]),
                              (st.sSU, st.v_setup)], inc=st.sAC)
                    evac[pp] = st.sAC.n
                    em.op("sync", lambda e, o=oc, t=th, l=tl, b=pp: e.dma_start(
                        out=st.ddelta.ap()[128 * o:128 * o + 128,
                                           512 * (2 * t + l):512 * (2 * t + l) + 512],
                        in_=dds[b][:, :]),
                        wait=[(st.sAC, evac[pp])], inc=st.S[pp], amt=16)
                    stv[pp] = st.S[pp].n
        em.run()


def _final(st):
    nc = st.nc
    em = Em(nc)
    w = [(st.S[0], st.S[0].n), (st.S[1], st.S[1].n)]
    em.op("sync", lambda e: None, wait=w)
    em.op("gpsimd", lambda e: None, wait=w)
    for tname, tout in st.taps.items():
        src = getattr(st, tname)
        em.op("gpsimd", lambda e, s=src, d=tout: e.dma_start(out=d.ap(), in_=s.ap()),
              inc=st.sCAo, amt=16)
    if st.taps:
        em.op("gpsimd", lambda e: None, wait=[(st.sCAo, st.sCAo.n)])
    em.run()


def _prep_inputs(c, I):
    half = c % 2
    b = c // 2
    f32 = np.float32
    bf16 = ml_dtypes.bfloat16

    x = np.ascontiguousarray(I["L0_attn_hidden_states"][b, half * TOK:(half + 1) * TOK, :],
                             dtype=f32)

    heads = [16 * half + i for i in range(16)]
    wqkv = I["L0_attn_w_qkv"]
    cols = []
    for base in (0, D, 2 * D):
        for h in heads:
            cols.extend(range(base + 128 * h, base + 128 * h + 128))
    cols = np.asarray(cols)
    wsl = np.asarray(wqkv)[:, cols]
    wqkvt = np.ascontiguousarray(
        wsl.reshape(NKC, 128, 48, 128).transpose(2, 1, 0, 3).reshape(48 * 128, NKC * 128),
        dtype=f32)
    bq = np.asarray(I["L0_attn_b_qkv"])[cols]
    bqkvt = np.ascontiguousarray(bq.reshape(48, 128).T, dtype=f32)

    wout = np.asarray(I["L0_attn_w_out"])
    woutt = np.ascontiguousarray(
        wout.reshape(NKC, 128, 32, 128).transpose(2, 1, 0, 3).reshape(32 * 128, NKC * 128),
        dtype=f32)
    boutt = np.ascontiguousarray(np.asarray(I["L0_attn_b_out"]).reshape(32, 128).T, dtype=f32)

    w1q = np.asarray(I["L0_ffn_w1q"])
    w3q = np.asarray(I["L0_ffn_w3q"])
    w2q = np.asarray(I["L0_ffn_w2q"])
    w1t = np.ascontiguousarray(
        w1q.reshape(NMC, 128, NQC, 128).transpose(0, 3, 2, 1).reshape(NMC * 128, DQ),
        dtype=f32)
    w3t = np.ascontiguousarray(
        w3q.reshape(NMC, 128, NQC, 128).transpose(0, 3, 2, 1).reshape(NMC * 128, DQ),
        dtype=f32)
    w2p = np.concatenate([w2q, np.zeros((D, NHC * 128 - HQ), dtype=w2q.dtype)], axis=1)
    w2t = np.ascontiguousarray(
        w2p.reshape(32, 128, NHC, 128).transpose(0, 3, 2, 1).reshape(32 * 128, NHC * 128),
        dtype=f32)

    b1t = np.ascontiguousarray(np.asarray(I["L0_ffn_b1"]).T, dtype=f32)
    b3t = np.ascontiguousarray(np.asarray(I["L0_ffn_b3"]).T, dtype=f32)
    b2t = np.ascontiguousarray(np.asarray(I["L0_ffn_b2"]).T, dtype=f32)
    a1t = np.ascontiguousarray(
        np.asarray(I["L0_ffn_a1"]).reshape(2, NKC, 128).transpose(2, 1, 0).reshape(128, 64),
        dtype=f32)
    a3t = np.ascontiguousarray(
        np.asarray(I["L0_ffn_a3"]).reshape(2, NKC, 128).transpose(2, 1, 0).reshape(128, 64),
        dtype=f32)
    a2t = np.ascontiguousarray(
        np.asarray(I["L0_ffn_a2"]).reshape(2, NMC, 128).transpose(2, 1, 0).reshape(128, 172),
        dtype=f32)

    identt = np.eye(128, dtype=bf16)
    p64t = np.roll(np.eye(128, dtype=np.float32), 64, axis=0).astype(bf16)

    fp = np.arange(1024)
    pp = np.arange(128)
    maskwt = np.where(fp[None, :] <= pp[:, None] + 512, 0.0, NEG).astype(f32)

    slopes = _alibi_slopes(32)
    al2t = np.zeros((NJ * 128, 512), dtype=f32)
    actt = np.zeros((NJ * 128, 256), dtype=f32)
    f5 = np.arange(512, dtype=f32)
    for hl in range(NJ):
        sl = slopes[16 * half + hl]
        al2t[128 * hl:128 * hl + 128, :] = sl * (f5[None, :] - pp[:, None].astype(f32))
        tab = np.zeros(256, dtype=f32)
        for j in range(8):
            for s_ in range(4):
                for kb in range(j + 1):
                    tab[j * 32 + s_ * 8 + kb] = sl * (512.0 * (kb - j) - 128.0 * s_)
        actt[128 * hl:128 * hl + 128, :] = tab[None, :]

    mat = np.full((128, 1), 1.0 if half == 0 else 0.0, dtype=f32)
    mbt = np.full((128, 1), 1.0 if half == 1 else 0.0, dtype=f32)

    return {
        "x": x, "w1t": w1t, "w3t": w3t, "w2t": w2t, "wqkvt": wqkvt, "woutt": woutt,
        "b1t": b1t, "b3t": b3t, "b2t": b2t, "a1t": a1t, "a3t": a3t, "a2t": a2t,
        "bqkvt": bqkvt, "boutt": boutt, "identt": identt, "p64t": p64t,
        "maskwt": maskwt, "al2t": al2t, "actt": actt, "mat": mat, "mbt": mbt,
    }


def kernel(**inputs):
    nc = build_nc()
    in_maps = [_prep_inputs(c, inputs) for c in range(NCORE)]
    res = run_bass_kernel_spmd(nc, in_maps, core_ids=list(range(NCORE)))
    out = np.empty((4, S, D), dtype=np.float32)
    for c in range(NCORE):
        out[c // 2, (c % 2) * TOK:(c % 2 + 1) * TOK, :] = res.results[c]["out"]
    return out
